# version 09
# baseline (speedup 1.0000x reference)
import numpy as np
from contextlib import ExitStack

import concourse.bass as bass
import concourse.mybir as mybir
from concourse.bass_utils import run_bass_kernel_spmd

F32 = mybir.dt.float32
BF16 = mybir.dt.bfloat16
I32 = mybir.dt.int32
ALU = mybir.AluOpType
ACT = mybir.ActivationFunctionType
AXC = mybir.AxisListType.C
AXX = mybir.AxisListType.X
AXALL = mybir.AxisListType.XYZWC

T = 2048
D = 1024
FC = 8
DI = 2048
DIC = 16
FF = 4096
FFC = 32
NCORES = 8
NTT = 4
TW = 512

UPTO = 99


class Sem:
    __slots__ = ("h", "n")
    def __init__(self, stack, nc, name):
        self.h = stack.enter_context(nc.semaphore(name))
        self.n = 0
    def inc(self, ins, k=1):
        ins.then_inc(self.h, k)
        self.n += k
        return self.n


def AP(t, off, dims):
    return bass.AP(t, off, dims)


def build_program(debug=False, upto=None):
    if upto is None:
        upto = UPTO
    nc = bass.Bass("TRN2", target_bir_lowering=False)
    st = ExitStack()

    x_in = nc.dram_tensor("x0", [T, D], F32, kind="ExternalInput")
    out = nc.dram_tensor("out", [T, D], F32, kind="ExternalOutput")
    W = {}
    for name, shape in [
        ("L0_attn_ln_g", [D]), ("L0_attn_ln_b", [D]),
        ("L0_attn_W_in", [D, 2 * DI]),
        ("L0_attn_conv_w", [DI, 4]), ("L0_attn_conv_b", [DI]),
        ("L0_attn_W_x", [DI, 128]),
        ("L0_attn_W_dt", [DI, DI]), ("L0_attn_b_dt", [DI]),
        ("L0_attn_A_log", [64]), ("L0_attn_D", [DI]),
        ("L0_attn_W_out", [DI, D]),
        ("L0_ffn_w_gate", [FF, D]), ("L0_ffn_w_up", [FF, D]), ("L0_ffn_w_down", [D, FF]),
        ("L1_attn_Wqkv", [D, 3 * D]), ("L1_attn_Wout", [D, D]),
        ("L1_ffn_w1", [FF, D]), ("L1_ffn_b1", [FF]),
        ("L1_ffn_w2", [D, FF]), ("L1_ffn_b2", [D]),
        ("rope_cos", [128, T]), ("rope_sin", [128, T]),
    ]:
        W[name] = nc.dram_tensor(name, shape, F32, kind="ExternalInput")

    xc_sp = nc.dram_tensor("xc_sp", [DIC, 128, T], BF16)
    rows_sp = nc.dram_tensor("rows_sp", [16, T], F32)
    nwg = nc.dram_tensor("nwg", [FF, D], BF16)
    nwu = nc.dram_tensor("nwu", [FF, D], BF16)
    nwd = nc.dram_tensor("nwd", [D, FF], BF16)
    nw1 = nc.dram_tensor("nw1", [FF, D], BF16)
    nw2 = nc.dram_tensor("nw2", [D, FF], BF16)
    nwgT = nc.dram_tensor("nwgT", [D, FF], BF16)
    nwuT = nc.dram_tensor("nwuT", [D, FF], BF16)
    vt_sp = nc.dram_tensor("vt_sp", [FFC, 128, T], BF16)
    o_sp = nc.dram_tensor("o_sp", [FC, 128, T], BF16)
    h_sp = nc.dram_tensor("h_sp", [FFC, 128, T], BF16)
    cc_b = [nc.dram_tensor(f"cc_b{i}", [1, 1], F32) for i in range(4)]

    dbg = {}
    def dbgt(name, shape, dtype=F32):
        if debug:
            dbg[name] = nc.dram_tensor("dbg_" + name, shape, dtype, kind="ExternalOutput")
            return dbg[name]
        return None

    sbt = lambda name, shape, dt=F32: st.enter_context(nc.sbuf_tensor(name, shape, dt))

    with st:
        X = [sbt(f"Xs{f}", [128, T]) for f in range(FC)]
        ln_g = sbt("v_lng", [128, FC]); ln_b = sbt("v_lnb", [128, FC])
        conv_b = sbt("v_convb", [128, DIC]); Dv = sbt("v_D", [128, DIC])
        b_dt = sbt("v_bdt", [128, DIC])
        conv_w = sbt("v_convw", [128, 4 * DIC])
        aet = sbt("v_aexp", [64, 2])
        b1v = sbt("v_b1", [128, FFC]); b2v = sbt("v_b2", [128, FC])
        sw1 = sbt("v_sw1", [128, FFC]); sw2 = sbt("v_sw2", [128, FC])
        bq1 = sbt("v_bq1", [128, FFC]); bq2 = sbt("v_bq2", [128, FC])
        sc1 = sbt("v_sc1", [128, FFC]); sc2 = sbt("v_sc2", [128, FC])
        bq2row = sbt("v_bq2row", [1, D])
        ones = sbt("c_ones", [128, 128])
        onesb = sbt("c_onesb", [128, 128], BF16)
        iden = sbt("c_iden", [128, 128])
        scal = sbt("scal", [1, 64])
        pcol = sbt("pcol", [128, 16])
        halfc = sbt("c_half", [128, 1])
        def cell(i):
            return scal[0:1, i:i + 1]
        def pc(i):
            return pcol[:, i:i + 1]

        sem = {n: Sem(st, nc, "s_" + n) for n in
               ["ds", "dg", "dc", "mm", "ev", "sv", "gp", "cc", "dv"]}
        ds, dg, dc, mm, ev, sv, gp, cc, dv = (sem[n] for n in
               ["ds", "dg", "dc", "mm", "ev", "sv", "gp", "cc", "dv"])

        with ExitStack() as ps:
            xstage = ps.enter_context(nc.sbuf_tensor("xstage", [128, 16 * D], F32))
            tp = [ps.enter_context(nc.psum_tensor(f"tp{i}", [128, 128], F32)) for i in range(4)]
            with nc.Block() as blk:
                @blk.sync
                def _(e):
                    e.dma_start(xstage[:, :],
                                AP(x_in, 0, [[D, 128], [128 * D, 16], [1, D]])).then_inc(ds.h, 16)
                    ds.n += 16
                    for t, src in [
                        (ln_g, AP(W["L0_attn_ln_g"], 0, [[1, 128], [128, FC]])),
                        (ln_b, AP(W["L0_attn_ln_b"], 0, [[1, 128], [128, FC]])),
                        (conv_b, AP(W["L0_attn_conv_b"], 0, [[1, 128], [128, DIC]])),
                        (Dv, AP(W["L0_attn_D"], 0, [[1, 128], [128, DIC]])),
                        (b_dt, AP(W["L0_attn_b_dt"], 0, [[1, 128], [128, DIC]])),
                        (conv_w, AP(W["L0_attn_conv_w"], 0, [[4, 128], [1, 4], [512, DIC]])),
                        (b1v, AP(W["L1_ffn_b1"], 0, [[1, 128], [128, FFC]])),
                        (b2v, AP(W["L1_ffn_b2"], 0, [[1, 128], [128, FC]])),
                        (aet[:, 0:1], AP(W["L0_attn_A_log"], 0, [[1, 64], [1, 1]])),
                    ]:
                        e.dma_start(t[:, :] if not isinstance(t, bass.AP) else t, src,
                                    allow_slow_non_contiguous=True).then_inc(ds.h, 16)
                        ds.n += 16
                    e.wait_ge(ds.h, ds.n)

                @blk.vector
                def _(e):
                    e.memset(ones[:, :], 1.0)
                    e.memset(onesb[:, :], 1.0)
                    e.memset(pcol[:, 0:1], 0.125)
                    e.memset(halfc[:, :], 0.5).then_inc(ev.h, 1)
                    ev.n += 1

            with nc.Block() as blk:
                @blk.gpsimd
                def _(e):
                    e.wait_ge(ev.h, ev.n)
                    e.affine_select(iden[:, :], ones[:, :], [[1, 128]], ALU.is_equal, 0.0,
                                    base=0, channel_multiplier=-1)
                    e.engine_nop().then_inc(gp.h, 1); gp.n += 1
                @blk.scalar
                def _(e):
                    e.activation(aet[:, 1:2], aet[:, 0:1], ACT.Exp)
            with nc.Block() as blk:
                @blk.vector
                def _(e):
                    e.tensor_scalar(aet[:, 1:2], aet[:, 1:2], -1.0, None, ALU.mult)
                    e.tensor_scalar(b_dt[:, :], b_dt[:, :], -1.0, None, ALU.mult)

            with nc.Block() as blk:
                @blk.tensor
                def _(e):
                    e.wait_ge(gp.h, gp.n)
                    ev0 = ev.n
                    i = 0
                    for tc in range(16):
                        for f in range(FC):
                            if i >= 4:
                                e.wait_ge(ev.h, ev0 + i - 3)
                            e.transpose(tp[i % 4][:, :],
                                        xstage[:, tc * D + 128 * f: tc * D + 128 * (f + 1)],
                                        iden[:, :]).then_inc(mm.h, 1)
                            mm.n += 1
                            i += 1
                @blk.vector
                def _(e):
                    base = mm.n - 128
                    i = 0
                    for tc in range(16):
                        for f in range(FC):
                            e.wait_ge(mm.h, base + i + 1)
                            e.tensor_copy(X[f][:, 128 * tc:128 * (tc + 1)],
                                          tp[i % 4][:, :]).then_inc(ev.h, 1)
                            ev.n += 1
                            i += 1

        if upto >= 1:
          with ExitStack() as ps:
            sb_ = lambda n, s, d=F32: ps.enter_context(nc.sbuf_tensor(n, s, d))
            pt_ = lambda n, s: ps.enter_context(nc.psum_tensor(n, s, F32))
            rows = sb_("p1rows", [1, 7 * T])
            RK = {"mu": 0, "var": 1, "tmp": 2, "a1": 3, "b1": 4, "A": 5, "B": 6}
            row = lambda k: rows[0:1, RK[k] * T:(RK[k] + 1) * T]
            xn = [sb_(f"p1xn{f}", [128, T], BF16) for f in range(FC)]
            w1h = sb_("p1w", [128, 8 * 1024], BF16)
            xpp = sb_("p1xp", [128, T + 3])
            xco = [sb_(f"p1xco{i}", [128, T], BF16) for i in range(2)]
            cacc = sb_("p1cacc", [128, T])
            sq = sb_("p1sq", [128, TW])
            scr = [sb_(f"p1scr{i}", [128, TW]) for i in range(2)]
            pst1 = pt_("p1s1", [1, TW]); pst2 = pt_("p1s2", [1, TW])
            pbA = pt_("p1bA", [128, TW]); pbB = pt_("p1bB", [128, TW])
            pxp = [pt_(f"p1px{i}", [128, TW]) for i in range(2)]

            with nc.Block() as blk:
                sv0 = sv.n; mm0 = mm.n; ev0 = ev.n
                @blk.scalar
                def _(e):
                    i = 0
                    for tt in range(NTT):
                        for f in range(FC):
                            if i >= 1:
                                e.wait_ge(mm.h, mm0 + 2 * i)
                            e.activation(sq[:, :], X[f][:, tt * TW:(tt + 1) * TW],
                                         ACT.Square).then_inc(sv.h, 1)
                            sv.n += 1
                            i += 1
                @blk.tensor
                def _(e):
                    e.wait_ge(ev.h, ev0)
                    i = 0
                    for tt in range(NTT):
                        for f in range(FC):
                            if f == 0 and tt > 0:
                                e.wait_ge(ev.h, ev0 + 2 * tt)
                            e.matmul(pst1[:, :], ones[:, 0:1], X[f][:, tt * TW:(tt + 1) * TW],
                                     start=(f == 0), stop=(f == FC - 1)).then_inc(mm.h, 1)
                            mm.n += 1
                            e.wait_ge(sv.h, sv0 + i + 1)
                            e.matmul(pst2[:, :], ones[:, 0:1], sq[:, :],
                                     start=(f == 0), stop=(f == FC - 1)).then_inc(mm.h, 1)
                            mm.n += 1
                            i += 1
                @blk.vector
                def _(e):
                    for tt in range(NTT):
                        e.wait_ge(mm.h, mm0 + (tt + 1) * 16)
                        sl = slice(tt * TW, (tt + 1) * TW)
                        e.tensor_scalar(row("mu")[:, sl], pst1[:, :], 1.0 / D, None, ALU.mult)
                        e.tensor_scalar(row("tmp")[:, sl], pst2[:, :], 1.0 / D, None,
                                        ALU.mult).then_inc(ev.h, 2)
                        ev.n += 2

            def rsqrt_polish(e_v, dst, srcE, scr_r):
                e_v.tensor_tensor(scr_r, dst, dst, ALU.mult)
                e_v.tensor_tensor(scr_r, scr_r, srcE, ALU.mult)
                e_v.tensor_scalar(scr_r, scr_r, -0.5, 1.5, ALU.mult, ALU.add)
                e_v.tensor_tensor(dst, dst, scr_r, ALU.mult)
            with nc.Block() as blk:
                @blk.vector
                def _(e):
                    e.tensor_tensor(row("var"), row("mu"), row("mu"), ALU.mult)
                    e.drain()
                    e.tensor_tensor(row("var"), row("tmp"), row("var"), ALU.subtract)
                    e.drain()
                    e.tensor_scalar(row("tmp"), row("var"), 1e-5, None,
                                    ALU.add).then_inc(ev.h, 1)
                    ev.n += 1
                @blk.scalar
                def _(e):
                    e.wait_ge(ev.h, ev.n)
                    e.activation(row("A"), row("tmp"), ACT.Ln)
                    e.drain()
                    e.activation(row("a1"), row("A"), ACT.Exp, scale=-0.5)
            with nc.Block() as blk:
                @blk.vector
                def _(e):
                    e.tensor_tensor(row("b1"), row("mu"), row("a1"), ALU.mult)
                    e.drain()
                    e.tensor_scalar(row("b1"), row("b1"), -1.0, None, ALU.mult)
                    e.tensor_tensor(row("tmp"), row("a1"), row("a1"), ALU.mult)
                    e.drain()
                    e.tensor_tensor(row("tmp"), row("tmp"), row("var"), ALU.mult)
                    e.drain()
                    e.tensor_scalar(row("tmp"), row("tmp"), 1e-5, None,
                                    ALU.add).then_inc(ev.h, 1)
                    ev.n += 1
                @blk.scalar
                def _(e):
                    e.wait_ge(ev.h, ev.n)
                    e.activation(row("B"), row("tmp"), ACT.Ln)
                    e.drain()
                    e.activation(row("mu"), row("B"), ACT.Exp, scale=-0.5)
            with nc.Block() as blk:
                @blk.vector
                def _(e):
                    e.tensor_tensor(row("A"), row("a1"), row("mu"), ALU.mult)
                    e.tensor_tensor(row("B"), row("b1"), row("mu"),
                                    ALU.mult).then_inc(ev.h, 1)
                    ev.n += 1

            with nc.Block() as blk:
                mm0 = mm.n; ev0 = ev.n; sv0 = sv.n
                @blk.tensor
                def _(e):
                    e.wait_ge(ev.h, ev0)
                    for tt in range(NTT):
                        sl = slice(tt * TW, (tt + 1) * TW)
                        if tt > 0:
                            e.wait_ge(ev.h, ev0 + tt * 8)
                        e.matmul(pbA[:, :], ones[0:1, :], row("A")[:, sl], start=True, stop=True)
                        mm.n += 1
                        e.matmul(pbB[:, :], ones[0:1, :], row("B")[:, sl],
                                 start=True, stop=True).then_inc(mm.h, 2)
                        mm.n += 1
                @blk.vector
                def _(e):
                    QW = TW // 4
                    qs = [q * QW for q in range(4)]
                    for tt in range(NTT):
                        e.wait_ge(mm.h, mm0 + 2 * (tt + 1))
                        for f in range(FC):
                            b = scr[f % 2]
                            for h0 in qs:
                                e.tensor_tensor(b[:, h0:h0 + QW],
                                                X[f][:, tt * TW + h0:tt * TW + h0 + QW],
                                                pbA[:, h0:h0 + QW], ALU.mult)
                            e.drain()
                            for h0 in qs:
                                e.tensor_tensor(b[:, h0:h0 + QW], b[:, h0:h0 + QW],
                                                pbB[:, h0:h0 + QW], ALU.add)
                            e.drain()
                            ins = None
                            for h0 in qs:
                                ins = e.tensor_scalar(
                                    xn[f][:, tt * TW + h0:tt * TW + h0 + QW],
                                    b[:, h0:h0 + QW], ln_g[:, f:f + 1],
                                    ln_b[:, f:f + 1], ALU.mult, ALU.add)
                            ins.then_inc(ev.h, 1)
                            ev.n += 1

            if debug:
                d_xn = dbgt("xn", [FC, 128, T], BF16)
                with nc.Block() as blk:
                    @blk.sync
                    def _(e):
                        for f in range(FC):
                            e.dma_start(AP(d_xn, f * 128 * T, [[T, 128], [1, T]]),
                                        xn[f][:, :]).then_inc(ds.h, 16)
                            ds.n += 16
                        e.wait_ge(ds.h, ds.n)
            with nc.Block() as blk:
                @blk.vector
                def _(e):
                    e.memset(xpp[:, 0:3], 0.0)

            with nc.Block() as blk:
                mm0 = mm.n; ev0 = ev.n; sv0 = sv.n; gp0 = gp.n; dg0 = dg.n; ds0 = ds.n
                @blk.gpsimd
                def _(e):
                    for oh in range(2):
                        if oh == 1:
                            e.wait_ge(mm.h, mm0 + 32)
                        e.dma_start(w1h[:, :],
                                    AP(W["L0_attn_W_in"], oh * 1024,
                                       [[2 * DI, 128], [128 * 2 * DI, 8], [1, 1024]])
                                    ).then_inc(dg.h, 16)
                        dg.n += 16
                @blk.tensor
                def _(e):
                    gi = 0
                    for oh in range(2):
                        e.wait_ge(dg.h, dg0 + 16 * (oh + 1))
                        for oc in range(8):
                            for tt in range(NTT):
                                if gi >= 2:
                                    e.wait_ge(ev.h, ev0 + gi - 1)
                                last = None
                                for kc in range(FC):
                                    last = e.matmul(
                                        pxp[gi % 2][:, :],
                                        w1h[:, kc * 1024 + 128 * oc: kc * 1024 + 128 * (oc + 1)],
                                        xn[kc][:, tt * TW:(tt + 1) * TW],
                                        start=(kc == 0), stop=(kc == FC - 1))
                                last.then_inc(mm.h, 1)
                                mm.n += 1
                                gi += 1
                @blk.scalar
                def _(e):
                    gi = 0
                    for og in range(DIC):
                        if og >= 1:
                            e.wait_ge(sv.h, sv0 + og)
                        for tt in range(NTT):
                            e.wait_ge(mm.h, mm0 + gi + 1)
                            e.activation(xpp[:, 3 + tt * TW: 3 + (tt + 1) * TW],
                                         pxp[gi % 2][:, :], ACT.Copy).then_inc(ev.h, 1)
                            ev.n += 1
                            gi += 1
                        e.wait_ge(sv.h, sv0 + og + 1)
                        if og >= 2:
                            e.wait_ge(ds.h, ds0 + 16 * og)
                        e.activation(xco[og % 2][:, :], cacc[:, :], ACT.Silu,
                                     bias=conv_b[:, og:og + 1]).then_inc(gp.h, 1)
                        gp.n += 1
                @blk.vector
                def _(e):
                    QWc = T // 4
                    qs = [q * QWc for q in range(4)]
                    for og in range(DIC):
                        e.wait_ge(ev.h, ev0 + 4 * (og + 1))
                        if og >= 1:
                            e.wait_ge(gp.h, gp0 + og)
                        for h0 in qs:
                            e.tensor_scalar(cacc[:, h0:h0 + QWc],
                                            xpp[:, 3 + h0:3 + h0 + QWc],
                                            conv_w[:, 48 + og:49 + og], None, ALU.mult)
                        for k in (2, 1, 0):
                            e.drain()
                            for h0 in qs:
                                ins = e.scalar_tensor_tensor(
                                    cacc[:, h0:h0 + QWc], xpp[:, k + h0:k + h0 + QWc],
                                    conv_w[:, 16 * k + og:16 * k + og + 1],
                                    cacc[:, h0:h0 + QWc], ALU.mult, ALU.add)
                        ins.then_inc(sv.h, 1)
                        sv.n += 1
                @blk.sync
                def _(e):
                    for og in range(DIC):
                        e.wait_ge(gp.h, gp0 + og + 1)
                        e.dma_start(AP(xc_sp, og * 128 * T, [[T, 128], [1, T]]),
                                    xco[og % 2][:, :]).then_inc(ds.h, 16)
                        ds.n += 16
                    e.wait_ge(ds.h, ds.n)

            with nc.Block() as blk:
                @blk.sync
                def _(e):
                    e.wait_ge(ev.h, ev.n)
                    for i, k in enumerate(["a1", "b1", "A", "B"]):
                        e.dma_start(AP(rows_sp, i * T, [[1, 1], [1, T]]),
                                    row(k)).then_inc(ds.h, 16)
                        ds.n += 16
                    e.wait_ge(ds.h, ds.n)

        if upto >= 2:
          with ExitStack() as ps:
            sb_ = lambda n, s, d=F32: ps.enter_context(nc.sbuf_tensor(n, s, d))
            pt_ = lambda n, s: ps.enter_context(nc.psum_tensor(n, s, F32))
            rows2 = sb_("p2rows", [1, 2 * T])
            r_delta = rows2[0:1, 0:T]; r_ys = rows2[0:1, T:2 * T]
            Bst = sb_("p2B", [64, T]); Cst = sb_("p2C", [64, T])
            with ExitStack() as ps2:
                wx = ps2.enter_context(nc.sbuf_tensor("p2wx", [128, DIC * 128], BF16))
                wdt = ps2.enter_context(nc.sbuf_tensor("p2wdt", [128, DIC * DI], BF16))
                xcs = ps2.enter_context(nc.sbuf_tensor("p2xcs", [128, DIC * TW], BF16))
                spl = [ps2.enter_context(nc.sbuf_tensor(f"p2spl{i}", [128, TW], F32))
                       for i in range(2)]
                pB = ps2.enter_context(nc.psum_tensor("p2pB", [64, TW], F32))
                pC = ps2.enter_context(nc.psum_tensor("p2pC", [64, TW], F32))
                pdl = ps2.enter_context(nc.psum_tensor("p2pd", [1, TW], F32))
                pdt = [ps2.enter_context(nc.psum_tensor(f"p2dt{i}", [128, TW], F32))
                       for i in range(2)]
                with nc.Block() as blk:
                    @blk.gpsimd
                    def _(e):
                        e.dma_start(wx[:, :],
                                    AP(W["L0_attn_W_x"], 0,
                                       [[128, 128], [128 * 128, DIC], [1, 128]])
                                    ).then_inc(dc.h, 16)
                        dc.n += 16
                        e.dma_start(wdt[:, :],
                                    AP(W["L0_attn_W_dt"], 0,
                                       [[DI, 128], [128 * DI, DIC], [1, DI]])
                                    ).then_inc(dc.h, 16)
                        dc.n += 16
                        e.wait_ge(dc.h, dc.n)
                with nc.Block() as blk:
                    mm0 = mm.n; ev0 = ev.n; sv0 = sv.n; ds0 = ds.n
                    @blk.sync
                    def _(e):
                        for tt in range(NTT):
                            if tt > 0:
                                e.wait_ge(mm.h, mm0 + tt * 34)
                            e.dma_start(xcs[:, :],
                                        AP(xc_sp, tt * TW,
                                           [[T, 128], [128 * T, DIC], [1, TW]])
                                        ).then_inc(ds.h, 16)
                            ds.n += 16
                    @blk.tensor
                    def _(e):
                        for tt in range(NTT):
                            e.wait_ge(ds.h, ds0 + 16 * (tt + 1))
                            if tt > 0:
                                e.wait_ge(ev.h, ev0 + tt * 19)
                            last = None
                            for kc in range(DIC):
                                last = e.matmul(pB[:, :], wx[:, kc * 128: kc * 128 + 64],
                                                xcs[:, kc * TW:(kc + 1) * TW],
                                                start=(kc == 0), stop=(kc == DIC - 1))
                            last.then_inc(mm.h, 1); mm.n += 1
                            last = None
                            for kc in range(DIC):
                                last = e.matmul(pC[:, :], wx[:, kc * 128 + 64: kc * 128 + 128],
                                                xcs[:, kc * TW:(kc + 1) * TW],
                                                start=(kc == 0), stop=(kc == DIC - 1))
                            last.then_inc(mm.h, 1); mm.n += 1
                            for og in range(DIC):
                                if og >= 2:
                                    e.wait_ge(ev.h, ev0 + tt * 19 + og + 1)
                                last = None
                                for kc in range(DIC):
                                    last = e.matmul(pdt[og % 2][:, :],
                                                    wdt[:, kc * DI + 128 * og: kc * DI + 128 * (og + 1)],
                                                    xcs[:, kc * TW:(kc + 1) * TW],
                                                    start=(kc == 0), stop=(kc == DIC - 1))
                                last.then_inc(mm.h, 1); mm.n += 1
                                e.wait_ge(ev.h, ev0 + tt * 19 + 2 + og + 1)
                                e.matmul(pdl[:, :], ones[:, 0:1], spl[og % 2][:, :],
                                         start=(og == 0), stop=(og == DIC - 1)).then_inc(mm.h, 1)
                                mm.n += 1
                    @blk.scalar
                    def _(e):
                        for tt in range(NTT):
                            e.wait_ge(mm.h, mm0 + tt * 34 + 1)
                            e.activation(Bst[:, tt * TW:(tt + 1) * TW], pB[:, :],
                                         ACT.Copy).then_inc(ev.h, 1)
                            ev.n += 1
                            e.wait_ge(mm.h, mm0 + tt * 34 + 2)
                            e.activation(Cst[:, tt * TW:(tt + 1) * TW], pC[:, :],
                                         ACT.Copy).then_inc(ev.h, 1)
                            ev.n += 1
                            for og in range(DIC):
                                e.wait_ge(mm.h, mm0 + tt * 34 + 2 * og + 3)
                                e.activation(spl[og % 2][:, :], pdt[og % 2][:, :], ACT.Sigmoid,
                                             scale=-1.0, bias=b_dt[:, og:og + 1])
                                e.drain()
                                e.activation(spl[og % 2][:, :], spl[og % 2][:, :],
                                             ACT.Ln).then_inc(ev.h, 1)
                                ev.n += 1
                    @blk.vector
                    def _(e):
                        for tt in range(NTT):
                            e.wait_ge(mm.h, mm0 + (tt + 1) * 34)
                            e.tensor_scalar(r_delta[:, tt * TW:(tt + 1) * TW], pdl[:, :],
                                            -1.0 / DI, None, ALU.mult).then_inc(ev.h, 1)
                            ev.n += 1
            Ad = sb_("p2Ad", [64, T]); hsc = sb_("p2h", [64, T])
            with ExitStack() as ps3:
                pAd = ps3.enter_context(nc.psum_tensor("p2pad", [64, TW], F32))
                pys = ps3.enter_context(nc.psum_tensor("p2pys", [1, TW], F32))
                with nc.Block() as blk:
                    mm0 = mm.n; ev0 = ev.n; sv0 = sv.n
                    @blk.tensor
                    def _(e):
                        e.wait_ge(ev.h, ev0)
                        for tt in range(NTT):
                            if tt > 0:
                                e.wait_ge(sv.h, sv0 + tt)
                            e.matmul(pAd[:, :], ones[0:1, 0:64],
                                     r_delta[:, tt * TW:(tt + 1) * TW],
                                     start=True, stop=True).then_inc(mm.h, 1)
                            mm.n += 1
                    @blk.scalar
                    def _(e):
                        for tt in range(NTT):
                            e.wait_ge(mm.h, mm0 + tt + 1)
                            e.activation(Ad[:, tt * TW:(tt + 1) * TW], pAd[:, :], ACT.Exp,
                                         scale=aet[:, 1:2]).then_inc(sv.h, 1)
                            sv.n += 1
                with nc.Block() as blk:
                    @blk.vector
                    def _(e):
                        e.tensor_tensor_scan(hsc[:, :], Ad[:, :], Bst[:, :], 0.0,
                                             ALU.mult, ALU.add)
                        e.drain()
                        e.tensor_tensor(Ad[:, :], hsc[:, :], Cst[:, :],
                                        ALU.mult).then_inc(ev.h, 1)
                        ev.n += 1
                with nc.Block() as blk:
                    mm0 = mm.n; ev0 = ev.n
                    @blk.tensor
                    def _(e):
                        e.wait_ge(ev.h, ev0)
                        for tt in range(NTT):
                            if tt > 0:
                                e.wait_ge(ev.h, ev0 + tt)
                            e.matmul(pys[:, :], ones[0:64, 0:1],
                                     Ad[:, tt * TW:(tt + 1) * TW],
                                     start=True, stop=True).then_inc(mm.h, 1)
                            mm.n += 1
                    @blk.vector
                    def _(e):
                        for tt in range(NTT):
                            e.wait_ge(mm.h, mm0 + tt + 1)
                            e.tensor_copy(r_ys[:, tt * TW:(tt + 1) * TW],
                                          pys[:, :]).then_inc(ev.h, 1)
                            ev.n += 1
                with nc.Block() as blk:
                    @blk.sync
                    def _(e):
                        e.wait_ge(ev.h, ev.n)
                        e.dma_start(AP(rows_sp, 4 * T, [[1, 1], [1, T]]),
                                    r_ys).then_inc(ds.h, 16)
                        ds.n += 16
                        e.wait_ge(ds.h, ds.n)
                if debug:
                    d_sc = dbgt("scan", [4, 64, T], F32)
                    d_rw = dbgt("rows", [6, T], F32)
                    with nc.Block() as blk:
                        @blk.sync
                        def _(e):
                            for i, tile in enumerate([Bst, Cst, Ad, hsc]):
                                e.dma_start(AP(d_sc, i * 64 * T, [[T, 64], [1, T]]),
                                            tile[:, :]).then_inc(ds.h, 16)
                                ds.n += 16
                            for i in range(5):
                                e.dma_start(AP(d_rw, i * T, [[1, 1], [1, T]]),
                                            AP(rows_sp, i * T, [[1, 1], [1, T]])).then_inc(ds.h, 16)
                                ds.n += 16
                            e.dma_start(AP(d_rw, 5 * T, [[1, 1], [1, T]]),
                                        r_delta).then_inc(ds.h, 16)
                            ds.n += 16
                            e.wait_ge(ds.h, ds.n)

        if upto >= 3:
          with ExitStack() as ps:
            sb_ = lambda n, s, d=F32: ps.enter_context(nc.sbuf_tensor(n, s, d))
            HT = 256
            rows3 = sb_("p3rows", [1, 5 * T])
            RK3 = {"A": 0, "B": 1, "ys": 2, "a1p": 3, "b1": 4}
            row3 = lambda k: rows3[0:1, RK3[k] * T:(RK3[k] + 1) * T]
            win2 = sb_("p3w2", [128, 8 * DI], BF16)
            wob = [sb_(f"p3wo{i}", [128, DIC * 128], BF16) for i in range(2)]
            xcs3 = [sb_(f"p3xc{i}", [128, DIC * HT], BF16) for i in range(2)]
            xn3 = sb_("p3xn", [128, FC * TW], BF16)
            ut = sb_("p3u", [128, DIC * TW], BF16)
            gt = [sb_(f"p3gt{i}", [128, TW], F32) for i in range(2)]
            scr3 = [sb_(f"p3sc{i}", [128, TW], F32) for i in range(2)]
            pbA = ps.enter_context(nc.psum_tensor("p3bA", [128, TW], F32))
            pbB = ps.enter_context(nc.psum_tensor("p3bB", [128, TW], F32))
            pysb = ps.enter_context(nc.psum_tensor("p3ys", [128, TW], F32))
            pa1p = ps.enter_context(nc.psum_tensor("p3a1", [128, TW], F32))
            pres = [ps.enter_context(nc.psum_tensor(f"p3re{i}", [128, TW], F32)) for i in range(2)]
            pout = [ps.enter_context(nc.psum_tensor(f"p3ou{i}", [128, TW], F32)) for i in range(2)]

            with nc.Block() as blk:
                dg0 = dg.n; ds0 = ds.n
                @blk.gpsimd
                def _(e):
                    e.dma_start(win2[:, :],
                                AP(W["L0_attn_W_in"], DI,
                                   [[2 * DI, 128], [128 * 2 * DI, 8], [1, DI]])
                                ).then_inc(dg.h, 16)
                    dg.n += 16
                    e.wait_ge(dg.h, dg.n)
                @blk.sync
                def _(e):
                    for i, k in enumerate(["a1", "b1", "A", "B", "ys"]):
                        dstk = {"a1": "a1p", "b1": "b1", "A": "A", "B": "B", "ys": "ys"}[k]
                        e.dma_start(row3(dstk), AP(rows_sp, i * T, [[1, 1], [1, T]])
                                    ).then_inc(ds.h, 16)
                        ds.n += 16
                    e.wait_ge(ds.h, ds.n)
            with nc.Block() as blk:
                @blk.vector
                def _(e):
                    e.tensor_scalar(row3("a1p"), row3("a1p"), 1.0, None,
                                    ALU.add).then_inc(ev.h, 1)
                    ev.n += 1

            with nc.Block() as blk:
                mm0 = mm.n; ev0 = ev.n; sv0 = sv.n; dg0 = dg.n; ds0 = ds.n; gp0 = gp.n
                MMT = 28; SVT = 16; EVT = 32
                @blk.gpsimd
                def _(e):
                    for tt in range(NTT):
                        for fcx in range(FC):
                            i = tt * FC + fcx
                            if i >= 2:
                                e.wait_ge(mm.h, mm0 + ((i - 2) // FC) * MMT + 20 + (i - 2) % FC + 1)
                            e.dma_start(wob[i % 2][:, :],
                                        AP(W["L0_attn_W_out"], fcx * 128,
                                           [[D, 128], [128 * D, DIC], [1, 128]])
                                        ).then_inc(dg.h, 16)
                            dg.n += 16
                @blk.sync
                def _(e):
                    for tt in range(NTT):
                        for hh in range(2):
                            i = tt * 2 + hh
                            if i >= 2:
                                e.wait_ge(ev.h, ev0 + ((i - 2) // 2) * EVT + 24)
                            e.dma_start(xcs3[i % 2][:, :],
                                        AP(xc_sp, tt * TW + hh * HT,
                                           [[T, 128], [128 * T, DIC], [1, HT]])
                                        ).then_inc(ds.h, 16)
                            ds.n += 16
                @blk.tensor
                def _(e):
                    e.wait_ge(ev.h, ev0)
                    for tt in range(NTT):
                        sl = slice(tt * TW, (tt + 1) * TW)
                        base_mm = mm0 + tt * MMT
                        base_ev = ev0 + tt * EVT
                        if tt > 0:
                            e.wait_ge(ev.h, ev0 + tt * EVT)
                        e.matmul(pbA[:, :], ones[0:1, :], row3("A")[:, sl], start=True, stop=True)
                        e.matmul(pbB[:, :], ones[0:1, :], row3("B")[:, sl], start=True, stop=True)
                        e.matmul(pysb[:, :], ones[0:1, :], row3("ys")[:, sl], start=True, stop=True)
                        e.matmul(pa1p[:, :], ones[0:1, :], row3("a1p")[:, sl],
                                 start=True, stop=True).then_inc(mm.h, 4)
                        mm.n += 4
                        e.wait_ge(ev.h, ev0 + tt * EVT + 8)
                        for og in range(DIC):
                            if og >= 2:
                                e.wait_ge(sv.h, sv0 + tt * SVT + og - 1)
                            last = None
                            for kc in range(FC):
                                last = e.matmul(pres[og % 2][:, :],
                                                win2[:, kc * DI + 128 * og: kc * DI + 128 * (og + 1)],
                                                xn3[:, kc * TW:(kc + 1) * TW],
                                                start=(kc == 0), stop=(kc == FC - 1))
                            last.then_inc(mm.h, 1); mm.n += 1
                        for fcx in range(FC):
                            i = tt * FC + fcx
                            e.wait_ge(dg.h, dg0 + 16 * (i + 1))
                            e.wait_ge(ev.h, ev0 + tt * EVT + 24)
                            if fcx >= 2:
                                e.wait_ge(ev.h, ev0 + tt * EVT + 24 + fcx - 1)
                            last = None
                            for og in range(DIC):
                                last = e.matmul(pout[fcx % 2][:, :],
                                                wob[i % 2][:, og * 128:(og + 1) * 128],
                                                ut[:, og * TW:(og + 1) * TW],
                                                start=(og == 0), stop=False)
                            e.matmul(pout[fcx % 2][:, :], ones[0:1, 0:128], row3("b1")[:, sl],
                                     start=False, stop=True).then_inc(mm.h, 1)
                            mm.n += 1
                @blk.vector
                def _(e):
                    for tt in range(NTT):
                        sl = slice(tt * TW, (tt + 1) * TW)
                        base_mm = mm0 + tt * MMT
                        e.wait_ge(mm.h, base_mm + 4)
                        QW = TW // 4
                        qs = [q * QW for q in range(4)]
                        for f in range(FC):
                            b = scr3[f % 2]
                            for h0 in qs:
                                e.tensor_tensor(b[:, h0:h0 + QW],
                                                X[f][:, tt * TW + h0:tt * TW + h0 + QW],
                                                pbA[:, h0:h0 + QW], ALU.mult)
                            e.drain()
                            for h0 in qs:
                                e.tensor_tensor(b[:, h0:h0 + QW], b[:, h0:h0 + QW],
                                                pbB[:, h0:h0 + QW], ALU.add)
                            e.drain()
                            ins = None
                            for h0 in qs:
                                ins = e.tensor_scalar(
                                    xn3[:, f * TW + h0:f * TW + h0 + QW],
                                    b[:, h0:h0 + QW], ln_g[:, f:f + 1],
                                    ln_b[:, f:f + 1], ALU.mult, ALU.add)
                            ins.then_inc(ev.h, 1)
                            ev.n += 1
                        HQ = HT // 2
                        def _stt(og):
                            b = scr3[og % 2]
                            for hh in range(2):
                                for hq in range(2):
                                    o0 = hh * HT + hq * HQ
                                    e.scalar_tensor_tensor(
                                        b[:, o0:o0 + HQ],
                                        xcs3[(tt * 2 + hh) % 2][:, og * HT + hq * HQ:
                                                                og * HT + hq * HQ + HQ],
                                        Dv[:, og:og + 1], pysb[:, o0:o0 + HQ],
                                        ALU.mult, ALU.add)
                        def _u(og):
                            e.wait_ge(sv.h, sv0 + tt * SVT + og + 1)
                            e.drain()
                            b = scr3[og % 2]
                            ins = None
                            for q in range(4):
                                o0 = q * HQ
                                ins = e.tensor_tensor(ut[:, og * TW + o0:og * TW + o0 + HQ],
                                                      b[:, o0:o0 + HQ],
                                                      gt[og % 2][:, o0:o0 + HQ], ALU.mult)
                            ins.then_inc(ev.h, 1)
                            ev.n += 1
                        e.wait_ge(ds.h, ds0 + 16 * (tt * 2 + 2))
                        for og in range(DIC):
                            _stt(og)
                            _u(og)
                        for f in range(FC):
                            b = scr3[f % 2]
                            e.wait_ge(mm.h, mm0 + tt * MMT + 20 + f + 1)
                            for h0 in qs:
                                e.tensor_tensor(b[:, h0:h0 + QW],
                                                X[f][:, tt * TW + h0:tt * TW + h0 + QW],
                                                pa1p[:, h0:h0 + QW], ALU.mult)
                            e.drain()
                            ins = None
                            for h0 in qs:
                                ins = e.tensor_tensor(
                                    X[f][:, tt * TW + h0:tt * TW + h0 + QW],
                                    b[:, h0:h0 + QW],
                                    pout[f % 2][:, h0:h0 + QW], ALU.add)
                            ins.then_inc(ev.h, 1)
                            ev.n += 1
                @blk.scalar
                def _(e):
                    for tt in range(NTT):
                        sl = slice(tt * TW, (tt + 1) * TW)
                        for og in range(DIC):
                            e.wait_ge(mm.h, mm0 + tt * MMT + 4 + og + 1)
                            if og >= 2:
                                e.wait_ge(ev.h, ev0 + tt * EVT + 8 + og - 1)
                            e.activation(gt[og % 2][:, :], pres[og % 2][:, :],
                                         ACT.Silu).then_inc(sv.h, 1)
                            sv.n += 1

        if upto >= 4:
          for wname, wsrc, wdst, nsl, slw, per_row, scell, swtile in [
              ("g", "L0_ffn_w_gate", nwg, 32, 1024, False, 0, None),
              ("u", "L0_ffn_w_up", nwu, 32, 1024, False, 1, None),
              ("d", "L0_ffn_w_down", nwd, 8, 4096, False, 2, None),
              ("w1", "L1_ffn_w1", nw1, 32, 1024, True, None, "sw1"),
              ("w2", "L1_ffn_w2", nw2, 8, 4096, True, None, "sw2"),
          ]:
            GR = (64 * 1024) // (slw * 4)
            NGR = nsl // GR
            NPC = slw // 1024
            with ExitStack() as ps:
                sb_ = lambda n, s, d=F32: ps.enter_context(nc.sbuf_tensor(n, s, d))
                wnat = sb_(f"p4w{wname}", [128, GR * slw])
                red = sb_(f"p4r{wname}", [128, 4])
                us = [sb_(f"p4u{wname}{i}", [128, 1024]) for i in range(2)]
                ui = [ps.enter_context(nc.sbuf_tensor(f"p4i{wname}{i}", [128, 1024], I32))
                      for i in range(2)]
                ug = [sb_(f"p4g{wname}{i}", [128, 1024]) for i in range(2)]
                nb = [ps.enter_context(nc.sbuf_tensor(f"p4n{wname}{i}", [128, slw], BF16))
                      for i in range(2)]
                rsw = sb_(f"p4s{wname}", [128, nsl if per_row else 1])
                pbc = ps.enter_context(nc.psum_tensor(f"p4p{wname}", [128, 1], F32))
                swt = {None: None, "sw1": sw1, "sw2": sw2}[swtile]
                redw = sb_(f"p4rw{wname}", [128, nsl])
                with nc.Block() as blk:
                    dg0 = dg.n; ev0 = ev.n
                    @blk.gpsimd
                    def _(e):
                        for gr in range(NGR):
                            if gr >= 1:
                                e.wait_ge(ev.h, ev0 + gr * GR)
                            e.dma_start(wnat[:, :],
                                        AP(W[wsrc], gr * GR * 128 * slw,
                                           [[slw, 128], [128 * slw, GR], [1, slw]])
                                        ).then_inc(dg.h, 16)
                            dg.n += 16
                    @blk.vector
                    def _(e):
                        for gr in range(NGR):
                            e.wait_ge(dg.h, dg0 + 16 * (gr + 1))
                            for sl_ in range(GR):
                                s_ = gr * GR + sl_
                                if per_row:
                                    ins = e.reduce_max(redw[:, s_:s_ + 1],
                                                       wnat[:, sl_ * slw:(sl_ + 1) * slw],
                                                       AXX, apply_absolute_value=True)
                                else:
                                    ins = e.reduce_sum(redw[:, s_:s_ + 1],
                                                       wnat[:, sl_ * slw:(sl_ + 1) * slw],
                                                       AXX, apply_absolute_value=True)
                                ins.then_inc(ev.h, 1)
                                ev.n += 1
                if per_row:
                    with nc.Block() as blk:
                        @blk.vector
                        def _(e):
                            e.tensor_scalar(redw[:, :], redw[:, :], 1e-8, None, ALU.max)
                            e.drain()
                            e.tensor_scalar(swt[:, :], redw[:, :], 1.0 / 127.0, None,
                                            ALU.mult)
                            e.reciprocal(rsw[:, :], redw[:, :])
                            e.drain()
                            e.tensor_scalar(rsw[:, :], rsw[:, :], 127.0, None,
                                            ALU.mult).then_inc(ev.h, 1)
                            ev.n += 1
                if not per_row:
                    with nc.Block() as blk:
                        @blk.vector
                        def _(e):
                            e.reduce_sum(red[:, 0:1], redw[:, :], AXX).then_inc(ev.h, 1)
                            ev.n += 1
                    with nc.Block() as blk:
                        @blk.gpsimd
                        def _(e):
                            e.wait_ge(ev.h, ev.n)
                            e.tensor_reduce(cell(8), red[:, 0:1], AXALL,
                                            ALU.add).then_inc(gp.h, 1)
                            gp.n += 1
                    with nc.Block() as blk:
                        @blk.vector
                        def _(e):
                            e.wait_ge(gp.h, gp.n)
                            e.tensor_scalar(cell(8), cell(8),
                                            1.0 / (nsl * slw * 128), 1e-5, ALU.mult, ALU.max)
                            e.drain()
                            e.reciprocal(cell(scell), cell(8))
                            e.drain()
                            ins = e.nop()
                            ins.then_inc(ev.h, 1)
                            ev.n += 1
                        @blk.tensor
                        def _(e):
                            e.wait_ge(ev.h, ev.n)
                            e.matmul(pbc[:, :], ones[0:1, :], cell(scell),
                                     start=True, stop=True).then_inc(mm.h, 1)
                            mm.n += 1
                    with nc.Block() as blk:
                        @blk.vector
                        def _(e):
                            e.wait_ge(mm.h, mm.n)
                            e.tensor_copy(rsw[:, 0:1], pbc[:, :])
                rsw_ready = ev.n
                with nc.Block() as blk:
                    sv0 = sv.n; ev0 = ev.n; ds0 = ds.n; dg0 = dg.n
                    NP = nsl * NPC
                    @blk.gpsimd
                    def _(e):
                        for gr in range(NGR):
                            if gr >= 1:
                                e.wait_ge(ev.h, ev0 + gr * GR * NPC * 4)
                            e.dma_start(wnat[:, :],
                                        AP(W[wsrc], gr * GR * 128 * slw,
                                           [[slw, 128], [128 * slw, GR], [1, slw]])
                                        ).then_inc(dg.h, 16)
                            dg.n += 16
                    @blk.scalar
                    def _(e):
                        e.wait_ge(ev.h, rsw_ready)
                        p = 0
                        for gr in range(NGR):
                            e.wait_ge(dg.h, dg0 + 16 * (gr + 1))
                            for sl_ in range(GR):
                                s_ = gr * GR + sl_
                                for pcx in range(NPC):
                                    if p >= 2:
                                        e.wait_ge(ev.h, ev0 + 4 * (p - 1))
                                    e.activation(us[p % 2][:, :],
                                                 wnat[:, sl_ * slw + pcx * 1024:
                                                      sl_ * slw + (pcx + 1) * 1024],
                                                 ACT.Identity, bias=halfc[:, 0:1],
                                                 scale=rsw[:, (s_ if per_row else 0):
                                                           (s_ + 1 if per_row else 1)]
                                                 ).then_inc(sv.h, 1)
                                    sv.n += 1
                                    p += 1
                    @blk.vector
                    def _(e):
                        NP2 = nsl * NPC
                        def _ws(p):
                            s_, pcx = divmod(p, NPC)
                            e.wait_ge(sv.h, sv0 + p + 1)
                            if s_ >= 2 and pcx == 0:
                                e.wait_ge(ds.h, ds0 + 16 * (s_ - 1))
                        def _outp(p):
                            s_, pcx = divmod(p, NPC)
                            nslice = nb[s_ % 2][:, pcx * 1024:(pcx + 1) * 1024]
                            if per_row:
                                ins = e.tensor_copy(nslice, us[p % 2][:, :])
                            else:
                                ins = e.tensor_scalar(nslice, us[p % 2][:, :], -1.0, 1.0,
                                                      ALU.max, ALU.min)
                            ins.then_inc(ev.h, 4)
                            ev.n += 4
                        for p0 in range(0, NP2, 2):
                            _ws(p0); _ws(p0 + 1)
                            e.tensor_copy(ui[0][:, :], us[0][:, :])
                            e.tensor_copy(ui[1][:, :], us[1][:, :])
                            e.drain()
                            e.tensor_tensor(ug[0][:, :], ui[0][:, :], us[0][:, :], ALU.is_gt)
                            e.tensor_tensor(ug[1][:, :], ui[1][:, :], us[1][:, :], ALU.is_gt)
                            e.drain()
                            e.scalar_tensor_tensor(us[0][:, :], ui[0][:, :], -1.0,
                                                   ug[0][:, :], ALU.mult, ALU.add)
                            e.scalar_tensor_tensor(us[1][:, :], ui[1][:, :], -1.0,
                                                   ug[1][:, :], ALU.mult, ALU.add)
                            e.drain()
                            _outp(p0); _outp(p0 + 1)
                    @blk.sync
                    def _(e):
                        for s_ in range(nsl):
                            e.wait_ge(ev.h, ev0 + 4 * NPC * (s_ + 1))
                            e.dma_start(AP(wdst, s_ * 128 * slw, [[slw, 128], [1, slw]]),
                                        nb[s_ % 2][:, :]).then_inc(ds.h, 16)
                            ds.n += 16
                        e.wait_ge(ds.h, ds.n)
          with ExitStack() as ps:
            wtmp = ps.enter_context(nc.sbuf_tensor("p4t", [128, 2 * FF], BF16))
            with nc.Block() as blk:
                @blk.sync
                def _(e):
                    for mi, (nat, tc) in enumerate([(nwg, nwgT), (nwu, nwuT)]):
                        for ic in range(FC):
                            i = mi * FC + ic
                            e.dma_start(wtmp[:, (i % 2) * FF:(i % 2 + 1) * FF],
                                        AP(nat, ic * 128, [[D, FF], [1, 128]]),
                                        transpose=True).then_inc(ds.h, 16)
                            ds.n += 16
                            e.wait_ge(ds.h, ds.n)
                            e.dma_start(AP(tc, ic * 128 * FF, [[FF, 128], [1, FF]]),
                                        wtmp[:, (i % 2) * FF:(i % 2 + 1) * FF]
                                        ).then_inc(ds.h, 16)
                            ds.n += 16
                    e.wait_ge(ds.h, ds.n)

        if upto >= 5:
          with ExitStack() as ps:
            sb_ = lambda n, s, d=F32: ps.enter_context(nc.sbuf_tensor(n, s, d))
            rows5o = sb_("p5rowso", [1, 2 * T])
            psm = ps.enter_context(ExitStack())
            sbm = lambda n, s, d=F32: psm.enter_context(nc.sbuf_tensor(n, s, d))
            rows5m = sbm("p5rowsm", [1, 3 * T])
            def r5(i):
                if i < 3:
                    return rows5m[0:1, i * T:(i + 1) * T]
                return rows5o[0:1, (i - 3) * T:(i - 2) * T]
            iq = [sbm(f"p5iq{f}", [128, T], BF16) for f in range(FC)]
            with ExitStack() as psa:
                sba = lambda n, s, d=F32: psa.enter_context(nc.sbuf_tensor(n, s, d))
                xn5 = [sba(f"p5xn{f}", [128, T], BF16) for f in range(FC)]
                m128 = sba("p5m128", [128, T])
                n128 = sba("p5n128", [128, T])
                sbb = sba("p5sb", [128, T])
                scr5 = sba("p5sc", [128, TW])
                u5 = sba("p5u", [128, TW])
                i5 = psa.enter_context(nc.sbuf_tensor("p5i", [128, TW], I32))
                g5 = sba("p5g", [128, TW])
                sq5 = u5
                pst1 = psa.enter_context(nc.psum_tensor("p5s1", [1, TW], F32))
                pst2 = psa.enter_context(nc.psum_tensor("p5s2", [1, TW], F32))
                pbA = psa.enter_context(nc.psum_tensor("p5bA", [128, TW], F32))
                pbB = psa.enter_context(nc.psum_tensor("p5bB", [128, TW], F32))
                with nc.Block() as blk:
                    sv0 = sv.n; mm0 = mm.n; ev0 = ev.n
                    @blk.scalar
                    def _(e):
                        i = 0
                        for tt in range(NTT):
                            for f in range(FC):
                                if i >= 1:
                                    e.wait_ge(mm.h, mm0 + 2 * i)
                                e.activation(sq5[:, :], X[f][:, tt * TW:(tt + 1) * TW],
                                             ACT.Square).then_inc(sv.h, 1)
                                sv.n += 1
                                i += 1
                    @blk.tensor
                    def _(e):
                        e.wait_ge(ev.h, ev0)
                        i = 0
                        for tt in range(NTT):
                            for f in range(FC):
                                if f == 0 and tt > 0:
                                    e.wait_ge(ev.h, ev0 + 2 * tt)
                                e.matmul(pst1[:, :], ones[:, 0:1],
                                         X[f][:, tt * TW:(tt + 1) * TW],
                                         start=(f == 0), stop=(f == FC - 1)).then_inc(mm.h, 1)
                                mm.n += 1
                                e.wait_ge(sv.h, sv0 + i + 1)
                                e.matmul(pst2[:, :], ones[:, 0:1], sq5[:, :],
                                         start=(f == 0), stop=(f == FC - 1)).then_inc(mm.h, 1)
                                mm.n += 1
                                i += 1
                    @blk.vector
                    def _(e):
                        for tt in range(NTT):
                            e.wait_ge(mm.h, mm0 + (tt + 1) * 16)
                            sl = slice(tt * TW, (tt + 1) * TW)
                            e.tensor_scalar(r5(0)[:, sl], pst1[:, :], 1.0 / D, None, ALU.mult)
                            e.tensor_scalar(r5(2)[:, sl], pst2[:, :], 1.0 / D, None,
                                            ALU.mult).then_inc(ev.h, 2)
                            ev.n += 2
                with nc.Block() as blk:
                    @blk.vector
                    def _(e):
                        e.tensor_tensor(r5(1), r5(0), r5(0), ALU.mult)
                        e.drain()
                        e.tensor_tensor(r5(1), r5(2), r5(1), ALU.subtract)
                        e.drain()
                        e.tensor_scalar(r5(2), r5(1), 1e-5, None, ALU.add).then_inc(ev.h, 1)
                        ev.n += 1
                    @blk.scalar
                    def _(e):
                        e.wait_ge(ev.h, ev.n)
                        e.activation(r5(3), r5(2), ACT.Ln)
                        e.drain()
                        e.activation(r5(3), r5(3), ACT.Exp, scale=-0.5)
                with nc.Block() as blk:
                    @blk.vector
                    def _(e):
                        e.tensor_tensor(r5(4), r5(0), r5(3), ALU.mult)
                        e.tensor_scalar(r5(4), r5(4), -1.0, None, ALU.mult)
                with nc.Block() as blk:
                    mm0 = mm.n; ev0 = ev.n
                    @blk.tensor
                    def _(e):
                        for tt in range(NTT):
                            sl = slice(tt * TW, (tt + 1) * TW)
                            if tt > 0:
                                e.wait_ge(ev.h, ev0 + tt * 8)
                            e.matmul(pbA[:, :], ones[0:1, :], r5(3)[:, sl], start=True, stop=True)
                            e.matmul(pbB[:, :], ones[0:1, :], r5(4)[:, sl],
                                     start=True, stop=True).then_inc(mm.h, 2)
                            mm.n += 2
                    @blk.vector
                    def _(e):
                        QW = TW // 4
                        for tt in range(NTT):
                            e.wait_ge(mm.h, mm0 + 2 * (tt + 1))
                            sl = slice(tt * TW, (tt + 1) * TW)
                            for f in range(FC):
                                hsl = [slice(tt * TW + q * QW, tt * TW + (q + 1) * QW)
                                       for q in range(4)]
                                rel = [slice(q * QW, (q + 1) * QW) for q in range(4)]
                                for hs, rs in zip(hsl, rel):
                                    e.tensor_tensor(scr5[:, rs], X[f][:, hs],
                                                    pbA[:, rs], ALU.mult)
                                e.drain()
                                for hs, rs in zip(hsl, rel):
                                    e.tensor_tensor(xn5[f][:, hs], scr5[:, rs],
                                                    pbB[:, rs], ALU.add)
                                e.drain()
                            ins = None
                            for acc, op in ((m128, ALU.max), (n128, ALU.min)):
                                for d, (a, bidx) in zip((scr5, u5, g5, None),
                                                        [(0, 1), (2, 3), (4, 5), (6, 7)]):
                                    dst = acc[:, sl] if d is None else d[:, :]
                                    e.tensor_tensor(dst, xn5[a][:, sl], xn5[bidx][:, sl], op)
                                e.drain()
                                e.tensor_tensor(scr5[:, :], scr5[:, :], u5[:, :], op)
                                e.tensor_tensor(g5[:, :], g5[:, :], acc[:, sl], op)
                                e.drain()
                                ins = e.tensor_tensor(acc[:, sl], scr5[:, :], g5[:, :], op)
                                e.drain()
                            ins.then_inc(ev.h, 8)
                            ev.n += 8
                with nc.Block() as blk:
                    @blk.vector
                    def _(e):
                        e.tensor_scalar(n128[:, :], n128[:, :], -1.0, None, ALU.mult)
                        e.drain()
                        e.tensor_tensor(m128[:, :], m128[:, :], n128[:, :],
                                        ALU.max).then_inc(ev.h, 1)
                        ev.n += 1
                        e.reciprocal(cell(9), cell(0))
                        e.reciprocal(cell(10), cell(1))
                        e.drain()
                with nc.Block() as blk:
                    @blk.gpsimd
                    def _(e):
                        e.wait_ge(ev.h, ev.n)
                        e.tensor_reduce(r5(0), m128[:, :], AXC, ALU.max).then_inc(gp.h, 1)
                        gp.n += 1
                    @blk.vector
                    def _(e):
                        e.wait_ge(gp.h, gp.n)
                        e.tensor_scalar(r5(2), r5(0), 1e-5, None, ALU.max)
                        e.drain()
                        e.reciprocal(r5(2), r5(2))
                        e.drain()
                        e.tensor_scalar(r5(2), r5(2), 127.0, None, ALU.mult)
                        e.tensor_tensor(r5(1), r5(0), r5(0), ALU.mult)
                        e.drain()
                        e.tensor_scalar(r5(1), r5(1), cell(9), cell(10), ALU.mult, ALU.mult)
                        e.drain()
                        e.tensor_scalar(r5(1), r5(1), 1.0 / (127.0 * 127.0), None, ALU.mult)
                with nc.Block() as blk:
                    mm0 = mm.n; ev0 = ev.n
                    @blk.tensor
                    def _(e):
                        for tt in range(NTT):
                            sl = slice(tt * TW, (tt + 1) * TW)
                            if tt > 0:
                                e.wait_ge(ev.h, ev0 + tt)
                            e.matmul(pbA[:, :], ones[0:1, :], r5(2)[:, sl],
                                     start=True, stop=True).then_inc(mm.h, 1)
                            mm.n += 1
                    @blk.vector
                    def _(e):
                        for tt in range(NTT):
                            e.wait_ge(mm.h, mm0 + tt + 1)
                            e.tensor_copy(sbb[:, tt * TW:(tt + 1) * TW],
                                          pbA[:, :]).then_inc(ev.h, 1)
                            ev.n += 1
                with nc.Block() as blk:
                    ev0 = ev.n; sv0 = sv.n
                    @blk.vector
                    def _(e):
                        i = 0
                        for f in range(FC):
                            for tt in range(NTT):
                                sl = slice(tt * TW, (tt + 1) * TW)
                                e.tensor_tensor(scr5[:, :], xn5[f][:, sl],
                                                sbb[:, sl], ALU.mult).then_inc(ev.h, 1)
                                ev.n += 1
                                e.wait_ge(sv.h, sv0 + i + 1)
                                e.tensor_copy(i5[:, :], u5[:, :])
                                e.drain()
                                e.tensor_tensor(g5[:, :], i5[:, :], u5[:, :], ALU.is_gt)
                                e.drain()
                                e.scalar_tensor_tensor(iq[f][:, sl], i5[:, :], -1.0,
                                                       g5[:, :], ALU.mult, ALU.add
                                                       ).then_inc(ev.h, 1)
                                ev.n += 1
                                i += 1
                    @blk.scalar
                    def _(e):
                        i = 0
                        for f in range(FC):
                            for tt in range(NTT):
                                e.wait_ge(ev.h, ev0 + 2 * i + 1)
                                e.activation(u5[:, :], scr5[:, :], ACT.Identity,
                                             bias=halfc[:, 0:1]).then_inc(sv.h, 1)
                                sv.n += 1
                                i += 1
            with ExitStack() as psb:
                sbb_ = lambda n, s, d=F32: psb.enter_context(nc.sbuf_tensor(n, s, d))
                mb = sbb_("p5mb", [128, T])
                cgb = sbb_("p5cgb", [128, T])
                mv = sbb_("p5mv", [128, T])
                mvn = sbb_("p5mvn", [128, T])
                wgb = [psb.enter_context(nc.sbuf_tensor(f"p5wg{i}", [128, FC * 128], BF16))
                       for i in range(2)]
                wub = [psb.enter_context(nc.sbuf_tensor(f"p5wu{i}", [128, FC * 128], BF16))
                       for i in range(2)]
                gbuf = [psb.enter_context(nc.sbuf_tensor(f"p5gb{i}", [128, TW], BF16))
                        for i in range(2)]
                ubuf = [psb.enter_context(nc.sbuf_tensor(f"p5ub{i}", [128, TW], BF16))
                        for i in range(2)]
                t1b = [psb.enter_context(nc.sbuf_tensor(f"p5t1{i}", [128, TW], BF16))
                       for i in range(2)]
                sgb = [psb.enter_context(nc.sbuf_tensor(f"p5sg{i}", [128, TW], BF16))
                       for i in range(2)]
                vt = [psb.enter_context(nc.sbuf_tensor(f"p5vt{i}", [128, T], BF16))
                      for i in range(2)]
                pG = [psb.enter_context(nc.psum_tensor(f"p5pG{i}", [128, TW], F32))
                      for i in range(2)]
                pU = [psb.enter_context(nc.psum_tensor(f"p5pU{i}", [128, TW], F32))
                      for i in range(2)]
                with nc.Block() as blk:
                    mm0 = mm.n
                    @blk.vector
                    def _(e):
                        e.tensor_scalar(cell(3), cell(0), 127.0, None, ALU.mult)
                        e.drain()
                        e.reciprocal(cell(3), cell(3))
                        e.drain()
                        e.nop().then_inc(ev.h, 1)
                        ev.n += 1
                    @blk.tensor
                    def _(e):
                        for tt in range(NTT):
                            e.matmul(pG[tt % 2][:, :], ones[0:1, :],
                                     r5(0)[:, tt * TW:(tt + 1) * TW],
                                     start=True, stop=True).then_inc(mm.h, 1)
                            mm.n += 1
                    @blk.gpsimd
                    def _(e):
                        pass
                with nc.Block() as blk:
                    @blk.vector
                    def _(e):
                        for tt in range(NTT):
                            e.tensor_copy(mb[:, tt * TW:(tt + 1) * TW], pG[tt % 2][:, :])
                        e.memset(mv[:, :], 0.0)
                with nc.Block() as blk:
                    @blk.tensor
                    def _(e):
                        e.matmul(pG[0][:, 0:1], ones[0:1, :], cell(3),
                                 start=True, stop=True).then_inc(mm.h, 1)
                        mm.n += 1
                with nc.Block() as blk:
                    @blk.vector
                    def _(e):
                        e.tensor_scalar(cgb[:, :], mb[:, :], pG[0][:, 0:1], None, ALU.mult)
                iq_ready = ev.n
                with nc.Block() as blk:
                    mm0 = mm.n; ev0 = ev.n; sv0 = sv.n; ds0 = ds.n; dg0 = dg.n
                    NU = FFC * 4
                    @blk.sync
                    def _(e):
                        for og in range(FFC):
                            if og >= 2:
                                e.wait_ge(mm.h, mm0 + 8 * (og - 1))
                            e.dma_start(wgb[og % 2][:, :],
                                        AP(nwgT, og * 128,
                                           [[FF, 128], [128 * FF, FC], [1, 128]])
                                        ).then_inc(ds.h, 16)
                            ds.n += 16
                            e.dma_start(wub[og % 2][:, :],
                                        AP(nwuT, og * 128,
                                           [[FF, 128], [128 * FF, FC], [1, 128]])
                                        ).then_inc(ds.h, 16)
                            ds.n += 16
                    @blk.gpsimd
                    def _(e):
                        for og in range(FFC):
                            e.wait_ge(ev.h, ev0 + 3 * (4 * og + 4))
                            e.dma_start(AP(vt_sp, og * 128 * T, [[T, 128], [1, T]]),
                                        vt[og % 2][:, :]).then_inc(dg.h, 16)
                            dg.n += 16
                    @blk.tensor
                    def _(e):
                        e.wait_ge(ev.h, iq_ready)
                        for og in range(FFC):
                            e.wait_ge(ds.h, ds0 + 32 * (og + 1))
                            for q in range(4):
                                u = og * 4 + q
                                sl = slice(q * TW, (q + 1) * TW)
                                if u >= 2:
                                    e.wait_ge(sv.h, sv0 + 3 * (u - 1))
                                last = None
                                for kc in range(FC):
                                    last = e.matmul(pG[u % 2][:, :],
                                                    wgb[og % 2][:, kc * 128:(kc + 1) * 128],
                                                    iq[kc][:, sl],
                                                    start=(kc == 0), stop=(kc == FC - 1))
                                last.then_inc(mm.h, 1); mm.n += 1
                                last = None
                                for kc in range(FC):
                                    last = e.matmul(pU[u % 2][:, :],
                                                    wub[og % 2][:, kc * 128:(kc + 1) * 128],
                                                    iq[kc][:, sl],
                                                    start=(kc == 0), stop=(kc == FC - 1))
                                last.then_inc(mm.h, 1); mm.n += 1
                    @blk.scalar
                    def _(e):
                        for og in range(FFC):
                            for q in range(4):
                                u = og * 4 + q
                                e.wait_ge(mm.h, mm0 + 2 * u + 1)
                                if u >= 2:
                                    e.wait_ge(ev.h, ev0 + 3 * (u - 2) + 2)
                                e.activation(gbuf[u % 2][:, :], pG[u % 2][:, :], ACT.Identity)
                                e.wait_ge(mm.h, mm0 + 2 * u + 2)
                                e.activation(ubuf[u % 2][:, :], pU[u % 2][:, :],
                                             ACT.Identity).then_inc(sv.h, 2)
                                sv.n += 2
                                e.wait_ge(ev.h, ev0 + 3 * u + 1)
                                e.activation(sgb[u % 2][:, :], t1b[u % 2][:, :],
                                             ACT.Sigmoid).then_inc(sv.h, 1)
                                sv.n += 1
                    @blk.vector
                    def _(e):
                        for og in range(FFC):
                            for q in range(4):
                                u = og * 4 + q
                                sl = slice(q * TW, (q + 1) * TW)
                                e.wait_ge(sv.h, sv0 + 3 * u + 2)
                                if u >= 2:
                                    e.wait_ge(sv.h, sv0 + 3 * (u - 2) + 3)
                                e.tensor_tensor(t1b[u % 2][:, :], gbuf[u % 2][:, :],
                                                cgb[:, sl], ALU.mult).then_inc(ev.h, 1)
                                ev.n += 1
                                e.wait_ge(sv.h, sv0 + 3 * u + 3)
                                if og >= 2 and q == 0:
                                    e.wait_ge(dg.h, dg0 + 16 * (og - 1))
                                e.tensor_tensor(vt[og % 2][:, sl], gbuf[u % 2][:, :],
                                                ubuf[u % 2][:, :], ALU.mult)
                                e.drain()
                                e.tensor_tensor(vt[og % 2][:, sl], vt[og % 2][:, sl],
                                                sgb[u % 2][:, :], ALU.mult).then_inc(ev.h, 1)
                                ev.n += 1
                                e.drain()
                                e.tensor_tensor(mv[:, sl], vt[og % 2][:, sl], mv[:, sl],
                                                ALU.max)
                                e.tensor_tensor(mvn[:, sl], vt[og % 2][:, sl], mvn[:, sl],
                                                ALU.min).then_inc(ev.h, 1)
                                ev.n += 1
                with nc.Block() as blk:
                    @blk.vector
                    def _(e):
                        e.tensor_scalar(mvn[:, :], mvn[:, :], -1.0, None, ALU.mult)
                        e.tensor_tensor(mv[:, :], mv[:, :], mvn[:, :],
                                        ALU.max).then_inc(ev.h, 1)
                        ev.n += 1
                    @blk.gpsimd
                    def _(e):
                        e.wait_ge(ev.h, ev.n)
                        e.wait_ge(dg.h, dg.n)
                        e.tensor_reduce(r5(2), mv[:, :], AXC, ALU.max).then_inc(gp.h, 1)
                        gp.n += 1
                if debug:
                    d_cg = dbgt("cg", [2, 128, T], F32)
                    d_sc = dbgt("cells", [1, 64], F32)
                    with nc.Block() as blk:
                        @blk.sync
                        def _(e):
                            e.dma_start(AP(d_cg, 0, [[T, 128], [1, T]]),
                                        cgb[:, :]).then_inc(ds.h, 16)
                            ds.n += 16
                            e.dma_start(AP(d_cg, 128 * T, [[T, 128], [1, T]]),
                                        mb[:, :]).then_inc(ds.h, 16)
                            ds.n += 16
                            e.dma_start(AP(d_sc, 0, [[1, 1], [1, 64]]),
                                        scal[0:1, :]).then_inc(ds.h, 16)
                            ds.n += 16
                            e.wait_ge(ds.h, ds.n)
                with nc.Block() as blk:
                    @blk.vector
                    def _(e):
                        e.wait_ge(gp.h, gp.n)
                        e.tensor_tensor(r5(2), r5(2), r5(1), ALU.mult)
                        e.drain()
                        e.tensor_scalar(r5(2), r5(2), 1e-5, None, ALU.max)
                        e.drain()
                        e.reciprocal(r5(3), r5(2))
                        e.drain()
                        e.tensor_tensor(r5(3), r5(3), r5(1), ALU.mult)
                        e.drain()
                        e.tensor_scalar(r5(3), r5(3), 127.0, None, ALU.mult)
                        e.reciprocal(cell(8), cell(2))
                        e.drain()
                        e.tensor_scalar(r5(4), r5(2), 1.0 / 127.0, cell(8),
                                        ALU.mult, ALU.mult)
            if debug:
                d_iq = dbgt("iq", [FC, 128, T], BF16)
                d_vt = dbgt("vt", [FFC, 128, T], BF16)
                d_r5 = dbgt("r5", [5, T], F32)
                d_nw = dbgt("nw", [3, FF * D], BF16)
                with nc.Block() as blk:
                    @blk.sync
                    def _(e):
                        for f in range(FC):
                            e.dma_start(AP(d_iq, f * 128 * T, [[T, 128], [1, T]]),
                                        iq[f][:, :]).then_inc(ds.h, 16)
                            ds.n += 16
                        e.dma_start(AP(d_vt, 0, [[1, FFC * 128 * T]]),
                                    AP(vt_sp, 0, [[1, FFC * 128 * T]])).then_inc(ds.h, 16)
                        ds.n += 16
                        for i in range(5):
                            e.dma_start(AP(d_r5, i * T, [[1, 1], [1, T]]),
                                        r5(i)).then_inc(ds.h, 16)
                            ds.n += 16
                        for i, nat in enumerate([nwg, nwu, nwd]):
                            e.dma_start(AP(d_nw, i * FF * D, [[1, FF * D]]),
                                        AP(nat, 0, [[1, FF * D]])).then_inc(ds.h, 16)
                            ds.n += 16
                        e.wait_ge(ds.h, ds.n)
            psm.close()
            with ExitStack() as psc:
                sbc = lambda n, s, d=F32: psc.enter_context(nc.sbuf_tensor(n, s, d))
                wdT = psc.enter_context(nc.sbuf_tensor("p5wd", [128, FFC * D], BF16))
                fbb = sbc("p5fbb", [128, TW])
                ebb = sbc("p5ebb", [128, TW])
                scr6 = sbc("p5sc6", [128, TW])
                u6 = sbc("p5u6", [128, TW])
                i6 = psc.enter_context(nc.sbuf_tensor("p5i6", [128, TW], I32))
                g6 = sbc("p5g6", [128, TW])
                iq2 = psc.enter_context(nc.sbuf_tensor("p5iqq", [128, FFC * TW], BF16))
                peb = psc.enter_context(nc.psum_tensor("p5peb", [128, TW], F32))
                pfb = psc.enter_context(nc.psum_tensor("p5pfb", [128, TW], F32))
                pout5 = [psc.enter_context(nc.psum_tensor(f"p5po{i}", [128, TW], F32))
                         for i in range(2)]
                with nc.Block() as blk:
                    @blk.sync
                    def _(e):
                        for kc in range(FFC):
                            e.dma_start(wdT[:, kc * D:(kc + 1) * D],
                                        AP(nwd, kc * 128, [[FF, D], [1, 128]]),
                                        transpose=True).then_inc(ds.h, 16)
                            ds.n += 16
                        e.wait_ge(ds.h, ds.n)
                with nc.Block() as blk:
                    mm0 = mm.n; ev0 = ev.n; sv0 = sv.n; ds0 = ds.n
                    @blk.sync
                    def _(e):
                        for tt in range(NTT):
                            if tt >= 1:
                                e.wait_ge(ev.h, ev0 + tt * 72)
                            e.dma_start(iq2[:, :],
                                        AP(vt_sp, tt * TW,
                                           [[T, 128], [128 * T, FFC], [1, TW]])
                                        ).then_inc(ds.h, 16)
                            ds.n += 16
                    @blk.tensor
                    def _(e):
                        for tt in range(NTT):
                            sl = slice(tt * TW, (tt + 1) * TW)
                            if tt > 0:
                                e.wait_ge(ev.h, ev0 + tt * 72)
                            e.matmul(pfb[:, :], ones[0:1, :], r5(3)[:, sl],
                                     start=True, stop=True)
                            e.matmul(peb[:, :], ones[0:1, :], r5(4)[:, sl],
                                     start=True, stop=True).then_inc(mm.h, 2)
                            mm.n += 2
                            e.wait_ge(ev.h, ev0 + tt * 72 + 64)
                            for fcx in range(FC):
                                if fcx >= 2:
                                    e.wait_ge(ev.h, ev0 + tt * 72 + 64 + (fcx - 1))
                                last = None
                                for kc in range(FFC):
                                    last = e.matmul(pout5[fcx % 2][:, :],
                                                    wdT[:, kc * D + fcx * 128:
                                                        kc * D + (fcx + 1) * 128],
                                                    iq2[:, kc * TW:(kc + 1) * TW],
                                                    start=(kc == 0), stop=(kc == FFC - 1))
                                last.then_inc(mm.h, 1); mm.n += 1
                    @blk.scalar
                    def _(e):
                        for tt in range(NTT):
                            for kc in range(FFC):
                                e.wait_ge(ev.h, ev0 + tt * 72 + 2 * kc + 1)
                                e.activation(u6[:, :], scr6[:, :], ACT.Identity,
                                             bias=halfc[:, 0:1]).then_inc(sv.h, 1)
                                sv.n += 1
                    @blk.vector
                    def _(e):
                        for tt in range(NTT):
                            sl = slice(tt * TW, (tt + 1) * TW)
                            e.wait_ge(mm.h, mm0 + tt * 10 + 2)
                            e.wait_ge(ds.h, ds0 + 16 * (tt + 1))
                            e.tensor_copy(fbb[:, :], pfb[:, :])
                            e.tensor_copy(ebb[:, :], peb[:, :])
                            for kc in range(FFC):
                                e.tensor_tensor(scr6[:, :],
                                                iq2[:, kc * TW:(kc + 1) * TW],
                                                fbb[:, :], ALU.mult).then_inc(ev.h, 1)
                                ev.n += 1
                                e.wait_ge(sv.h, sv0 + tt * FFC + kc + 1)
                                e.tensor_copy(i6[:, :], u6[:, :])
                                e.drain()
                                e.tensor_tensor(g6[:, :], i6[:, :], u6[:, :], ALU.is_gt)
                                e.drain()
                                e.scalar_tensor_tensor(iq2[:, kc * TW:(kc + 1) * TW],
                                                       i6[:, :], -1.0, g6[:, :],
                                                       ALU.mult, ALU.add).then_inc(ev.h, 1)
                                ev.n += 1
                            for f in range(FC):
                                e.wait_ge(mm.h, mm0 + tt * 10 + 2 + f + 1)
                                e.tensor_tensor(scr6[:, :], pout5[f % 2][:, :],
                                                ebb[:, :], ALU.mult)
                                e.drain()
                                e.tensor_tensor(X[f][:, sl], X[f][:, sl], scr6[:, :],
                                                ALU.add).then_inc(ev.h, 1)
                                ev.n += 1

        if upto >= 6:
          with ExitStack() as ps:
            psmid = ps.enter_context(ExitStack())
            sb_ = lambda n, s, d=F32: psmid.enter_context(nc.sbuf_tensor(n, s, d))
            HT6 = 1024
            rows6 = sb_("p6rows", [1, 3 * T])
            r6 = lambda i: rows6[0:1, i * T:(i + 1) * T]
            xn6 = [sb_(f"p6xn{f}", [128, T], BF16) for f in range(FC)]
            Opair2 = psmid.enter_context(nc.sbuf_tensor("p6Op", [64, 2 * T], BF16))
            cosb = sb_("p6cos", [128, T]); sinb = sb_("p6sin", [128, T])
            tri0 = psmid.enter_context(nc.sbuf_tensor("p6t0", [128, 128], BF16))
            tri4 = psmid.enter_context(nc.sbuf_tensor("p6t4", [128, 128], BF16))
            qaug = [psmid.enter_context(nc.sbuf_tensor(f"p6qa{h}", [65, T], BF16)) for h in range(2)]
            kaug = [psmid.enter_context(nc.sbuf_tensor(f"p6ka{h}", [65, T], BF16)) for h in range(2)]
            vtm65 = psmid.enter_context(nc.sbuf_tensor("p6vtm", [128, 16 * 130], BF16))
            qf = sb_("p6qf", [128, HT6]); qr = sb_("p6qr", [128, HT6])
            qhat = psmid.enter_context(nc.sbuf_tensor("p6qh", [128, HT6], BF16))
            wqb0 = psmid.enter_context(nc.sbuf_tensor("p6wq0", [128, FC * 128], BF16))
            wqb = [wqb0, wqb0]
            pt6 = [psmid.enter_context(nc.sbuf_tensor(f"p6pt{i}", [128, 640], BF16))
                   for i in range(2)]


            with nc.Block() as blk:
                @blk.sync
                def _(e):
                    e.dma_start(cosb[:, :], W["rope_cos"].ap()).then_inc(ds.h, 16)
                    ds.n += 16
                    e.dma_start(sinb[:, :], W["rope_sin"].ap()).then_inc(ds.h, 16)
                    ds.n += 16
                    e.wait_ge(ds.h, ds.n)
                @blk.gpsimd
                def _(e):
                    e.affine_select(tri0[:, :], onesb[:, :], [[1, 128]], ALU.is_ge, 0.0,
                                    base=0, channel_multiplier=-1)
                    e.affine_select(tri4[:, :], onesb[:, :], [[-1, 128]], ALU.is_gt, 0.0,
                                    base=0, channel_multiplier=1).then_inc(gp.h, 1)
                    gp.n += 1
                @blk.vector
                def _(e):
                    last = None
                    for tc in range(16):
                        e.memset(vtm65[:, tc * 130 + 64:tc * 130 + 65], 1.0)
                        last = e.memset(vtm65[:, tc * 130 + 129:tc * 130 + 130], 1.0)
                    last.then_inc(ev.h, 1)
                    ev.n += 1

            with ExitStack() as psa:
                sba = lambda n, s, d=F32: psa.enter_context(nc.sbuf_tensor(n, s, d))
                sq6 = sba("p6sq6", [128, TW])
                scr = [sba(f"p6scr{i}", [128, TW]) for i in range(2)]
                pst1 = psa.enter_context(nc.psum_tensor("p6s1", [1, TW], F32))
                pst2 = psa.enter_context(nc.psum_tensor("p6s2", [1, TW], F32))
                pbA = psa.enter_context(nc.psum_tensor("p6bA", [128, TW], F32))
                pbB = psa.enter_context(nc.psum_tensor("p6bB", [128, TW], F32))
                with nc.Block() as blk:
                    sv0 = sv.n; mm0 = mm.n; ev0 = ev.n
                    @blk.scalar
                    def _(e):
                        i = 0
                        for tt in range(NTT):
                            for f in range(FC):
                                if i >= 1:
                                    e.wait_ge(mm.h, mm0 + 2 * i)
                                e.activation(sq6[:, :], X[f][:, tt * TW:(tt + 1) * TW],
                                             ACT.Square).then_inc(sv.h, 1)
                                sv.n += 1
                                i += 1
                    @blk.tensor
                    def _(e):
                        e.wait_ge(ev.h, ev0)
                        i = 0
                        for tt in range(NTT):
                            for f in range(FC):
                                if f == 0 and tt > 0:
                                    e.wait_ge(ev.h, ev0 + 2 * tt)
                                e.matmul(pst1[:, :], ones[:, 0:1],
                                         X[f][:, tt * TW:(tt + 1) * TW],
                                         start=(f == 0), stop=(f == FC - 1)).then_inc(mm.h, 1)
                                mm.n += 1
                                e.wait_ge(sv.h, sv0 + i + 1)
                                e.matmul(pst2[:, :], ones[:, 0:1], sq6[:, :],
                                         start=(f == 0), stop=(f == FC - 1)).then_inc(mm.h, 1)
                                mm.n += 1
                                i += 1
                    @blk.vector
                    def _(e):
                        for tt in range(NTT):
                            e.wait_ge(mm.h, mm0 + (tt + 1) * 16)
                            sl = slice(tt * TW, (tt + 1) * TW)
                            e.tensor_scalar(r6(0)[:, sl], pst1[:, :], 1.0 / D, None, ALU.mult)
                            e.tensor_scalar(r6(2)[:, sl], pst2[:, :], 1.0 / D, None,
                                            ALU.mult).then_inc(ev.h, 2)
                            ev.n += 2
                with nc.Block() as blk:
                    @blk.vector
                    def _(e):
                        e.tensor_tensor(r6(1), r6(0), r6(0), ALU.mult)
                        e.drain()
                        e.tensor_tensor(r6(1), r6(2), r6(1), ALU.subtract)
                        e.drain()
                        e.tensor_scalar(r6(2), r6(1), 1e-5, None, ALU.add).then_inc(ev.h, 1)
                        ev.n += 1
                    @blk.scalar
                    def _(e):
                        e.wait_ge(ev.h, ev.n)
                        e.activation(r6(1), r6(2), ACT.Ln)
                        e.drain()
                        e.activation(r6(1), r6(1), ACT.Exp, scale=-0.5)
                with nc.Block() as blk:
                    @blk.vector
                    def _(e):
                        e.tensor_tensor(r6(0), r6(0), r6(1), ALU.mult)
                        e.drain()
                        e.tensor_scalar(r6(0), r6(0), -1.0, None,
                                        ALU.mult).then_inc(ev.h, 1)
                        ev.n += 1
                with nc.Block() as blk:
                    mm0 = mm.n; ev0 = ev.n; sv0 = sv.n
                    @blk.tensor
                    def _(e):
                        e.wait_ge(ev.h, ev0)
                        for tt in range(NTT):
                            sl = slice(tt * TW, (tt + 1) * TW)
                            if tt > 0:
                                e.wait_ge(ev.h, ev0 + tt * 8)
                            e.matmul(pbA[:, :], ones[0:1, :], r6(1)[:, sl], start=True, stop=True)
                            e.matmul(pbB[:, :], ones[0:1, :], r6(0)[:, sl],
                                     start=True, stop=True).then_inc(mm.h, 2)
                            mm.n += 2
                    @blk.vector
                    def _(e):
                        QW = TW // 4
                        qs = [q * QW for q in range(4)]
                        for tt in range(NTT):
                            e.wait_ge(mm.h, mm0 + 2 * (tt + 1))
                            for f in range(FC):
                                b = scr[f % 2]
                                for h0 in qs:
                                    e.tensor_tensor(b[:, h0:h0 + QW],
                                                    X[f][:, tt * TW + h0:tt * TW + h0 + QW],
                                                    pbA[:, h0:h0 + QW], ALU.mult)
                                e.drain()
                                ins = None
                                for h0 in qs:
                                    ins = e.tensor_tensor(
                                        xn6[f][:, tt * TW + h0:tt * TW + h0 + QW],
                                        b[:, h0:h0 + QW], pbB[:, h0:h0 + QW], ALU.add)
                                ins.then_inc(ev.h, 1)
                                ev.n += 1

            for p_ in range(8):
              with ExitStack() as psp:
                psqkv = psp.enter_context(ExitStack())
                pqk = [psqkv.enter_context(nc.psum_tensor(f"p6qk{p_}{i}", [128, TW], F32))
                       for i in range(2)]
                pnrm = psqkv.enter_context(nc.psum_tensor(f"p6nr{p_}", [1, TW], F32))
                for which, coloff in [("q", 128 * p_), ("k", D + 128 * p_)]:
                    with nc.Block() as blk:
                        dg0 = dg.n; mm0 = mm.n; sv0 = sv.n; ev0 = ev.n; ds0 = ds.n
                        @blk.gpsimd
                        def _(e):
                            e.dma_start(wqb0[:, :],
                                        AP(W["L1_attn_Wqkv"], coloff,
                                           [[3 * D, 128], [128 * 3 * D, FC], [1, 128]])
                                        ).then_inc(dg.h, 16)
                            dg.n += 16
                        @blk.tensor
                        def _(e):
                            e.wait_ge(dg.h, dg0 + 16)
                            e.wait_ge(sv.h, sv0)
                            for hh in range(2):
                                for q2 in range(2):
                                    if hh + q2 >= 1:
                                        e.wait_ge(sv.h, sv0 + 2 * (2 * hh + q2))
                                    last = None
                                    for kc in range(FC):
                                        last = e.matmul(
                                            pqk[q2][:, :],
                                            wqb0[:, kc * 128:(kc + 1) * 128],
                                            xn6[kc][:, (2 * hh + q2) * TW:(2 * hh + q2 + 1) * TW],
                                            start=(kc == 0), stop=(kc == FC - 1))
                                    last.then_inc(mm.h, 1); mm.n += 1
                        @blk.scalar
                        def _(e):
                            for hh in range(2):
                                if hh == 1:
                                    e.wait_ge(ds.h, ds0 + 96)
                                    e.wait_ge(ev.h, ev0 + 1)
                                for q2 in range(2):
                                    i = 2 * hh + q2
                                    e.wait_ge(mm.h, mm0 + i + 1)
                                    if which == "q":
                                        e.activation(qf[:, q2 * TW:(q2 + 1) * TW],
                                                     pqk[q2][:, :], ACT.Identity,
                                                     scale=pcol[:, 0:1]).then_inc(sv.h, 2)
                                    else:
                                        e.activation(qf[:, q2 * TW:(q2 + 1) * TW],
                                                     pqk[q2][:, :],
                                                     ACT.Identity).then_inc(sv.h, 2)
                                    sv.n += 2
                        @blk.sync
                        def _(e):
                            dsts = qaug if which == "q" else kaug
                            for hh in range(2):
                                hsl = slice(hh * HT6, (hh + 1) * HT6)
                                e.wait_ge(sv.h, sv0 + 4 * hh + 4)
                                for (d0, s0) in [(0, 32), (32, 0), (64, 96), (96, 64)]:
                                    e.dma_start(qr[d0:d0 + 32, :], qf[s0:s0 + 32, :]
                                                ).then_inc(ds.h, 16)
                                    ds.n += 16
                                e.wait_ge(ev.h, ev0 + hh + 1)
                                e.dma_start(dsts[0][0:64, hsl],
                                            qhat[0:64, :]).then_inc(ds.h, 16)
                                ds.n += 16
                                e.dma_start(dsts[1][0:64, hsl],
                                            qhat[64:128, :]).then_inc(ds.h, 16)
                                ds.n += 16
                            e.wait_ge(ds.h, ds.n)
                        @blk.vector
                        def _(e):
                            for hh in range(2):
                                e.wait_ge(ds.h, ds0 + 96 * hh + 64)
                                hsl = slice(hh * HT6, (hh + 1) * HT6)
                                e.drain()
                                e.tensor_tensor(qf[:, :], qf[:, :], cosb[:, hsl], ALU.mult)
                                e.tensor_tensor(qr[:, :], qr[:, :], sinb[:, hsl], ALU.mult)
                                e.drain()
                                e.tensor_tensor(qhat[:, :], qf[:, :], qr[:, :],
                                                ALU.add).then_inc(ev.h, 1)
                                ev.n += 1
                with nc.Block() as blk:
                    @blk.vector
                    def _(e):
                        e.memset(kaug[0][64:65, :], -1.0)
                        e.memset(kaug[1][64:65, :], -1.0).then_inc(ev.h, 1)
                        ev.n += 1
                for hl in range(2):
                    with nc.Block() as blk:
                        mm0 = mm.n; sv0 = sv.n; ev0 = ev.n
                        @blk.scalar
                        def _(e):
                            for hh in range(2):
                                if hh > 0:
                                    e.wait_ge(mm.h, mm0 + 2 * hh)
                                hsl = slice(hh * HT6, (hh + 1) * HT6)
                                e.activation(qhat[0:64, :], qaug[hl][0:64, hsl],
                                             ACT.Square).then_inc(sv.h, 1)
                                sv.n += 1
                        @blk.tensor
                        def _(e):
                            for tt in range(NTT):
                                e.wait_ge(sv.h, sv0 + tt // 2 + 1)
                                if tt > 0:
                                    e.wait_ge(ev.h, ev0 + tt)
                                e.matmul(pnrm[:, :], onesb[0:64, 0:1],
                                         qhat[0:64, (tt % 2) * TW:(tt % 2 + 1) * TW],
                                         start=True, stop=True).then_inc(mm.h, 1)
                                mm.n += 1
                        @blk.vector
                        def _(e):
                            for tt in range(NTT):
                                e.wait_ge(mm.h, mm0 + tt + 1)
                                e.tensor_copy(r6(2)[:, tt * TW:(tt + 1) * TW],
                                              pnrm[:, :]).then_inc(ev.h, 1)
                                ev.n += 1
                    with nc.Block() as blk:
                        mm0 = mm.n; sv0 = sv.n; ev0 = ev.n
                        @blk.scalar
                        def _(e):
                            for hh in range(2):
                                if hh > 0:
                                    e.wait_ge(mm.h, mm0 + 2 * hh)
                                hsl = slice(hh * HT6, (hh + 1) * HT6)
                                e.activation(qhat[0:64, :], kaug[hl][0:64, hsl],
                                             ACT.Square).then_inc(sv.h, 1)
                                sv.n += 1
                        @blk.tensor
                        def _(e):
                            for tt in range(NTT):
                                e.wait_ge(sv.h, sv0 + tt // 2 + 1)
                                if tt > 0:
                                    e.wait_ge(ev.h, ev0 + tt)
                                e.matmul(pnrm[:, :], onesb[0:64, 0:1],
                                         qhat[0:64, (tt % 2) * TW:(tt % 2 + 1) * TW],
                                         start=True, stop=True).then_inc(mm.h, 1)
                                mm.n += 1
                        @blk.vector
                        def _(e):
                            for tt in range(NTT):
                                e.wait_ge(mm.h, mm0 + tt + 1)
                                e.reduce_max(scal[0:1, 12 + tt:13 + tt], pnrm[:, :],
                                             AXX).then_inc(ev.h, 1)
                                ev.n += 1
                            e.drain()
                            e.tensor_scalar(cell(12), cell(12), cell(13), None, ALU.max)
                            e.drain()
                            e.tensor_scalar(cell(12), cell(12), cell(14), None, ALU.max)
                            e.drain()
                            e.tensor_scalar(cell(12), cell(12), cell(15), None, ALU.max)
                            e.drain()
                    with nc.Block() as blk:
                        @blk.scalar
                        def _(e):
                            e.activation(r6(2), r6(2), ACT.Sqrt)
                            e.activation(cell(12), cell(12), ACT.Sqrt)
                            e.nop().then_inc(sv.h, 1); sv.n += 1
                    with nc.Block() as blk:
                        @blk.vector
                        def _(e):
                            e.tensor_scalar(cell(12), cell(12), 1.0625, None, ALU.mult)
                            e.drain()
                            e.tensor_scalar(r6(2), r6(2), cell(12), 1.0, ALU.mult,
                                            ALU.add).then_inc(ev.h, 1)
                            ev.n += 1
                    with nc.Block() as blk:
                        @blk.gpsimd
                        def _(e):
                            e.wait_ge(ev.h, ev.n)
                            e.dma_start(qaug[hl][64:65, :], r6(2)).then_inc(dg.h, 16)
                            dg.n += 16
                            e.wait_ge(dg.h, dg.n)
                with nc.Block() as blk:
                    dg0 = dg.n; mm0 = mm.n; sv0 = sv.n
                    @blk.gpsimd
                    def _(e):
                        e.dma_start(wqb0[:, :],
                                    AP(W["L1_attn_Wqkv"], 2 * D + 128 * p_,
                                       [[3 * D, 128], [128 * 3 * D, FC], [1, 128]])
                                    ).then_inc(dg.h, 16)
                        dg.n += 16
                    @blk.tensor
                    def _(e):
                        e.wait_ge(dg.h, dg0 + 16)
                        for tc in range(16):
                            if tc >= 2:
                                e.wait_ge(sv.h, sv0 + tc - 1)
                            last = None
                            for kc in range(FC):
                                last = e.matmul(pqk[tc % 2][:, 0:128],
                                                xn6[kc][:, tc * 128:(tc + 1) * 128],
                                                wqb0[:, kc * 128:(kc + 1) * 128],
                                                start=(kc == 0), stop=(kc == FC - 1))
                            last.then_inc(mm.h, 1); mm.n += 1
                    @blk.scalar
                    def _(e):
                        for tc in range(16):
                            e.wait_ge(mm.h, mm0 + tc + 1)
                            e.activation(vtm65[:, tc * 130:tc * 130 + 64],
                                         pqk[tc % 2][:, 0:64], ACT.Identity)
                            e.activation(vtm65[:, tc * 130 + 65:tc * 130 + 129],
                                         pqk[tc % 2][:, 64:128],
                                         ACT.Identity).then_inc(sv.h, 1)
                            sv.n += 1
                psqkv.close()
                with ExitStack() as psat:
                    pst6 = [psat.enter_context(nc.psum_tensor(f"p6st{p_}{i}", [128, 640], F32))
                            for i in range(2)]
                    po6 = [psat.enter_context(nc.psum_tensor(f"p6po{p_}{i}", [128, 128], F32))
                           for i in range(2)]
                    prb6 = [psat.enter_context(nc.psum_tensor(f"p6rb{p_}{i}", [128, TW], F32))
                            for i in range(2)]
                    srow = psat.enter_context(nc.sbuf_tensor(f"p6sr{p_}", [1, 2 * T], F32))
                    with nc.Block() as blk:
                        mm0 = mm.n; sv0 = sv.n; ev0 = ev.n
                        NI = 32
                        steps = [(hl, qt) for hl in range(2) for qt in range(16)]
                        _posST = {}; _posPV = {}
                        _c = 0
                        for _j in (0, 1):
                            _c += 1; _posST[_j] = _c
                        for _i in range(NI):
                            _c += 1; _posPV[_i] = _c
                            if _i + 2 < NI:
                                _c += 1; _posST[_i + 2] = _c
                        def emit_ST(e, j):
                            hl, qt = steps[j]
                            kcs = list(range(max(0, qt - 4), qt + 1))
                            if j >= 2:
                                e.wait_ge(sv.h, sv0 + (j - 2) + 1)
                            last = None
                            for ki, kc in enumerate(kcs):
                                last = e.matmul(pst6[j % 2][:, 128 * ki:128 * (ki + 1)],
                                                kaug[hl][:, 128 * kc:128 * (kc + 1)],
                                                qaug[hl][:, 128 * qt:128 * (qt + 1)],
                                                start=True, stop=True)
                            last.then_inc(mm.h, 1); mm.n += 1
                        def emit_PV(e, i):
                            hl, qt = steps[i]
                            kcs = list(range(max(0, qt - 4), qt + 1))
                            e.wait_ge(ev.h, ev0 + 2 * i + 1)
                            last = None
                            for ki, kc in enumerate(kcs):
                                last = e.matmul(po6[i % 2][0:65, :],
                                                vtm65[:, kc * 130 + 65 * hl:
                                                      kc * 130 + 65 * hl + 65],
                                                pt6[i % 2][:, 128 * ki:128 * (ki + 1)],
                                                start=(ki == 0), stop=(ki == len(kcs) - 1))
                            last.then_inc(mm.h, 1); mm.n += 1
                        @blk.tensor
                        def _(e):
                            e.wait_ge(ev.h, ev0)
                            emit_ST(e, 0)
                            emit_ST(e, 1)
                            for i in range(NI):
                                emit_PV(e, i)
                                if i + 2 < NI:
                                    emit_ST(e, i + 2)
                        @blk.scalar
                        def _(e):
                            for i in range(NI):
                                hl, qt = steps[i]
                                nk = min(qt, 4) + 1
                                e.wait_ge(mm.h, mm0 + _posST[i])
                                e.activation(pt6[i % 2][:, 0:128 * nk],
                                             pst6[i % 2][:, 0:128 * nk],
                                             ACT.Exp).then_inc(sv.h, 1)
                                sv.n += 1
                        @blk.vector
                        def _(e):
                            for i in range(NI):
                                hl, qt = steps[i]
                                nk = min(qt, 4) + 1
                                e.wait_ge(sv.h, sv0 + i + 1)
                                ins = e.tensor_tensor(
                                    pt6[i % 2][:, 128 * (nk - 1):128 * nk],
                                    pt6[i % 2][:, 128 * (nk - 1):128 * nk],
                                    tri0[:, :], ALU.mult)
                                if qt >= 4:
                                    ins = e.tensor_tensor(pt6[i % 2][:, 0:128],
                                                          pt6[i % 2][:, 0:128],
                                                          tri4[:, :], ALU.mult)
                                ins.then_inc(ev.h, 1)
                                ev.n += 1
                                e.wait_ge(mm.h, mm0 + _posPV[i])
                                e.tensor_copy(Opair2[0:64, hl * T + 128 * qt:
                                                      hl * T + 128 * (qt + 1)],
                                              po6[i % 2][0:64, :])
                                e.tensor_copy(srow[0:1, hl * T + 128 * qt: hl * T + 128 * (qt + 1)],
                                              po6[i % 2][64:65, :]).then_inc(ev.h, 1)
                                ev.n += 1
                    with nc.Block() as blk:
                        mm0 = mm.n; ev0 = ev.n; sv0 = sv.n
                        @blk.scalar
                        def _(e):
                            e.wait_ge(ev.h, ev0)
                            e.activation(rows6[0:1, 0:2 * T], srow[0:1, 0:2 * T], ACT.Ln)
                            e.drain()
                            e.activation(srow[0:1, 0:2 * T], rows6[0:1, 0:2 * T],
                                         ACT.Exp, scale=-1.0).then_inc(sv.h, 1)
                            sv.n += 1
                        @blk.tensor
                        def _(e):
                            e.wait_ge(sv.h, sv0 + 1)
                            k = 0
                            for hl in range(2):
                                for tt in range(NTT):
                                    if k >= 2:
                                        e.wait_ge(ev.h, ev0 + k - 1)
                                    e.matmul(prb6[k % 2][0:64, :],
                                             ones[0:1, 0:64],
                                             srow[0:1, hl * T + tt * TW: hl * T + (tt + 1) * TW],
                                             start=True, stop=True).then_inc(mm.h, 1)
                                    mm.n += 1
                                    k += 1
                        @blk.vector
                        def _(e):
                            e.drain()
                            k = 0
                            for hl in range(2):
                                for tt in range(NTT):
                                    e.wait_ge(mm.h, mm0 + k + 1)
                                    e.tensor_tensor(Opair2[0:64, hl * T + tt * TW:
                                                           hl * T + (tt + 1) * TW],
                                                    Opair2[0:64, hl * T + tt * TW:
                                                           hl * T + (tt + 1) * TW],
                                                    prb6[k % 2][0:64, :],
                                                    ALU.mult).then_inc(ev.h, 1)
                                    ev.n += 1
                                    k += 1
                with nc.Block() as blk:
                    @blk.sync
                    def _(e):
                        e.wait_ge(ev.h, ev.n)
                        for hl in range(2):
                            e.dma_start(AP(o_sp, p_ * 128 * T + hl * 64 * T,
                                           [[T, 64], [1, T]]),
                                        Opair2[0:64, hl * T:(hl + 1) * T]).then_inc(ds.h, 16)
                            ds.n += 16
                        e.wait_ge(ds.h, ds.n)
                if debug and p_ == 0:
                    d_aug = dbgt("aug", [4, 65, T], BF16)
                    d_vtm = dbgt("vtm", [128, 16 * 130], BF16)
                    with nc.Block() as blk:
                        @blk.sync
                        def _(e):
                            for i_, tile in enumerate([qaug[0], qaug[1], kaug[0], kaug[1]]):
                                e.dma_start(AP(d_aug, i_ * 65 * T, [[T, 65], [1, T]]),
                                            tile[:, :]).then_inc(ds.h, 16)
                                ds.n += 16
                            e.dma_start(AP(d_vtm, 0, [[16 * 130, 128], [1, 16 * 130]]),
                                        vtm65[:, :]).then_inc(ds.h, 16)
                            ds.n += 16
                            e.wait_ge(ds.h, ds.n)

            psmid.close()
            with ExitStack() as psw:
                wo6 = psw.enter_context(nc.sbuf_tensor("p6wo", [128, FC * D], BF16))
                ost = [psw.enter_context(nc.sbuf_tensor(f"p6os{i}", [128, FC * TW], BF16))
                       for i in range(2)]
                pout6 = [psw.enter_context(nc.psum_tensor(f"p6ou{i}", [128, TW], F32))
                         for i in range(2)]
                with nc.Block() as blk:
                    dg0 = dg.n
                    @blk.gpsimd
                    def _(e):
                        e.dma_start(wo6[:, :],
                                    AP(W["L1_attn_Wout"], 0,
                                       [[D, 128], [128 * D, FC], [1, D]])
                                    ).then_inc(dg.h, 16)
                        dg.n += 16
                        e.wait_ge(dg.h, dg.n)
                with nc.Block() as blk:
                    mm0 = mm.n; ev0 = ev.n; ds0 = ds.n
                    @blk.sync
                    def _(e):
                        for tt in range(NTT):
                            if tt >= 2:
                                e.wait_ge(mm.h, mm0 + (tt - 1) * FC)
                            e.dma_start(ost[tt % 2][:, :],
                                        AP(o_sp, tt * TW,
                                           [[T, 128], [128 * T, FC], [1, TW]])
                                        ).then_inc(ds.h, 16)
                            ds.n += 16
                    @blk.tensor
                    def _(e):
                        e.wait_ge(ev.h, ev0)
                        i = 0
                        for tt in range(NTT):
                            e.wait_ge(ds.h, ds0 + 16 * (tt + 1))
                            for fcx in range(FC):
                                if i >= 2:
                                    e.wait_ge(ev.h, ev0 + i - 1)
                                last = None
                                for kc in range(FC):
                                    last = e.matmul(pout6[i % 2][:, :],
                                                    wo6[:, kc * D + fcx * 128:
                                                        kc * D + (fcx + 1) * 128],
                                                    ost[tt % 2][:, kc * TW:(kc + 1) * TW],
                                                    start=(kc == 0), stop=(kc == FC - 1))
                                last.then_inc(mm.h, 1); mm.n += 1
                                i += 1
                    @blk.vector
                    def _(e):
                        i = 0
                        for tt in range(NTT):
                            sl = slice(tt * TW, (tt + 1) * TW)
                            for fcx in range(FC):
                                e.wait_ge(mm.h, mm0 + i + 1)
                                e.tensor_tensor(X[fcx][:, sl], X[fcx][:, sl],
                                                pout6[i % 2][:, :], ALU.add).then_inc(ev.h, 1)
                                ev.n += 1
                                i += 1

        if upto >= 7:
          with ExitStack() as ps:
            sb_ = lambda n, s, d=F32: ps.enter_context(nc.sbuf_tensor(n, s, d))
            rows7 = sb_("p7rows", [1, 3 * T])
            r7 = lambda i: rows7[0:1, i * T:(i + 1) * T]
            ones_row = sb_("p7or", [1, TW])
            brow1 = sb_("p7br", [1, D])
            with ExitStack() as psm7:
                sbm = lambda n, s, d=F32: psm7.enter_context(nc.sbuf_tensor(n, s, d))
                scr7 = sbm("p7sc", [128, TW])
                u7 = sbm("p7u", [128, TW])
                i7 = psm7.enter_context(nc.sbuf_tensor("p7i", [128, TW], I32))
                g7 = sbm("p7g", [128, TW])
                ps_iq = psm7.enter_context(ExitStack())
                iq7 = [ps_iq.enter_context(nc.sbuf_tensor(f"p7iq{f}", [128, T], BF16))
                       for f in range(FC)]
                mx7 = ps_iq.enter_context(nc.sbuf_tensor("p7mx", [128, T], F32))
                pst1 = psm7.enter_context(nc.psum_tensor("p7s1", [1, TW], F32))
                pst2 = psm7.enter_context(nc.psum_tensor("p7s2", [1, TW], F32))
                pbA = psm7.enter_context(nc.psum_tensor("p7bA", [128, TW], F32))
                pbB = psm7.enter_context(nc.psum_tensor("p7bB", [128, TW], F32))
                with nc.Block() as blk:
                    sv0 = sv.n; mm0 = mm.n; ev0 = ev.n
                    @blk.scalar
                    def _(e):
                        i = 0
                        for tt in range(NTT):
                            for f in range(FC):
                                if i >= 1:
                                    e.wait_ge(mm.h, mm0 + 2 * i)
                                e.activation(u7[:, :], X[f][:, tt * TW:(tt + 1) * TW],
                                             ACT.Square).then_inc(sv.h, 1)
                                sv.n += 1
                                i += 1
                    @blk.tensor
                    def _(e):
                        e.wait_ge(ev.h, ev0)
                        i = 0
                        for tt in range(NTT):
                            for f in range(FC):
                                if f == 0 and tt > 0:
                                    e.wait_ge(ev.h, ev0 + 2 * tt)
                                e.matmul(pst1[:, :], ones[:, 0:1],
                                         X[f][:, tt * TW:(tt + 1) * TW],
                                         start=(f == 0), stop=(f == FC - 1)).then_inc(mm.h, 1)
                                mm.n += 1
                                e.wait_ge(sv.h, sv0 + i + 1)
                                e.matmul(pst2[:, :], ones[:, 0:1], u7[:, :],
                                         start=(f == 0), stop=(f == FC - 1)).then_inc(mm.h, 1)
                                mm.n += 1
                                i += 1
                    @blk.vector
                    def _(e):
                        for tt in range(NTT):
                            e.wait_ge(mm.h, mm0 + (tt + 1) * 16)
                            sl = slice(tt * TW, (tt + 1) * TW)
                            e.tensor_scalar(r7(0)[:, sl], pst1[:, :], 1.0 / D, None, ALU.mult)
                            e.tensor_scalar(r7(2)[:, sl], pst2[:, :], 1.0 / D, None,
                                            ALU.mult).then_inc(ev.h, 2)
                            ev.n += 2
                with nc.Block() as blk:
                    @blk.vector
                    def _(e):
                        e.tensor_tensor(r7(1), r7(0), r7(0), ALU.mult)
                        e.drain()
                        e.tensor_tensor(r7(1), r7(2), r7(1), ALU.subtract)
                        e.drain()
                        e.tensor_scalar(r7(2), r7(1), 1e-5, None, ALU.add).then_inc(ev.h, 1)
                        ev.n += 1
                    @blk.scalar
                    def _(e):
                        e.wait_ge(ev.h, ev.n)
                        e.activation(r7(1), r7(2), ACT.Ln)
                        e.drain()
                        e.activation(r7(1), r7(1), ACT.Exp, scale=-0.5)
                with nc.Block() as blk:
                    @blk.vector
                    def _(e):
                        e.tensor_tensor(r7(0), r7(0), r7(1), ALU.mult)
                        e.drain()
                        e.tensor_scalar(r7(0), r7(0), -1.0, None, ALU.mult).then_inc(ev.h, 1)
                        ev.n += 1
                with nc.Block() as blk:
                    mm0 = mm.n; ev0 = ev.n
                    @blk.tensor
                    def _(e):
                        e.wait_ge(ev.h, ev0)
                        for tt in range(NTT):
                            sl = slice(tt * TW, (tt + 1) * TW)
                            if tt > 0:
                                e.wait_ge(ev.h, ev0 + tt * 8)
                            e.matmul(pbA[:, :], ones[0:1, :], r7(1)[:, sl], start=True, stop=True)
                            e.matmul(pbB[:, :], ones[0:1, :], r7(0)[:, sl],
                                     start=True, stop=True).then_inc(mm.h, 2)
                            mm.n += 2
                    @blk.vector
                    def _(e):
                        for tt in range(NTT):
                            e.wait_ge(mm.h, mm0 + 2 * (tt + 1))
                            sl = slice(tt * TW, (tt + 1) * TW)
                            ins = None
                            for f in range(FC):
                                e.tensor_tensor(scr7[:, :], X[f][:, sl], pbA[:, :], ALU.mult)
                                e.drain()
                                e.tensor_tensor(scr7[:, :], scr7[:, :], pbB[:, :], ALU.add)
                                e.drain()
                                col = tt * FC + f
                                ins = e.reduce_max(mx7[:, col:col + 1], scr7[:, :], AXX)
                            ins.then_inc(ev.h, 8)
                            ev.n += 8
                        e.drain()
                        e.nop().then_inc(ev.h, 1)
                        ev.n += 1
                with nc.Block() as blk:
                    @blk.gpsimd
                    def _(e):
                        e.wait_ge(ev.h, ev.n)
                        e.tensor_reduce(cell(5), mx7[:, 0:32], AXALL,
                                        ALU.max).then_inc(gp.h, 1)
                        gp.n += 1
                with nc.Block() as blk:
                    @blk.gpsimd
                    def _(e):
                        e.wait_ge(gp.h, gp.n)
                        e.dma_start(cc_b[0].ap(), cell(5)).then_inc(dg.h, 16)
                        dg.n += 16
                        e.wait_ge(dg.h, dg.n)
                        e.collective_compute(
                            "AllReduce", ALU.max,
                            replica_groups=[list(range(NCORES))],
                            ins=[cc_b[0].ap().opt()],
                            outs=[cc_b[1].ap().opt()],
                        ).then_inc(cc.h, 1)
                        cc.n += 1
                        e.wait_ge(cc.h, cc.n)
                        e.dma_start(cell(5), cc_b[1].ap()).then_inc(dg.h, 16)
                        dg.n += 16
                        e.wait_ge(dg.h, dg.n)
                with nc.Block() as blk:
                    @blk.vector
                    def _(e):
                        e.tensor_scalar(cell(5), cell(5), 1e-8, None, ALU.max)
                        e.drain()
                        e.tensor_scalar(cell(6), cell(5), 1.0 / 255.0, None, ALU.mult)
                        e.reciprocal(cell(7), cell(5))
                        e.drain()
                        e.tensor_scalar(cell(7), cell(7), 255.0, None, ALU.mult)
                        e.drain()
                        e.tensor_scalar(r7(1), r7(1), cell(7), None, ALU.mult)
                        e.tensor_scalar(r7(0), r7(0), cell(7), 0.5, ALU.mult,
                                        ALU.add).then_inc(ev.h, 1)
                        ev.n += 1
                    @blk.tensor
                    def _(e):
                        e.wait_ge(ev.h, ev.n)
                        e.matmul(pbA[:, 0:1], ones[0:1, :], cell(6),
                                 start=True, stop=True).then_inc(mm.h, 1)
                        mm.n += 1
                with nc.Block() as blk:
                    @blk.vector
                    def _(e):
                        e.wait_ge(mm.h, mm.n)
                        e.tensor_copy(pcol[:, 2:3], pbA[:, 0:1]).then_inc(ev.h, 1)
                        ev.n += 1
                with nc.Block() as blk:
                    mm0 = mm.n; ev0 = ev.n; sv0 = sv.n
                    @blk.tensor
                    def _(e):
                        e.wait_ge(ev.h, ev0)
                        for tt in range(NTT):
                            sl = slice(tt * TW, (tt + 1) * TW)
                            if tt > 0:
                                e.wait_ge(ev.h, ev0 + tt * 3 * FC)
                            e.matmul(pbA[:, :], ones[0:1, :], r7(1)[:, sl], start=True, stop=True)
                            e.matmul(pbB[:, :], ones[0:1, :], r7(0)[:, sl],
                                     start=True, stop=True).then_inc(mm.h, 2)
                            mm.n += 2
                    @blk.vector
                    def _(e):
                        for tt in range(NTT):
                            e.wait_ge(mm.h, mm0 + 2 * (tt + 1))
                            sl = slice(tt * TW, (tt + 1) * TW)
                            for f in range(FC):
                                e.tensor_tensor(scr7[:, :], X[f][:, sl], pbA[:, :], ALU.mult)
                                e.drain()
                                e.tensor_tensor(scr7[:, :], scr7[:, :], pbB[:, :], ALU.add)
                                e.drain()
                                e.tensor_copy(i7[:, :], scr7[:, :])
                                e.drain()
                                e.tensor_tensor(g7[:, :], i7[:, :], scr7[:, :], ALU.is_gt)
                                e.drain()
                                e.scalar_tensor_tensor(g7[:, :], i7[:, :], -1.0,
                                                       g7[:, :], ALU.mult, ALU.add)
                                e.drain()
                                e.tensor_scalar(iq7[f][:, sl], g7[:, :], 0.0, None,
                                                ALU.min).then_inc(ev.h, 3)
                                ev.n += 3
                with ExitStack() as ps1:
                    HF = FF // 2
                    w1T = ps1.enter_context(nc.sbuf_tensor("p7w1", [128, FC * HF], BF16))
                    hsl_t = ps1.enter_context(nc.sbuf_tensor("p7h", [128, 16 * TW], BF16))
                    ph7 = [ps1.enter_context(nc.psum_tensor(f"p7ph{i}", [128, TW], F32))
                           for i in range(2)]
                    with nc.Block() as blk:
                        @blk.vector
                        def _(e):
                            e.tensor_scalar(sc1[:, :], sw1[:, :], pcol[:, 2:3], None, ALU.mult)
                            e.drain()
                            e.reciprocal(bq1[:, :], sc1[:, :])
                            e.drain()
                            e.tensor_tensor(bq1[:, :], b1v[:, :], bq1[:, :], ALU.mult)
                            e.drain()
                            e.tensor_scalar(bq1[:, :], bq1[:, :], 0.5, None, ALU.add)
                            e.drain()
                            e.memset(mx7[:, :], 0.0)
                            e.drain()
                        @blk.gpsimd
                        def _(e):
                            pass
                    with nc.Block() as blk:
                        @blk.vector
                        def _(e):
                            e.tensor_copy(i7[:, 0:FFC], bq1[:, :])
                            e.tensor_tensor(g7[:, 0:FFC], i7[:, 0:FFC], bq1[:, :], ALU.is_gt)
                            e.drain()
                            e.scalar_tensor_tensor(bq1[:, :], i7[:, 0:FFC], 1.0,
                                                   g7[:, 0:FFC], ALU.mult, ALU.subtract)
                            e.drain()
                            e.tensor_tensor(bq1[:, :], bq1[:, :], sc1[:, :], ALU.mult)
                            e.drain()
                            e.nop().then_inc(ev.h, 1)
                            ev.n += 1
                    with nc.Block() as blk:
                        mm0 = mm.n; sv0 = sv.n; ev0 = ev.n; ds0 = ds.n; dg0 = dg.n
                        @blk.sync
                        def _(e):
                            for oh2 in range(2):
                                if oh2 > 0:
                                    e.wait_ge(mm.h, mm0 + 64 * oh2)
                                for kc in range(FC):
                                    e.dma_start(w1T[:, kc * HF:(kc + 1) * HF],
                                                AP(nw1, oh2 * HF * D + kc * 128,
                                                   [[D, HF], [1, 128]]),
                                                transpose=True).then_inc(dg.h, 16)
                                    dg.n += 16
                                for tt in range(NTT):
                                    s_ = oh2 * NTT + tt
                                    e.wait_ge(ev.h, ev0 + (s_ + 1) * 16)
                                    e.dma_start(AP(h_sp, oh2 * 16 * 128 * T + tt * TW,
                                                   [[T, 128], [128 * T, 16], [1, TW]]),
                                                hsl_t[:, :]).then_inc(ds.h, 16)
                                    ds.n += 16
                            e.wait_ge(ds.h, ds.n)
                        @blk.tensor
                        def _(e):
                            e.wait_ge(ev.h, ev0)
                            for oh2 in range(2):
                                e.wait_ge(dg.h, dg0 + 128 * (oh2 + 1))
                                for tt in range(NTT):
                                    s_ = oh2 * NTT + tt
                                    if s_ > 0:
                                        e.wait_ge(ds.h, ds0 + 16 * s_)
                                    for og in range(16):
                                        u = s_ * 16 + og
                                        if u >= 2:
                                            e.wait_ge(sv.h, sv0 + u - 1)
                                        last = None
                                        for kc in range(FC):
                                            last = e.matmul(
                                                ph7[u % 2][:, :],
                                                w1T[:, kc * HF + og * 128:
                                                    kc * HF + (og + 1) * 128],
                                                iq7[kc][:, tt * TW:(tt + 1) * TW],
                                                start=(kc == 0), stop=(kc == FC - 1))
                                        last.then_inc(mm.h, 1); mm.n += 1
                        @blk.scalar
                        def _(e):
                            for oh2 in range(2):
                                for tt in range(NTT):
                                    s_ = oh2 * NTT + tt
                                    for og in range(16):
                                        u = s_ * 16 + og
                                        e.wait_ge(mm.h, mm0 + u + 1)
                                        if u >= 2:
                                            e.wait_ge(ev.h, ev0 + u - 1)
                                        gog = oh2 * 16 + og
                                        e.activation(hsl_t[:, og * TW:(og + 1) * TW],
                                                     ph7[u % 2][:, :], ACT.Relu,
                                                     scale=sc1[:, gog:gog + 1],
                                                     bias=bq1[:, gog:gog + 1]).then_inc(sv.h, 1)
                                        sv.n += 1
                        @blk.vector
                        def _(e):
                            for oh2 in range(2):
                                for tt in range(NTT):
                                    sl = slice(tt * TW, (tt + 1) * TW)
                                    for og in range(16):
                                        u = (oh2 * NTT + tt) * 16 + og
                                        e.wait_ge(sv.h, sv0 + u + 1)
                                        e.reduce_max(mx7[:, u:u + 1],
                                                     hsl_t[:, og * TW:(og + 1) * TW],
                                                     AXX).then_inc(ev.h, 1)
                                        ev.n += 1

                    with nc.Block() as blk:
                        @blk.vector
                        def _(e):
                            pass
                            e.drain()
                            e.nop().then_inc(ev.h, 1)
                            ev.n += 1
                        @blk.gpsimd
                        def _(e):
                            e.wait_ge(ev.h, ev.n)
                            e.tensor_reduce(cell(5), mx7[:, 0:128], AXALL,
                                            ALU.max).then_inc(gp.h, 1)
                            gp.n += 1
                    with nc.Block() as blk:
                        @blk.gpsimd
                        def _(e):
                            e.wait_ge(gp.h, gp.n)
                            e.dma_start(cc_b[2].ap(), cell(5)).then_inc(dg.h, 16)
                            dg.n += 16
                            e.wait_ge(dg.h, dg.n)
                            e.collective_compute(
                                "AllReduce", ALU.max,
                                replica_groups=[list(range(NCORES))],
                                ins=[cc_b[2].ap().opt()],
                                outs=[cc_b[3].ap().opt()],
                            ).then_inc(cc.h, 1)
                            cc.n += 1
                            e.wait_ge(cc.h, cc.n)
                            e.dma_start(cell(5), cc_b[3].ap()).then_inc(dg.h, 16)
                            dg.n += 16
                            e.wait_ge(dg.h, dg.n)
                    with nc.Block() as blk:
                        @blk.vector
                        def _(e):
                            e.tensor_scalar(cell(5), cell(5), 1e-8, None, ALU.max)
                            e.drain()
                            e.tensor_scalar(cell(6), cell(5), 1.0 / 255.0, None, ALU.mult)
                            e.reciprocal(cell(7), cell(5))
                            e.drain()
                            e.tensor_scalar(cell(7), cell(7), 255.0, None, ALU.mult)
                            e.drain()
                            e.nop().then_inc(ev.h, 1)
                            ev.n += 1
                        @blk.tensor
                        def _(e):
                            e.wait_ge(ev.h, ev.n)
                            e.matmul(ph7[0][:, 0:1], ones[0:1, :], cell(7),
                                     start=True, stop=True)
                            e.matmul(ph7[1][:, 0:1], ones[0:1, :], cell(6),
                                     start=True, stop=True).then_inc(mm.h, 2)
                            mm.n += 2
                    with nc.Block() as blk:
                        @blk.vector
                        def _(e):
                            e.tensor_copy(pcol[:, 1:2], ph7[0][:, 0:1])
                            e.tensor_copy(pcol[:, 3:4], ph7[1][:, 0:1])
                            e.drain()
                            e.tensor_scalar(sc2[:, :], sw2[:, :], pcol[:, 3:4], None, ALU.mult)
                            e.drain()
                            e.reciprocal(bq2[:, :], sc2[:, :])
                            e.drain()
                            e.tensor_tensor(bq2[:, :], b2v[:, :], bq2[:, :], ALU.mult)
                            e.drain()
                            e.tensor_scalar(bq2[:, :], bq2[:, :], 0.5, None, ALU.add)
                            e.drain()
                            e.tensor_copy(i7[:, 0:FC], bq2[:, :])
                            e.tensor_tensor(g7[:, 0:FC], i7[:, 0:FC], bq2[:, :], ALU.is_gt)
                            e.drain()
                            e.scalar_tensor_tensor(bq2[:, :], i7[:, 0:FC], 1.0,
                                                   g7[:, 0:FC], ALU.mult, ALU.subtract)
                            e.drain()
                            e.nop().then_inc(ev.h, 1)
                            ev.n += 1
                        @blk.sync
                        def _(e):
                            e.wait_ge(ev.h, ev.n)
                            e.dma_start(AP(rows_sp, 8 * T, [[1, 128], [128, FC]]),
                                        bq2[:, :], allow_slow_non_contiguous=True
                                        ).then_inc(ds.h, 16)
                            ds.n += 16
                            e.wait_ge(ds.h, ds.n)
                            e.dma_start(brow1[0:1, :],
                                        AP(rows_sp, 8 * T, [[1, 1], [1, D]])).then_inc(ds.h, 16)
                            ds.n += 16
                            e.wait_ge(ds.h, ds.n)
                        @blk.scalar
                        def _(e):
                            pass
                ps_iq.close()
                with ExitStack() as ps2:
                    w2T = ps2.enter_context(nc.sbuf_tensor("p7w2", [128, FFC * D], BF16))
                    pout7 = [ps2.enter_context(nc.psum_tensor(f"p7po{i}", [128, TW], F32))
                             for i in range(2)]
                    with nc.Block() as blk:
                        @blk.sync
                        def _(e):
                            for kc in range(FFC):
                                e.dma_start(w2T[:, kc * D:(kc + 1) * D],
                                            AP(nw2, kc * 128, [[FF, D], [1, 128]]),
                                            transpose=True).then_inc(ds.h, 16)
                                ds.n += 16
                            e.wait_ge(ds.h, ds.n)
                        @blk.vector
                        def _(e):
                            e.memset(ones_row[0:1, :], 1.0).then_inc(ev.h, 1)
                            ev.n += 1
                    with nc.Block() as blk:
                        mm0 = mm.n; sv0 = sv.n; ev0 = ev.n; ds0 = ds.n; dg0 = dg.n
                        h2 = ps2.enter_context(nc.sbuf_tensor("p7h2", [128, FFC * TW], BF16))
                        @blk.gpsimd
                        def _(e):
                            for tt in range(NTT):
                                if tt >= 1:
                                    e.wait_ge(ev.h, ev0 + tt * (3 * FFC + FC))
                                e.dma_start(h2[:, :],
                                            AP(h_sp, tt * TW,
                                               [[T, 128], [128 * T, FFC], [1, TW]])
                                            ).then_inc(dg.h, 16)
                                dg.n += 16
                        @blk.scalar
                        def _(e):
                            for tt in range(NTT):
                                e.wait_ge(dg.h, dg0 + 16 * (tt + 1))
                                for og in range(FFC):
                                    u = tt * FFC + og
                                    if u >= 1:
                                        e.wait_ge(ev.h, ev0 + tt * (3 * FFC + FC) + 3 * og)
                                    e.activation(u7[:, :], h2[:, og * TW:(og + 1) * TW],
                                                 ACT.Identity, scale=pcol[:, 1:2],
                                                 bias=halfc[:, 0:1]).then_inc(sv.h, 1)
                                    sv.n += 1
                        @blk.vector
                        def _(e):
                            for tt in range(NTT):
                                sl = slice(tt * TW, (tt + 1) * TW)
                                for og in range(FFC):
                                    e.wait_ge(sv.h, sv0 + tt * FFC + og + 1)
                                    e.tensor_copy(i7[:, :], u7[:, :])
                                    e.drain()
                                    e.tensor_tensor(g7[:, :], i7[:, :], u7[:, :], ALU.is_gt)
                                    e.drain()
                                    e.scalar_tensor_tensor(h2[:, og * TW:(og + 1) * TW],
                                                           i7[:, :], -1.0, g7[:, :],
                                                           ALU.mult, ALU.add).then_inc(ev.h, 3)
                                    ev.n += 3
                                for fcx in range(FC):
                                    e.wait_ge(mm.h, mm0 + tt * FC + fcx + 1)
                                    e.scalar_tensor_tensor(X[fcx][:, sl],
                                                           pout7[fcx % 2][:, :],
                                                           sc2[:, fcx:fcx + 1],
                                                           X[fcx][:, sl],
                                                           ALU.mult, ALU.add).then_inc(ev.h, 1)
                                    ev.n += 1
                        @blk.tensor
                        def _(e):
                            for tt in range(NTT):
                                for fcx in range(FC):
                                    u = tt * FC + fcx
                                    e.wait_ge(ev.h, ev0 + tt * (3 * FFC + FC) + 3 * FFC)
                                    if u >= 2:
                                        e.wait_ge(ev.h, ev0 + (u - 2) // FC * (3 * FFC + FC)
                                                  + 3 * FFC + (u - 2) % FC + 1)
                                    last = None
                                    for kc in range(FFC):
                                        last = e.matmul(pout7[fcx % 2][:, :],
                                                        w2T[:, kc * D + fcx * 128:
                                                            kc * D + (fcx + 1) * 128],
                                                        h2[:, kc * TW:(kc + 1) * TW],
                                                        start=(kc == 0), stop=False)
                                    last = e.matmul(pout7[fcx % 2][:, :],
                                                    brow1[0:1, fcx * 128:(fcx + 1) * 128],
                                                    ones_row[0:1, :],
                                                    start=False, stop=True)
                                    last.then_inc(mm.h, 1); mm.n += 1

        if debug and upto >= 6:
            d_osp = dbgt("osp", [FC, 128, T], BF16)
            with nc.Block() as blk:
                @blk.sync
                def _(e):
                    e.dma_start(AP(d_osp, 0, [[1, FC * 128 * T]]),
                                AP(o_sp, 0, [[1, FC * 128 * T]])).then_inc(ds.h, 16)
                    ds.n += 16
                    e.wait_ge(ds.h, ds.n)
        if debug:
            d_xc = dbgt("xc", [DIC, 128, T], BF16)
            with nc.Block() as blk:
                @blk.sync
                def _(e):
                    e.dma_start(AP(d_xc, 0, [[1, DIC * 128 * T]]),
                                AP(xc_sp, 0, [[1, DIC * 128 * T]])).then_inc(ds.h, 16)
                    ds.n += 16
                    e.wait_ge(ds.h, ds.n)

        with ExitStack() as ps:
            ostage = ps.enter_context(nc.sbuf_tensor("ostage", [128, 16 * D], F32))
            tp = [ps.enter_context(nc.psum_tensor(f"otp{i}", [128, 128], F32)) for i in range(4)]
            with nc.Block() as blk:
                @blk.tensor
                def _(e):
                    ev0 = ev.n
                    e.wait_ge(ev.h, ev0)
                    e.wait_ge(sv.h, sv.n)
                    i = 0
                    for tc in range(16):
                        for f in range(FC):
                            if i >= 4:
                                e.wait_ge(ev.h, ev0 + i - 3)
                            e.transpose(tp[i % 4][:, :], X[f][:, 128 * tc:128 * (tc + 1)],
                                        iden[:, :]).then_inc(mm.h, 1)
                            mm.n += 1
                            i += 1
                @blk.vector
                def _(e):
                    base = mm.n - 128
                    i = 0
                    for tc in range(16):
                        for f in range(FC):
                            e.wait_ge(mm.h, base + i + 1)
                            e.tensor_copy(ostage[:, tc * D + 128 * f: tc * D + 128 * (f + 1)],
                                          tp[i % 4][:, :]).then_inc(ev.h, 1)
                            ev.n += 1
                            i += 1
            with nc.Block() as blk:
                @blk.sync
                def _(e):
                    e.dma_start(AP(out, 0, [[D, 128], [128 * D, 16], [1, D]]),
                                ostage[:, :]).then_inc(ds.h, 16)
                    ds.n += 16
                    e.wait_ge(ds.h, ds.n)

    return nc, list(dbg.keys())


def _rope_tables():
    half = 32
    theta = 1.0 / 10000.0 ** (np.arange(half, dtype=np.float64) / half)
    pos = np.arange(T, dtype=np.float64)
    fr = pos[:, None] * theta[None, :]
    fr = np.concatenate([fr, fr], axis=1)
    cos = np.cos(fr).T.astype(np.float32)
    sin = np.sin(fr).T.astype(np.float32)
    sin_signed = sin.copy()
    sin_signed[:half, :] *= -1.0
    cos2 = np.concatenate([cos, cos], axis=0)
    sin2 = np.concatenate([sin_signed, sin_signed], axis=0)
    return np.ascontiguousarray(cos2), np.ascontiguousarray(sin2)


_NC_CACHE = {}

def kernel(**inputs):
    key = ("full", UPTO)
    if key not in _NC_CACHE:
        _NC_CACHE[key] = build_program(debug=False)
    nc, _ = _NC_CACHE[key]
    cos2, sin2 = _rope_tables()
    x = np.ascontiguousarray(np.asarray(inputs["L0_attn_x"], dtype=np.float32))
    base = {k: np.ascontiguousarray(np.asarray(v, dtype=np.float32))
            for k, v in inputs.items() if k != "L0_attn_x"}
    base["rope_cos"] = cos2
    base["rope_sin"] = sin2
    in_maps = []
    for c in range(NCORES):
        m = dict(base)
        m["x0"] = x[c]
        in_maps.append(m)
    res = run_bass_kernel_spmd(nc, in_maps, core_ids=list(range(NCORES)))
    return np.stack([np.asarray(res.results[c]["out"]) for c in range(NCORES)], axis=0)


# version 10
# speedup vs baseline: 1.0775x; 1.0775x over previous
import numpy as np
from contextlib import ExitStack

import concourse.bass as bass
import concourse.mybir as mybir
from concourse.bass_utils import run_bass_kernel_spmd

F32 = mybir.dt.float32
BF16 = mybir.dt.bfloat16
I32 = mybir.dt.int32
ALU = mybir.AluOpType
ACT = mybir.ActivationFunctionType
AXC = mybir.AxisListType.C
AXX = mybir.AxisListType.X
AXALL = mybir.AxisListType.XYZWC

T = 2048
D = 1024
FC = 8
DI = 2048
DIC = 16
FF = 4096
FFC = 32
NCORES = 8
NTT = 4
TW = 512

UPTO = 99


class Sem:
    __slots__ = ("h", "n")
    def __init__(self, stack, nc, name):
        self.h = stack.enter_context(nc.semaphore(name))
        self.n = 0
    def inc(self, ins, k=1):
        ins.then_inc(self.h, k)
        self.n += k
        return self.n


def AP(t, off, dims):
    return bass.AP(t, off, dims)


def build_program(debug=False, upto=None):
    if upto is None:
        upto = UPTO
    nc = bass.Bass("TRN2", target_bir_lowering=False)
    st = ExitStack()

    x_in = nc.dram_tensor("x0", [T, D], F32, kind="ExternalInput")
    out = nc.dram_tensor("out", [T, D], F32, kind="ExternalOutput")
    W = {}
    for name, shape in [
        ("L0_attn_ln_g", [D]), ("L0_attn_ln_b", [D]),
        ("L0_attn_W_in", [D, 2 * DI]),
        ("L0_attn_conv_w", [DI, 4]), ("L0_attn_conv_b", [DI]),
        ("L0_attn_W_x", [DI, 128]),
        ("L0_attn_W_dt", [DI, DI]), ("L0_attn_b_dt", [DI]),
        ("L0_attn_A_log", [64]), ("L0_attn_D", [DI]),
        ("L0_attn_W_out", [DI, D]),
        ("L0_ffn_w_gate", [FF, D]), ("L0_ffn_w_up", [FF, D]), ("L0_ffn_w_down", [D, FF]),
        ("L1_attn_Wqkv", [D, 3 * D]), ("L1_attn_Wout", [D, D]),
        ("L1_ffn_w1", [FF, D]), ("L1_ffn_b1", [FF]),
        ("L1_ffn_w2", [D, FF]), ("L1_ffn_b2", [D]),
        ("rope_cos", [128, T]), ("rope_sin", [128, T]),
    ]:
        W[name] = nc.dram_tensor(name, shape, F32, kind="ExternalInput")

    xc_sp = nc.dram_tensor("xc_sp", [DIC, 128, T], BF16)
    rows_sp = nc.dram_tensor("rows_sp", [16, T], F32)
    nwg = nc.dram_tensor("nwg", [FF, D], BF16)
    nwu = nc.dram_tensor("nwu", [FF, D], BF16)
    nwd = nc.dram_tensor("nwd", [D, FF], BF16)
    nw1 = nc.dram_tensor("nw1", [FF, D], BF16)
    nw2 = nc.dram_tensor("nw2", [D, FF], BF16)
    nwgT = nc.dram_tensor("nwgT", [D, FF], BF16)
    nwuT = nc.dram_tensor("nwuT", [D, FF], BF16)
    vt_sp = nc.dram_tensor("vt_sp", [FFC, 128, T], BF16)
    o_sp = nc.dram_tensor("o_sp", [FC, 128, T], BF16)
    h_sp = nc.dram_tensor("h_sp", [FFC, 128, T], BF16)
    cc_b = [nc.dram_tensor(f"cc_b{i}", [1, 1], F32) for i in range(4)]

    dbg = {}
    def dbgt(name, shape, dtype=F32):
        if debug:
            dbg[name] = nc.dram_tensor("dbg_" + name, shape, dtype, kind="ExternalOutput")
            return dbg[name]
        return None

    sbt = lambda name, shape, dt=F32: st.enter_context(nc.sbuf_tensor(name, shape, dt))

    with st:
        X = [sbt(f"Xs{f}", [128, T]) for f in range(FC)]
        ln_g = sbt("v_lng", [128, FC]); ln_b = sbt("v_lnb", [128, FC])
        conv_b = sbt("v_convb", [128, DIC]); Dv = sbt("v_D", [128, DIC])
        b_dt = sbt("v_bdt", [128, DIC])
        conv_w = sbt("v_convw", [128, 4 * DIC])
        aet = sbt("v_aexp", [64, 2])
        b1v = sbt("v_b1", [128, FFC]); b2v = sbt("v_b2", [128, FC])
        sw1 = sbt("v_sw1", [128, FFC]); sw2 = sbt("v_sw2", [128, FC])
        bq1 = sbt("v_bq1", [128, FFC]); bq2 = sbt("v_bq2", [128, FC])
        sc1 = sbt("v_sc1", [128, FFC]); sc2 = sbt("v_sc2", [128, FC])
        bq2row = sbt("v_bq2row", [1, D])
        ones = sbt("c_ones", [128, 128])
        onesb = sbt("c_onesb", [128, 128], BF16)
        iden = sbt("c_iden", [128, 128])
        scal = sbt("scal", [1, 64])
        pcol = sbt("pcol", [128, 16])
        halfc = sbt("c_half", [128, 1])
        def cell(i):
            return scal[0:1, i:i + 1]
        def pc(i):
            return pcol[:, i:i + 1]

        sem = {n: Sem(st, nc, "s_" + n) for n in
               ["ds", "dg", "dc", "mm", "ev", "sv", "gp", "cc", "dv"]}
        ds, dg, dc, mm, ev, sv, gp, cc, dv = (sem[n] for n in
               ["ds", "dg", "dc", "mm", "ev", "sv", "gp", "cc", "dv"])

        with ExitStack() as ps:
            xstage = ps.enter_context(nc.sbuf_tensor("xstage", [128, 16 * D], F32))
            tp = [ps.enter_context(nc.psum_tensor(f"tp{i}", [128, 128], F32)) for i in range(4)]
            with nc.Block() as blk:
                @blk.sync
                def _(e):
                    e.dma_start(xstage[:, :],
                                AP(x_in, 0, [[D, 128], [128 * D, 16], [1, D]])).then_inc(ds.h, 16)
                    ds.n += 16
                    for t, src in [
                        (ln_g, AP(W["L0_attn_ln_g"], 0, [[1, 128], [128, FC]])),
                        (ln_b, AP(W["L0_attn_ln_b"], 0, [[1, 128], [128, FC]])),
                        (conv_b, AP(W["L0_attn_conv_b"], 0, [[1, 128], [128, DIC]])),
                        (Dv, AP(W["L0_attn_D"], 0, [[1, 128], [128, DIC]])),
                        (b_dt, AP(W["L0_attn_b_dt"], 0, [[1, 128], [128, DIC]])),
                        (conv_w, AP(W["L0_attn_conv_w"], 0, [[4, 128], [1, 4], [512, DIC]])),
                        (b1v, AP(W["L1_ffn_b1"], 0, [[1, 128], [128, FFC]])),
                        (b2v, AP(W["L1_ffn_b2"], 0, [[1, 128], [128, FC]])),
                        (aet[:, 0:1], AP(W["L0_attn_A_log"], 0, [[1, 64], [1, 1]])),
                    ]:
                        e.dma_start(t[:, :] if not isinstance(t, bass.AP) else t, src,
                                    allow_slow_non_contiguous=True).then_inc(ds.h, 16)
                        ds.n += 16
                    e.wait_ge(ds.h, ds.n)

                @blk.vector
                def _(e):
                    e.memset(ones[:, :], 1.0)
                    e.memset(onesb[:, :], 1.0)
                    e.memset(pcol[:, 0:1], 0.125)
                    e.memset(halfc[:, :], 0.5).then_inc(ev.h, 1)
                    ev.n += 1

            with nc.Block() as blk:
                @blk.gpsimd
                def _(e):
                    e.wait_ge(ev.h, ev.n)
                    e.affine_select(iden[:, :], ones[:, :], [[1, 128]], ALU.is_equal, 0.0,
                                    base=0, channel_multiplier=-1)
                    e.engine_nop().then_inc(gp.h, 1); gp.n += 1
                @blk.scalar
                def _(e):
                    e.activation(aet[:, 1:2], aet[:, 0:1], ACT.Exp)
            with nc.Block() as blk:
                @blk.vector
                def _(e):
                    e.tensor_scalar(aet[:, 1:2], aet[:, 1:2], -1.0, None, ALU.mult)
                    e.tensor_scalar(b_dt[:, :], b_dt[:, :], -1.0, None, ALU.mult)

            with nc.Block() as blk:
                @blk.tensor
                def _(e):
                    e.wait_ge(gp.h, gp.n)
                    ev0 = ev.n
                    i = 0
                    for tc in range(16):
                        for f in range(FC):
                            if i >= 4:
                                e.wait_ge(ev.h, ev0 + i - 3)
                            e.transpose(tp[i % 4][:, :],
                                        xstage[:, tc * D + 128 * f: tc * D + 128 * (f + 1)],
                                        iden[:, :]).then_inc(mm.h, 1)
                            mm.n += 1
                            i += 1
                @blk.vector
                def _(e):
                    base = mm.n - 128
                    i = 0
                    for tc in range(16):
                        for f in range(FC):
                            e.wait_ge(mm.h, base + i + 1)
                            e.tensor_copy(X[f][:, 128 * tc:128 * (tc + 1)],
                                          tp[i % 4][:, :]).then_inc(ev.h, 1)
                            ev.n += 1
                            i += 1

        if upto >= 1:
          with ExitStack() as ps:
            sb_ = lambda n, s, d=F32: ps.enter_context(nc.sbuf_tensor(n, s, d))
            pt_ = lambda n, s: ps.enter_context(nc.psum_tensor(n, s, F32))
            rows = sb_("p1rows", [1, 7 * T])
            RK = {"mu": 0, "var": 1, "tmp": 2, "a1": 3, "b1": 4, "A": 5, "B": 6}
            row = lambda k: rows[0:1, RK[k] * T:(RK[k] + 1) * T]
            xn = [sb_(f"p1xn{f}", [128, T], BF16) for f in range(FC)]
            w1h = sb_("p1w", [128, 8 * 1024], BF16)
            xpp = sb_("p1xp", [128, T + 3])
            xco = [sb_(f"p1xco{i}", [128, T], BF16) for i in range(2)]
            cacc = sb_("p1cacc", [128, T])
            sq = sb_("p1sq", [128, TW])
            scr = [sb_(f"p1scr{i}", [128, TW]) for i in range(2)]
            pst1 = pt_("p1s1", [1, TW]); pst2 = pt_("p1s2", [1, TW])
            pbA = pt_("p1bA", [128, TW]); pbB = pt_("p1bB", [128, TW])
            pxp = [pt_(f"p1px{i}", [128, TW]) for i in range(2)]

            with nc.Block() as blk:
                sv0 = sv.n; mm0 = mm.n; ev0 = ev.n
                @blk.scalar
                def _(e):
                    i = 0
                    for tt in range(NTT):
                        for f in range(FC):
                            if i >= 1:
                                e.wait_ge(mm.h, mm0 + 2 * i)
                            e.activation(sq[:, :], X[f][:, tt * TW:(tt + 1) * TW],
                                         ACT.Square).then_inc(sv.h, 1)
                            sv.n += 1
                            i += 1
                @blk.tensor
                def _(e):
                    e.wait_ge(ev.h, ev0)
                    i = 0
                    for tt in range(NTT):
                        for f in range(FC):
                            if f == 0 and tt > 0:
                                e.wait_ge(ev.h, ev0 + 2 * tt)
                            e.matmul(pst1[:, :], ones[:, 0:1], X[f][:, tt * TW:(tt + 1) * TW],
                                     start=(f == 0), stop=(f == FC - 1)).then_inc(mm.h, 1)
                            mm.n += 1
                            e.wait_ge(sv.h, sv0 + i + 1)
                            e.matmul(pst2[:, :], ones[:, 0:1], sq[:, :],
                                     start=(f == 0), stop=(f == FC - 1)).then_inc(mm.h, 1)
                            mm.n += 1
                            i += 1
                @blk.vector
                def _(e):
                    for tt in range(NTT):
                        e.wait_ge(mm.h, mm0 + (tt + 1) * 16)
                        sl = slice(tt * TW, (tt + 1) * TW)
                        e.tensor_scalar(row("mu")[:, sl], pst1[:, :], 1.0 / D, None, ALU.mult)
                        e.tensor_scalar(row("tmp")[:, sl], pst2[:, :], 1.0 / D, None,
                                        ALU.mult).then_inc(ev.h, 2)
                        ev.n += 2

            def rsqrt_polish(e_v, dst, srcE, scr_r):
                e_v.tensor_tensor(scr_r, dst, dst, ALU.mult)
                e_v.tensor_tensor(scr_r, scr_r, srcE, ALU.mult)
                e_v.tensor_scalar(scr_r, scr_r, -0.5, 1.5, ALU.mult, ALU.add)
                e_v.tensor_tensor(dst, dst, scr_r, ALU.mult)
            with nc.Block() as blk:
                @blk.vector
                def _(e):
                    e.tensor_tensor(row("var"), row("mu"), row("mu"), ALU.mult)
                    e.tensor_tensor(row("var"), row("tmp"), row("var"), ALU.subtract)
                    e.tensor_scalar(row("tmp"), row("var"), 1e-5, None,
                                    ALU.add).then_inc(ev.h, 1)
                    ev.n += 1
                @blk.scalar
                def _(e):
                    e.wait_ge(ev.h, ev.n)
                    e.activation(row("A"), row("tmp"), ACT.Ln)
                    e.activation(row("a1"), row("A"), ACT.Exp, scale=-0.5)
            with nc.Block() as blk:
                @blk.vector
                def _(e):
                    e.tensor_tensor(row("b1"), row("mu"), row("a1"), ALU.mult)
                    e.tensor_scalar(row("b1"), row("b1"), -1.0, None, ALU.mult)
                    e.tensor_tensor(row("tmp"), row("a1"), row("a1"), ALU.mult)
                    e.tensor_tensor(row("tmp"), row("tmp"), row("var"), ALU.mult)
                    e.tensor_scalar(row("tmp"), row("tmp"), 1e-5, None,
                                    ALU.add).then_inc(ev.h, 1)
                    ev.n += 1
                @blk.scalar
                def _(e):
                    e.wait_ge(ev.h, ev.n)
                    e.activation(row("B"), row("tmp"), ACT.Ln)
                    e.activation(row("mu"), row("B"), ACT.Exp, scale=-0.5)
            with nc.Block() as blk:
                @blk.vector
                def _(e):
                    e.tensor_tensor(row("A"), row("a1"), row("mu"), ALU.mult)
                    e.tensor_tensor(row("B"), row("b1"), row("mu"),
                                    ALU.mult).then_inc(ev.h, 1)
                    ev.n += 1

            with nc.Block() as blk:
                mm0 = mm.n; ev0 = ev.n; sv0 = sv.n
                @blk.tensor
                def _(e):
                    e.wait_ge(ev.h, ev0)
                    for tt in range(NTT):
                        sl = slice(tt * TW, (tt + 1) * TW)
                        if tt > 0:
                            e.wait_ge(ev.h, ev0 + tt * 8)
                        e.matmul(pbA[:, :], ones[0:1, :], row("A")[:, sl], start=True, stop=True)
                        mm.n += 1
                        e.matmul(pbB[:, :], ones[0:1, :], row("B")[:, sl],
                                 start=True, stop=True).then_inc(mm.h, 2)
                        mm.n += 1
                @blk.vector
                def _(e):
                    i = 0
                    for tt in range(NTT):
                        e.wait_ge(mm.h, mm0 + 2 * (tt + 1))
                        sl = slice(tt * TW, (tt + 1) * TW)
                        for f in range(FC):
                            b = scr[i % 2]
                            e.tensor_tensor(b[:, :], X[f][:, sl], pbA[:, :], ALU.mult)
                            e.tensor_tensor(b[:, :], b[:, :], pbB[:, :], ALU.add)
                            e.tensor_scalar(xn[f][:, sl], b[:, :], ln_g[:, f:f + 1],
                                            ln_b[:, f:f + 1], ALU.mult,
                                            ALU.add).then_inc(ev.h, 1)
                            ev.n += 1
                            i += 1

            if debug:
                d_xn = dbgt("xn", [FC, 128, T], BF16)
                with nc.Block() as blk:
                    @blk.sync
                    def _(e):
                        for f in range(FC):
                            e.dma_start(AP(d_xn, f * 128 * T, [[T, 128], [1, T]]),
                                        xn[f][:, :]).then_inc(ds.h, 16)
                            ds.n += 16
                        e.wait_ge(ds.h, ds.n)
            with nc.Block() as blk:
                @blk.vector
                def _(e):
                    e.memset(xpp[:, 0:3], 0.0)

            with nc.Block() as blk:
                mm0 = mm.n; ev0 = ev.n; sv0 = sv.n; gp0 = gp.n; dg0 = dg.n; ds0 = ds.n
                @blk.gpsimd
                def _(e):
                    for oh in range(2):
                        if oh == 1:
                            e.wait_ge(mm.h, mm0 + 32)
                        e.dma_start(w1h[:, :],
                                    AP(W["L0_attn_W_in"], oh * 1024,
                                       [[2 * DI, 128], [128 * 2 * DI, 8], [1, 1024]])
                                    ).then_inc(dg.h, 16)
                        dg.n += 16
                @blk.tensor
                def _(e):
                    gi = 0
                    for oh in range(2):
                        e.wait_ge(dg.h, dg0 + 16 * (oh + 1))
                        for oc in range(8):
                            for tt in range(NTT):
                                if gi >= 2:
                                    e.wait_ge(ev.h, ev0 + gi - 1)
                                last = None
                                for kc in range(FC):
                                    last = e.matmul(
                                        pxp[gi % 2][:, :],
                                        w1h[:, kc * 1024 + 128 * oc: kc * 1024 + 128 * (oc + 1)],
                                        xn[kc][:, tt * TW:(tt + 1) * TW],
                                        start=(kc == 0), stop=(kc == FC - 1))
                                last.then_inc(mm.h, 1)
                                mm.n += 1
                                gi += 1
                @blk.scalar
                def _(e):
                    gi = 0
                    for og in range(DIC):
                        if og >= 1:
                            e.wait_ge(sv.h, sv0 + og)
                        for tt in range(NTT):
                            e.wait_ge(mm.h, mm0 + gi + 1)
                            e.activation(xpp[:, 3 + tt * TW: 3 + (tt + 1) * TW],
                                         pxp[gi % 2][:, :], ACT.Copy).then_inc(ev.h, 1)
                            ev.n += 1
                            gi += 1
                        e.wait_ge(sv.h, sv0 + og + 1)
                        if og >= 2:
                            e.wait_ge(ds.h, ds0 + 16 * (og - 1))
                        e.activation(xco[og % 2][:, :], cacc[:, :], ACT.Silu,
                                     bias=conv_b[:, og:og + 1]).then_inc(gp.h, 1)
                        gp.n += 1
                @blk.vector
                def _(e):
                    for og in range(DIC):
                        e.wait_ge(ev.h, ev0 + 4 * (og + 1))
                        if og >= 1:
                            e.wait_ge(gp.h, gp0 + og)
                        e.tensor_scalar(cacc[:, :], xpp[:, 3:3 + T],
                                        conv_w[:, 48 + og:49 + og], None, ALU.mult)
                        for k in (2, 1, 0):
                            ins = e.scalar_tensor_tensor(cacc[:, :], xpp[:, k:k + T],
                                                         conv_w[:, 16 * k + og:16 * k + og + 1],
                                                         cacc[:, :], ALU.mult, ALU.add)
                        ins.then_inc(sv.h, 1)
                        sv.n += 1
                @blk.sync
                def _(e):
                    for og in range(DIC):
                        e.wait_ge(gp.h, gp0 + og + 1)
                        e.dma_start(AP(xc_sp, og * 128 * T, [[T, 128], [1, T]]),
                                    xco[og % 2][:, :]).then_inc(ds.h, 16)
                        ds.n += 16
                    e.wait_ge(ds.h, ds.n)

            with nc.Block() as blk:
                @blk.sync
                def _(e):
                    e.wait_ge(ev.h, ev.n)
                    for i, k in enumerate(["a1", "b1", "A", "B"]):
                        e.dma_start(AP(rows_sp, i * T, [[1, 1], [1, T]]),
                                    row(k)).then_inc(ds.h, 16)
                        ds.n += 16
                    e.wait_ge(ds.h, ds.n)

        if upto >= 2:
          with ExitStack() as ps:
            sb_ = lambda n, s, d=F32: ps.enter_context(nc.sbuf_tensor(n, s, d))
            pt_ = lambda n, s: ps.enter_context(nc.psum_tensor(n, s, F32))
            rows2 = sb_("p2rows", [1, 2 * T])
            r_delta = rows2[0:1, 0:T]; r_ys = rows2[0:1, T:2 * T]
            Bst = sb_("p2B", [64, T]); Cst = sb_("p2C", [64, T])
            with ExitStack() as ps2:
                wx = ps2.enter_context(nc.sbuf_tensor("p2wx", [128, DIC * 128], BF16))
                wdt = ps2.enter_context(nc.sbuf_tensor("p2wdt", [128, DIC * DI], BF16))
                xcs = ps2.enter_context(nc.sbuf_tensor("p2xcs", [128, DIC * TW], BF16))
                spl = [ps2.enter_context(nc.sbuf_tensor(f"p2spl{i}", [128, TW], F32))
                       for i in range(2)]
                pB = ps2.enter_context(nc.psum_tensor("p2pB", [64, TW], F32))
                pC = ps2.enter_context(nc.psum_tensor("p2pC", [64, TW], F32))
                pdl = ps2.enter_context(nc.psum_tensor("p2pd", [1, TW], F32))
                pdt = [ps2.enter_context(nc.psum_tensor(f"p2dt{i}", [128, TW], F32))
                       for i in range(2)]
                with nc.Block() as blk:
                    dg0 = dg.n
                    @blk.gpsimd
                    def _(e):
                        e.dma_start(wx[:, :],
                                    AP(W["L0_attn_W_x"], 0,
                                       [[128, 128], [128 * 128, DIC], [1, 128]])
                                    ).then_inc(dg.h, 16)
                        dg.n += 16
                        e.dma_start(wdt[:, :],
                                    AP(W["L0_attn_W_dt"], 0,
                                       [[DI, 128], [128 * DI, DIC], [1, DI]])
                                    ).then_inc(dg.h, 16)
                        dg.n += 16
                        e.wait_ge(dg.h, dg.n)
                with nc.Block() as blk:
                    mm0 = mm.n; ev0 = ev.n; sv0 = sv.n; ds0 = ds.n
                    @blk.sync
                    def _(e):
                        for tt in range(NTT):
                            if tt > 0:
                                e.wait_ge(mm.h, mm0 + tt * 34)
                            e.dma_start(xcs[:, :],
                                        AP(xc_sp, tt * TW,
                                           [[T, 128], [128 * T, DIC], [1, TW]])
                                        ).then_inc(ds.h, 16)
                            ds.n += 16
                    @blk.tensor
                    def _(e):
                        for tt in range(NTT):
                            e.wait_ge(ds.h, ds0 + 16 * (tt + 1))
                            if tt > 0:
                                e.wait_ge(ev.h, ev0 + tt * 19)
                            last = None
                            for kc in range(DIC):
                                last = e.matmul(pB[:, :], wx[:, kc * 128: kc * 128 + 64],
                                                xcs[:, kc * TW:(kc + 1) * TW],
                                                start=(kc == 0), stop=(kc == DIC - 1))
                            last.then_inc(mm.h, 1); mm.n += 1
                            last = None
                            for kc in range(DIC):
                                last = e.matmul(pC[:, :], wx[:, kc * 128 + 64: kc * 128 + 128],
                                                xcs[:, kc * TW:(kc + 1) * TW],
                                                start=(kc == 0), stop=(kc == DIC - 1))
                            last.then_inc(mm.h, 1); mm.n += 1
                            for og in range(DIC):
                                if og >= 2:
                                    e.wait_ge(ev.h, ev0 + tt * 19 + og + 1)
                                last = None
                                for kc in range(DIC):
                                    last = e.matmul(pdt[og % 2][:, :],
                                                    wdt[:, kc * DI + 128 * og: kc * DI + 128 * (og + 1)],
                                                    xcs[:, kc * TW:(kc + 1) * TW],
                                                    start=(kc == 0), stop=(kc == DIC - 1))
                                last.then_inc(mm.h, 1); mm.n += 1
                                e.wait_ge(ev.h, ev0 + tt * 19 + 2 + og + 1)
                                e.matmul(pdl[:, :], ones[:, 0:1], spl[og % 2][:, :],
                                         start=(og == 0), stop=(og == DIC - 1)).then_inc(mm.h, 1)
                                mm.n += 1
                    @blk.scalar
                    def _(e):
                        for tt in range(NTT):
                            e.wait_ge(mm.h, mm0 + tt * 34 + 1)
                            e.activation(Bst[:, tt * TW:(tt + 1) * TW], pB[:, :],
                                         ACT.Copy).then_inc(ev.h, 1)
                            ev.n += 1
                            e.wait_ge(mm.h, mm0 + tt * 34 + 2)
                            e.activation(Cst[:, tt * TW:(tt + 1) * TW], pC[:, :],
                                         ACT.Copy).then_inc(ev.h, 1)
                            ev.n += 1
                            for og in range(DIC):
                                e.wait_ge(mm.h, mm0 + tt * 34 + 2 * og + 3)
                                e.activation(spl[og % 2][:, :], pdt[og % 2][:, :], ACT.Sigmoid,
                                             scale=-1.0, bias=b_dt[:, og:og + 1])
                                e.activation(spl[og % 2][:, :], spl[og % 2][:, :],
                                             ACT.Ln).then_inc(ev.h, 1)
                                ev.n += 1
                    @blk.vector
                    def _(e):
                        for tt in range(NTT):
                            e.wait_ge(mm.h, mm0 + (tt + 1) * 34)
                            e.tensor_scalar(r_delta[:, tt * TW:(tt + 1) * TW], pdl[:, :],
                                            -1.0 / DI, None, ALU.mult).then_inc(ev.h, 1)
                            ev.n += 1
            Ad = sb_("p2Ad", [64, T]); hsc = sb_("p2h", [64, T])
            with ExitStack() as ps3:
                pAd = ps3.enter_context(nc.psum_tensor("p2pad", [64, TW], F32))
                pys = ps3.enter_context(nc.psum_tensor("p2pys", [1, TW], F32))
                with nc.Block() as blk:
                    mm0 = mm.n; ev0 = ev.n; sv0 = sv.n
                    @blk.tensor
                    def _(e):
                        e.wait_ge(ev.h, ev0)
                        for tt in range(NTT):
                            if tt > 0:
                                e.wait_ge(sv.h, sv0 + tt)
                            e.matmul(pAd[:, :], ones[0:1, 0:64],
                                     r_delta[:, tt * TW:(tt + 1) * TW],
                                     start=True, stop=True).then_inc(mm.h, 1)
                            mm.n += 1
                    @blk.scalar
                    def _(e):
                        for tt in range(NTT):
                            e.wait_ge(mm.h, mm0 + tt + 1)
                            e.activation(Ad[:, tt * TW:(tt + 1) * TW], pAd[:, :], ACT.Exp,
                                         scale=aet[:, 1:2]).then_inc(sv.h, 1)
                            sv.n += 1
                with nc.Block() as blk:
                    @blk.vector
                    def _(e):
                        e.tensor_tensor_scan(hsc[:, :], Ad[:, :], Bst[:, :], 0.0,
                                             ALU.mult, ALU.add)
                        e.tensor_tensor(Ad[:, :], hsc[:, :], Cst[:, :],
                                        ALU.mult).then_inc(ev.h, 1)
                        ev.n += 1
                with nc.Block() as blk:
                    mm0 = mm.n; ev0 = ev.n
                    @blk.tensor
                    def _(e):
                        e.wait_ge(ev.h, ev0)
                        for tt in range(NTT):
                            if tt > 0:
                                e.wait_ge(ev.h, ev0 + tt)
                            e.matmul(pys[:, :], ones[0:64, 0:1],
                                     Ad[:, tt * TW:(tt + 1) * TW],
                                     start=True, stop=True).then_inc(mm.h, 1)
                            mm.n += 1
                    @blk.vector
                    def _(e):
                        for tt in range(NTT):
                            e.wait_ge(mm.h, mm0 + tt + 1)
                            e.tensor_copy(r_ys[:, tt * TW:(tt + 1) * TW],
                                          pys[:, :]).then_inc(ev.h, 1)
                            ev.n += 1
                with nc.Block() as blk:
                    @blk.sync
                    def _(e):
                        e.wait_ge(ev.h, ev.n)
                        e.dma_start(AP(rows_sp, 4 * T, [[1, 1], [1, T]]),
                                    r_ys).then_inc(ds.h, 16)
                        ds.n += 16
                        e.wait_ge(ds.h, ds.n)
                if debug:
                    d_sc = dbgt("scan", [4, 64, T], F32)
                    d_rw = dbgt("rows", [6, T], F32)
                    with nc.Block() as blk:
                        @blk.sync
                        def _(e):
                            for i, tile in enumerate([Bst, Cst, Ad, hsc]):
                                e.dma_start(AP(d_sc, i * 64 * T, [[T, 64], [1, T]]),
                                            tile[:, :]).then_inc(ds.h, 16)
                                ds.n += 16
                            for i in range(5):
                                e.dma_start(AP(d_rw, i * T, [[1, 1], [1, T]]),
                                            AP(rows_sp, i * T, [[1, 1], [1, T]])).then_inc(ds.h, 16)
                                ds.n += 16
                            e.dma_start(AP(d_rw, 5 * T, [[1, 1], [1, T]]),
                                        r_delta).then_inc(ds.h, 16)
                            ds.n += 16
                            e.wait_ge(ds.h, ds.n)

        if upto >= 3:
          with ExitStack() as ps:
            sb_ = lambda n, s, d=F32: ps.enter_context(nc.sbuf_tensor(n, s, d))
            HT = 256
            rows3 = sb_("p3rows", [1, 5 * T])
            RK3 = {"A": 0, "B": 1, "ys": 2, "a1p": 3, "b1": 4}
            row3 = lambda k: rows3[0:1, RK3[k] * T:(RK3[k] + 1) * T]
            win2 = sb_("p3w2", [128, 8 * DI], BF16)
            wob = [sb_(f"p3wo{i}", [128, DIC * 128], BF16) for i in range(2)]
            xcs3 = [sb_(f"p3xc{i}", [128, DIC * HT], BF16) for i in range(2)]
            xn3 = sb_("p3xn", [128, FC * TW], BF16)
            ut = sb_("p3u", [128, DIC * TW], BF16)
            gt = [sb_(f"p3gt{i}", [128, TW], F32) for i in range(2)]
            scr3 = [sb_(f"p3sc{i}", [128, TW], F32) for i in range(2)]
            pbA = ps.enter_context(nc.psum_tensor("p3bA", [128, TW], F32))
            pbB = ps.enter_context(nc.psum_tensor("p3bB", [128, TW], F32))
            pysb = ps.enter_context(nc.psum_tensor("p3ys", [128, TW], F32))
            pa1p = ps.enter_context(nc.psum_tensor("p3a1", [128, TW], F32))
            pres = [ps.enter_context(nc.psum_tensor(f"p3re{i}", [128, TW], F32)) for i in range(2)]
            pout = [ps.enter_context(nc.psum_tensor(f"p3ou{i}", [128, TW], F32)) for i in range(2)]

            with nc.Block() as blk:
                dg0 = dg.n; ds0 = ds.n
                @blk.gpsimd
                def _(e):
                    e.dma_start(win2[:, :],
                                AP(W["L0_attn_W_in"], DI,
                                   [[2 * DI, 128], [128 * 2 * DI, 8], [1, DI]])
                                ).then_inc(dg.h, 16)
                    dg.n += 16
                    e.wait_ge(dg.h, dg.n)
                @blk.sync
                def _(e):
                    for i, k in enumerate(["a1", "b1", "A", "B", "ys"]):
                        dstk = {"a1": "a1p", "b1": "b1", "A": "A", "B": "B", "ys": "ys"}[k]
                        e.dma_start(row3(dstk), AP(rows_sp, i * T, [[1, 1], [1, T]])
                                    ).then_inc(ds.h, 16)
                        ds.n += 16
                    e.wait_ge(ds.h, ds.n)
            with nc.Block() as blk:
                @blk.vector
                def _(e):
                    e.tensor_scalar(row3("a1p"), row3("a1p"), 1.0, None,
                                    ALU.add).then_inc(ev.h, 1)
                    ev.n += 1

            with nc.Block() as blk:
                mm0 = mm.n; ev0 = ev.n; sv0 = sv.n; dg0 = dg.n; ds0 = ds.n; gp0 = gp.n
                MMT = 28; SVT = 16; EVT = 32
                @blk.gpsimd
                def _(e):
                    for tt in range(NTT):
                        for fcx in range(FC):
                            i = tt * FC + fcx
                            if i >= 2:
                                e.wait_ge(mm.h, mm0 + ((i - 2) // FC) * MMT + 20 + (i - 2) % FC + 1)
                            e.dma_start(wob[i % 2][:, :],
                                        AP(W["L0_attn_W_out"], fcx * 128,
                                           [[D, 128], [128 * D, DIC], [1, 128]])
                                        ).then_inc(dg.h, 16)
                            dg.n += 16
                @blk.sync
                def _(e):
                    for tt in range(NTT):
                        for hh in range(2):
                            i = tt * 2 + hh
                            if i >= 2:
                                e.wait_ge(ev.h, ev0 + ((i - 2) // 2) * EVT + 24)
                            e.dma_start(xcs3[i % 2][:, :],
                                        AP(xc_sp, tt * TW + hh * HT,
                                           [[T, 128], [128 * T, DIC], [1, HT]])
                                        ).then_inc(ds.h, 16)
                            ds.n += 16
                @blk.tensor
                def _(e):
                    e.wait_ge(ev.h, ev0)
                    for tt in range(NTT):
                        sl = slice(tt * TW, (tt + 1) * TW)
                        base_mm = mm0 + tt * MMT
                        base_ev = ev0 + tt * EVT
                        if tt > 0:
                            e.wait_ge(ev.h, ev0 + tt * EVT)
                        e.matmul(pbA[:, :], ones[0:1, :], row3("A")[:, sl], start=True, stop=True)
                        e.matmul(pbB[:, :], ones[0:1, :], row3("B")[:, sl], start=True, stop=True)
                        e.matmul(pysb[:, :], ones[0:1, :], row3("ys")[:, sl], start=True, stop=True)
                        e.matmul(pa1p[:, :], ones[0:1, :], row3("a1p")[:, sl],
                                 start=True, stop=True).then_inc(mm.h, 4)
                        mm.n += 4
                        e.wait_ge(ev.h, ev0 + tt * EVT + 8)
                        for og in range(DIC):
                            if og >= 2:
                                e.wait_ge(sv.h, sv0 + tt * SVT + og - 1)
                            last = None
                            for kc in range(FC):
                                last = e.matmul(pres[og % 2][:, :],
                                                win2[:, kc * DI + 128 * og: kc * DI + 128 * (og + 1)],
                                                xn3[:, kc * TW:(kc + 1) * TW],
                                                start=(kc == 0), stop=(kc == FC - 1))
                            last.then_inc(mm.h, 1); mm.n += 1
                        for fcx in range(FC):
                            i = tt * FC + fcx
                            e.wait_ge(dg.h, dg0 + 16 * (i + 1))
                            e.wait_ge(ev.h, ev0 + tt * EVT + 24)
                            if fcx >= 2:
                                e.wait_ge(ev.h, ev0 + tt * EVT + 24 + fcx - 1)
                            last = None
                            for og in range(DIC):
                                last = e.matmul(pout[fcx % 2][:, :],
                                                wob[i % 2][:, og * 128:(og + 1) * 128],
                                                ut[:, og * TW:(og + 1) * TW],
                                                start=(og == 0), stop=False)
                            e.matmul(pout[fcx % 2][:, :], ones[0:1, 0:128], row3("b1")[:, sl],
                                     start=False, stop=True).then_inc(mm.h, 1)
                            mm.n += 1
                @blk.vector
                def _(e):
                    for tt in range(NTT):
                        sl = slice(tt * TW, (tt + 1) * TW)
                        base_mm = mm0 + tt * MMT
                        e.wait_ge(mm.h, base_mm + 4)
                        for f in range(FC):
                            b = scr3[f % 2]
                            e.tensor_tensor(b[:, :], X[f][:, sl], pbA[:, :], ALU.mult)
                            e.tensor_tensor(b[:, :], b[:, :], pbB[:, :], ALU.add)
                            e.tensor_scalar(xn3[:, f * TW:(f + 1) * TW], b[:, :],
                                            ln_g[:, f:f + 1], ln_b[:, f:f + 1],
                                            ALU.mult, ALU.add).then_inc(ev.h, 1)
                            ev.n += 1
                        for og in range(DIC):
                            e.wait_ge(sv.h, sv0 + tt * SVT + og + 1)
                            b = scr3[og % 2]
                            for hh in range(2):
                                hsl = slice(hh * HT, (hh + 1) * HT)
                                e.wait_ge(ds.h, ds0 + 16 * (tt * 2 + hh + 1))
                                e.scalar_tensor_tensor(b[:, hsl],
                                                       xcs3[(tt * 2 + hh) % 2][:, og * HT:(og + 1) * HT],
                                                       Dv[:, og:og + 1], pysb[:, hsl],
                                                       ALU.mult, ALU.add)
                            e.tensor_tensor(ut[:, og * TW:(og + 1) * TW], b[:, :],
                                            gt[og % 2][:, :], ALU.mult).then_inc(ev.h, 1)
                            ev.n += 1
                        for fcx in range(FC):
                            e.wait_ge(mm.h, mm0 + tt * MMT + 20 + fcx + 1)
                            b = scr3[fcx % 2]
                            e.tensor_tensor(b[:, :], X[fcx][:, sl], pa1p[:, :], ALU.mult)
                            e.tensor_tensor(X[fcx][:, sl], b[:, :], pout[fcx % 2][:, :],
                                            ALU.add).then_inc(ev.h, 1)
                            ev.n += 1
                @blk.scalar
                def _(e):
                    for tt in range(NTT):
                        sl = slice(tt * TW, (tt + 1) * TW)
                        for og in range(DIC):
                            e.wait_ge(mm.h, mm0 + tt * MMT + 4 + og + 1)
                            if og >= 2:
                                e.wait_ge(ev.h, ev0 + tt * EVT + 8 + og - 1)
                            e.activation(gt[og % 2][:, :], pres[og % 2][:, :],
                                         ACT.Silu).then_inc(sv.h, 1)
                            sv.n += 1

        if upto >= 4:
          for wname, wsrc, wdst, nsl, slw, per_row, scell, swtile in [
              ("g", "L0_ffn_w_gate", nwg, 32, 1024, False, 0, None),
              ("u", "L0_ffn_w_up", nwu, 32, 1024, False, 1, None),
              ("d", "L0_ffn_w_down", nwd, 8, 4096, False, 2, None),
              ("w1", "L1_ffn_w1", nw1, 32, 1024, True, None, "sw1"),
              ("w2", "L1_ffn_w2", nw2, 8, 4096, True, None, "sw2"),
          ]:
            GR = (64 * 1024) // (slw * 4)
            NGR = nsl // GR
            NPC = slw // 1024
            with ExitStack() as ps:
                sb_ = lambda n, s, d=F32: ps.enter_context(nc.sbuf_tensor(n, s, d))
                wnat = sb_(f"p4w{wname}", [128, GR * slw])
                red = sb_(f"p4r{wname}", [128, 4])
                us = [sb_(f"p4u{wname}{i}", [128, 1024]) for i in range(2)]
                ui = [ps.enter_context(nc.sbuf_tensor(f"p4i{wname}{i}", [128, 1024], I32))
                      for i in range(2)]
                ug = [sb_(f"p4g{wname}{i}", [128, 1024]) for i in range(2)]
                nb = [ps.enter_context(nc.sbuf_tensor(f"p4n{wname}{i}", [128, slw], BF16))
                      for i in range(2)]
                rsw = sb_(f"p4s{wname}", [128, nsl if per_row else 1])
                pbc = ps.enter_context(nc.psum_tensor(f"p4p{wname}", [128, 1], F32))
                swt = {None: None, "sw1": sw1, "sw2": sw2}[swtile]
                redw = sb_(f"p4rw{wname}", [128, nsl])
                with nc.Block() as blk:
                    dg0 = dg.n; ev0 = ev.n
                    @blk.gpsimd
                    def _(e):
                        for gr in range(NGR):
                            if gr >= 1:
                                e.wait_ge(ev.h, ev0 + gr * GR)
                            e.dma_start(wnat[:, :],
                                        AP(W[wsrc], gr * GR * 128 * slw,
                                           [[slw, 128], [128 * slw, GR], [1, slw]])
                                        ).then_inc(dg.h, 16)
                            dg.n += 16
                    @blk.vector
                    def _(e):
                        for gr in range(NGR):
                            e.wait_ge(dg.h, dg0 + 16 * (gr + 1))
                            for sl_ in range(GR):
                                s_ = gr * GR + sl_
                                if per_row:
                                    ins = e.reduce_max(redw[:, s_:s_ + 1],
                                                       wnat[:, sl_ * slw:(sl_ + 1) * slw],
                                                       AXX, apply_absolute_value=True)
                                else:
                                    ins = e.reduce_sum(redw[:, s_:s_ + 1],
                                                       wnat[:, sl_ * slw:(sl_ + 1) * slw],
                                                       AXX, apply_absolute_value=True)
                                ins.then_inc(ev.h, 1)
                                ev.n += 1
                if per_row:
                    with nc.Block() as blk:
                        @blk.vector
                        def _(e):
                            e.tensor_scalar(redw[:, :], redw[:, :], 1e-8, None, ALU.max)
                            e.drain()
                            e.tensor_scalar(swt[:, :], redw[:, :], 1.0 / 127.0, None,
                                            ALU.mult)
                            e.reciprocal(rsw[:, :], redw[:, :])
                            e.drain()
                            e.tensor_scalar(rsw[:, :], rsw[:, :], 127.0, None,
                                            ALU.mult).then_inc(ev.h, 1)
                            ev.n += 1
                if not per_row:
                    with nc.Block() as blk:
                        @blk.vector
                        def _(e):
                            e.reduce_sum(red[:, 0:1], redw[:, :], AXX).then_inc(ev.h, 1)
                            ev.n += 1
                    with nc.Block() as blk:
                        @blk.gpsimd
                        def _(e):
                            e.wait_ge(ev.h, ev.n)
                            e.tensor_reduce(cell(8), red[:, 0:1], AXALL,
                                            ALU.add).then_inc(gp.h, 1)
                            gp.n += 1
                    with nc.Block() as blk:
                        @blk.vector
                        def _(e):
                            e.wait_ge(gp.h, gp.n)
                            e.tensor_scalar(cell(8), cell(8),
                                            1.0 / (nsl * slw * 128), 1e-5, ALU.mult, ALU.max)
                            e.drain()
                            e.reciprocal(cell(scell), cell(8))
                            e.drain()
                            ins = e.nop()
                            ins.then_inc(ev.h, 1)
                            ev.n += 1
                        @blk.tensor
                        def _(e):
                            e.wait_ge(ev.h, ev.n)
                            e.matmul(pbc[:, :], ones[0:1, :], cell(scell),
                                     start=True, stop=True).then_inc(mm.h, 1)
                            mm.n += 1
                    with nc.Block() as blk:
                        @blk.vector
                        def _(e):
                            e.wait_ge(mm.h, mm.n)
                            e.tensor_copy(rsw[:, 0:1], pbc[:, :])
                rsw_ready = ev.n
                with nc.Block() as blk:
                    sv0 = sv.n; ev0 = ev.n; ds0 = ds.n; dg0 = dg.n
                    NP = nsl * NPC
                    @blk.gpsimd
                    def _(e):
                        for gr in range(NGR):
                            if gr >= 1:
                                e.wait_ge(ev.h, ev0 + gr * GR * NPC * 4)
                            e.dma_start(wnat[:, :],
                                        AP(W[wsrc], gr * GR * 128 * slw,
                                           [[slw, 128], [128 * slw, GR], [1, slw]])
                                        ).then_inc(dg.h, 16)
                            dg.n += 16
                    @blk.scalar
                    def _(e):
                        e.wait_ge(ev.h, rsw_ready)
                        p = 0
                        for gr in range(NGR):
                            e.wait_ge(dg.h, dg0 + 16 * (gr + 1))
                            for sl_ in range(GR):
                                s_ = gr * GR + sl_
                                for pcx in range(NPC):
                                    if p >= 2:
                                        e.wait_ge(ev.h, ev0 + 4 * (p - 1))
                                    e.activation(us[p % 2][:, :],
                                                 wnat[:, sl_ * slw + pcx * 1024:
                                                      sl_ * slw + (pcx + 1) * 1024],
                                                 ACT.Identity, bias=halfc[:, 0:1],
                                                 scale=rsw[:, (s_ if per_row else 0):
                                                           (s_ + 1 if per_row else 1)]
                                                 ).then_inc(sv.h, 1)
                                    sv.n += 1
                                    p += 1
                    @blk.vector
                    def _(e):
                        p = 0
                        for s_ in range(nsl):
                            for pcx in range(NPC):
                                e.wait_ge(sv.h, sv0 + p + 1)
                                if (p // NPC) >= 2 and pcx == 0:
                                    e.wait_ge(ds.h, ds0 + 16 * (s_ - 1))
                                b = p % 2
                                nslice = nb[s_ % 2][:, pcx * 1024:(pcx + 1) * 1024]
                                e.tensor_copy(ui[b][:, :], us[b][:, :])
                                e.tensor_tensor(ug[b][:, :], ui[b][:, :], us[b][:, :],
                                                ALU.is_gt)
                                e.scalar_tensor_tensor(us[b][:, :], ui[b][:, :], -1.0,
                                                       ug[b][:, :], ALU.mult, ALU.add)
                                if per_row:
                                    ins = e.tensor_copy(nslice, us[b][:, :])
                                else:
                                    ins = e.tensor_scalar(nslice, us[b][:, :], -1.0, 1.0,
                                                          ALU.max, ALU.min)
                                ins.then_inc(ev.h, 4)
                                ev.n += 4
                                p += 1
                    @blk.sync
                    def _(e):
                        for s_ in range(nsl):
                            e.wait_ge(ev.h, ev0 + 4 * NPC * (s_ + 1))
                            e.dma_start(AP(wdst, s_ * 128 * slw, [[slw, 128], [1, slw]]),
                                        nb[s_ % 2][:, :]).then_inc(ds.h, 16)
                            ds.n += 16
                        e.wait_ge(ds.h, ds.n)
          with ExitStack() as ps:
            wtmp = ps.enter_context(nc.sbuf_tensor("p4t", [128, 2 * FF], BF16))
            with nc.Block() as blk:
                @blk.sync
                def _(e):
                    for mi, (nat, tc) in enumerate([(nwg, nwgT), (nwu, nwuT)]):
                        for ic in range(FC):
                            i = mi * FC + ic
                            e.dma_start(wtmp[:, (i % 2) * FF:(i % 2 + 1) * FF],
                                        AP(nat, ic * 128, [[D, FF], [1, 128]]),
                                        transpose=True).then_inc(ds.h, 16)
                            ds.n += 16
                            e.wait_ge(ds.h, ds.n)
                            e.dma_start(AP(tc, ic * 128 * FF, [[FF, 128], [1, FF]]),
                                        wtmp[:, (i % 2) * FF:(i % 2 + 1) * FF]
                                        ).then_inc(ds.h, 16)
                            ds.n += 16
                    e.wait_ge(ds.h, ds.n)

        if upto >= 5:
          with ExitStack() as ps:
            sb_ = lambda n, s, d=F32: ps.enter_context(nc.sbuf_tensor(n, s, d))
            rows5o = sb_("p5rowso", [1, 2 * T])
            psm = ps.enter_context(ExitStack())
            sbm = lambda n, s, d=F32: psm.enter_context(nc.sbuf_tensor(n, s, d))
            rows5m = sbm("p5rowsm", [1, 3 * T])
            def r5(i):
                if i < 3:
                    return rows5m[0:1, i * T:(i + 1) * T]
                return rows5o[0:1, (i - 3) * T:(i - 2) * T]
            iq = [sbm(f"p5iq{f}", [128, T], BF16) for f in range(FC)]
            with ExitStack() as psa:
                sba = lambda n, s, d=F32: psa.enter_context(nc.sbuf_tensor(n, s, d))
                xn5 = [sba(f"p5xn{f}", [128, T], BF16) for f in range(FC)]
                m128 = sba("p5m128", [128, T])
                n128 = sba("p5n128", [128, T])
                sbb = sba("p5sb", [128, T])
                scr5 = sba("p5sc", [128, TW])
                u5 = sba("p5u", [128, TW])
                i5 = psa.enter_context(nc.sbuf_tensor("p5i", [128, TW], I32))
                g5 = sba("p5g", [128, TW])
                sq5 = u5
                pst1 = psa.enter_context(nc.psum_tensor("p5s1", [1, TW], F32))
                pst2 = psa.enter_context(nc.psum_tensor("p5s2", [1, TW], F32))
                pbA = psa.enter_context(nc.psum_tensor("p5bA", [128, TW], F32))
                pbB = psa.enter_context(nc.psum_tensor("p5bB", [128, TW], F32))
                with nc.Block() as blk:
                    sv0 = sv.n; mm0 = mm.n; ev0 = ev.n
                    @blk.scalar
                    def _(e):
                        i = 0
                        for tt in range(NTT):
                            for f in range(FC):
                                if i >= 1:
                                    e.wait_ge(mm.h, mm0 + 2 * i)
                                e.activation(sq5[:, :], X[f][:, tt * TW:(tt + 1) * TW],
                                             ACT.Square).then_inc(sv.h, 1)
                                sv.n += 1
                                i += 1
                    @blk.tensor
                    def _(e):
                        e.wait_ge(ev.h, ev0)
                        i = 0
                        for tt in range(NTT):
                            for f in range(FC):
                                if f == 0 and tt > 0:
                                    e.wait_ge(ev.h, ev0 + 2 * tt)
                                e.matmul(pst1[:, :], ones[:, 0:1],
                                         X[f][:, tt * TW:(tt + 1) * TW],
                                         start=(f == 0), stop=(f == FC - 1)).then_inc(mm.h, 1)
                                mm.n += 1
                                e.wait_ge(sv.h, sv0 + i + 1)
                                e.matmul(pst2[:, :], ones[:, 0:1], sq5[:, :],
                                         start=(f == 0), stop=(f == FC - 1)).then_inc(mm.h, 1)
                                mm.n += 1
                                i += 1
                    @blk.vector
                    def _(e):
                        for tt in range(NTT):
                            e.wait_ge(mm.h, mm0 + (tt + 1) * 16)
                            sl = slice(tt * TW, (tt + 1) * TW)
                            e.tensor_scalar(r5(0)[:, sl], pst1[:, :], 1.0 / D, None, ALU.mult)
                            e.tensor_scalar(r5(2)[:, sl], pst2[:, :], 1.0 / D, None,
                                            ALU.mult).then_inc(ev.h, 2)
                            ev.n += 2
                with nc.Block() as blk:
                    @blk.vector
                    def _(e):
                        e.tensor_tensor(r5(1), r5(0), r5(0), ALU.mult)
                        e.tensor_tensor(r5(1), r5(2), r5(1), ALU.subtract)
                        e.tensor_scalar(r5(2), r5(1), 1e-5, None, ALU.add).then_inc(ev.h, 1)
                        ev.n += 1
                    @blk.scalar
                    def _(e):
                        e.wait_ge(ev.h, ev.n)
                        e.activation(r5(3), r5(2), ACT.Ln)
                        e.activation(r5(3), r5(3), ACT.Exp, scale=-0.5)
                with nc.Block() as blk:
                    @blk.vector
                    def _(e):
                        e.tensor_tensor(r5(4), r5(0), r5(3), ALU.mult)
                        e.tensor_scalar(r5(4), r5(4), -1.0, None, ALU.mult)
                with nc.Block() as blk:
                    mm0 = mm.n; ev0 = ev.n
                    @blk.tensor
                    def _(e):
                        for tt in range(NTT):
                            sl = slice(tt * TW, (tt + 1) * TW)
                            if tt > 0:
                                e.wait_ge(ev.h, ev0 + tt * 8)
                            e.matmul(pbA[:, :], ones[0:1, :], r5(3)[:, sl], start=True, stop=True)
                            e.matmul(pbB[:, :], ones[0:1, :], r5(4)[:, sl],
                                     start=True, stop=True).then_inc(mm.h, 2)
                            mm.n += 2
                    @blk.vector
                    def _(e):
                        for tt in range(NTT):
                            e.wait_ge(mm.h, mm0 + 2 * (tt + 1))
                            sl = slice(tt * TW, (tt + 1) * TW)
                            for f in range(FC):
                                e.tensor_tensor(scr5[:, :], X[f][:, sl], pbA[:, :], ALU.mult)
                                e.tensor_tensor(xn5[f][:, sl], scr5[:, :], pbB[:, :], ALU.add)
                                if f == 0:
                                    e.tensor_copy(m128[:, sl], xn5[f][:, sl])
                                    ins = e.tensor_copy(n128[:, sl], xn5[f][:, sl])
                                else:
                                    e.tensor_tensor(m128[:, sl], xn5[f][:, sl],
                                                    m128[:, sl], ALU.max)
                                    ins = e.tensor_tensor(n128[:, sl], xn5[f][:, sl],
                                                          n128[:, sl], ALU.min)
                            ins.then_inc(ev.h, 8)
                            ev.n += 8
                with nc.Block() as blk:
                    @blk.vector
                    def _(e):
                        e.tensor_scalar(n128[:, :], n128[:, :], -1.0, None, ALU.mult)
                        e.tensor_tensor(m128[:, :], m128[:, :], n128[:, :],
                                        ALU.max).then_inc(ev.h, 1)
                        ev.n += 1
                        e.reciprocal(cell(9), cell(0))
                        e.reciprocal(cell(10), cell(1))
                        e.drain()
                with nc.Block() as blk:
                    @blk.gpsimd
                    def _(e):
                        e.wait_ge(ev.h, ev.n)
                        e.tensor_reduce(r5(0), m128[:, :], AXC, ALU.max).then_inc(gp.h, 1)
                        gp.n += 1
                    @blk.vector
                    def _(e):
                        e.wait_ge(gp.h, gp.n)
                        e.tensor_scalar(r5(2), r5(0), 1e-5, None, ALU.max)
                        e.reciprocal(r5(2), r5(2))
                        e.tensor_scalar(r5(2), r5(2), 127.0, None, ALU.mult)
                        e.tensor_tensor(r5(1), r5(0), r5(0), ALU.mult)
                        e.tensor_scalar(r5(1), r5(1), cell(9), cell(10), ALU.mult, ALU.mult)
                        e.tensor_scalar(r5(1), r5(1), 1.0 / (127.0 * 127.0), None, ALU.mult)
                with nc.Block() as blk:
                    mm0 = mm.n; ev0 = ev.n
                    @blk.tensor
                    def _(e):
                        for tt in range(NTT):
                            sl = slice(tt * TW, (tt + 1) * TW)
                            if tt > 0:
                                e.wait_ge(ev.h, ev0 + tt)
                            e.matmul(pbA[:, :], ones[0:1, :], r5(2)[:, sl],
                                     start=True, stop=True).then_inc(mm.h, 1)
                            mm.n += 1
                    @blk.vector
                    def _(e):
                        for tt in range(NTT):
                            e.wait_ge(mm.h, mm0 + tt + 1)
                            e.tensor_copy(sbb[:, tt * TW:(tt + 1) * TW],
                                          pbA[:, :]).then_inc(ev.h, 1)
                            ev.n += 1
                with nc.Block() as blk:
                    ev0 = ev.n; sv0 = sv.n
                    @blk.vector
                    def _(e):
                        for f in range(FC):
                            for tt in range(NTT):
                                sl = slice(tt * TW, (tt + 1) * TW)
                                e.tensor_tensor(scr5[:, :], xn5[f][:, sl], sbb[:, sl],
                                                ALU.mult)
                                e.tensor_scalar(u5[:, :], scr5[:, :], 0.5, None, ALU.add)
                                e.tensor_copy(i5[:, :], u5[:, :])
                                e.tensor_tensor(g5[:, :], i5[:, :], u5[:, :], ALU.is_gt)
                                e.scalar_tensor_tensor(iq[f][:, sl], i5[:, :], -1.0,
                                                       g5[:, :], ALU.mult, ALU.add
                                                       ).then_inc(ev.h, 2)
                                ev.n += 2
            with ExitStack() as psb:
                sbb_ = lambda n, s, d=F32: psb.enter_context(nc.sbuf_tensor(n, s, d))
                mb = sbb_("p5mb", [128, T])
                cgb = sbb_("p5cgb", [128, T])
                mv = sbb_("p5mv", [128, T])
                mvn = sbb_("p5mvn", [128, T])
                wgb = [psb.enter_context(nc.sbuf_tensor(f"p5wg{i}", [128, FC * 128], BF16))
                       for i in range(2)]
                wub = [psb.enter_context(nc.sbuf_tensor(f"p5wu{i}", [128, FC * 128], BF16))
                       for i in range(2)]
                gbuf = [psb.enter_context(nc.sbuf_tensor(f"p5gb{i}", [128, TW], BF16))
                        for i in range(2)]
                ubuf = [psb.enter_context(nc.sbuf_tensor(f"p5ub{i}", [128, TW], BF16))
                        for i in range(2)]
                t1b = [psb.enter_context(nc.sbuf_tensor(f"p5t1{i}", [128, TW], BF16))
                       for i in range(2)]
                sgb = [psb.enter_context(nc.sbuf_tensor(f"p5sg{i}", [128, TW], BF16))
                       for i in range(2)]
                vt = [psb.enter_context(nc.sbuf_tensor(f"p5vt{i}", [128, T], BF16))
                      for i in range(2)]
                pG = [psb.enter_context(nc.psum_tensor(f"p5pG{i}", [128, TW], F32))
                      for i in range(2)]
                pU = [psb.enter_context(nc.psum_tensor(f"p5pU{i}", [128, TW], F32))
                      for i in range(2)]
                with nc.Block() as blk:
                    mm0 = mm.n
                    @blk.vector
                    def _(e):
                        e.tensor_scalar(cell(3), cell(0), 127.0, None, ALU.mult)
                        e.drain()
                        e.reciprocal(cell(3), cell(3))
                        e.drain()
                        e.nop().then_inc(ev.h, 1)
                        ev.n += 1
                    @blk.tensor
                    def _(e):
                        for tt in range(NTT):
                            e.matmul(pG[tt % 2][:, :], ones[0:1, :],
                                     r5(0)[:, tt * TW:(tt + 1) * TW],
                                     start=True, stop=True).then_inc(mm.h, 1)
                            mm.n += 1
                    @blk.gpsimd
                    def _(e):
                        pass
                with nc.Block() as blk:
                    @blk.vector
                    def _(e):
                        for tt in range(NTT):
                            e.tensor_copy(mb[:, tt * TW:(tt + 1) * TW], pG[tt % 2][:, :])
                        e.memset(mv[:, :], 0.0)
                with nc.Block() as blk:
                    @blk.tensor
                    def _(e):
                        e.matmul(pG[0][:, 0:1], ones[0:1, :], cell(3),
                                 start=True, stop=True).then_inc(mm.h, 1)
                        mm.n += 1
                with nc.Block() as blk:
                    @blk.vector
                    def _(e):
                        e.tensor_scalar(cgb[:, :], mb[:, :], pG[0][:, 0:1], None, ALU.mult)
                iq_ready = ev.n
                with nc.Block() as blk:
                    mm0 = mm.n; ev0 = ev.n; sv0 = sv.n; ds0 = ds.n; dg0 = dg.n
                    NU = FFC * 4
                    @blk.sync
                    def _(e):
                        for og in range(FFC):
                            if og >= 2:
                                e.wait_ge(mm.h, mm0 + 8 * (og - 1))
                            e.dma_start(wgb[og % 2][:, :],
                                        AP(nwgT, og * 128,
                                           [[FF, 128], [128 * FF, FC], [1, 128]])
                                        ).then_inc(ds.h, 16)
                            ds.n += 16
                            e.dma_start(wub[og % 2][:, :],
                                        AP(nwuT, og * 128,
                                           [[FF, 128], [128 * FF, FC], [1, 128]])
                                        ).then_inc(ds.h, 16)
                            ds.n += 16
                    @blk.gpsimd
                    def _(e):
                        for og in range(FFC):
                            e.wait_ge(ev.h, ev0 + 3 * (4 * og + 4))
                            e.dma_start(AP(vt_sp, og * 128 * T, [[T, 128], [1, T]]),
                                        vt[og % 2][:, :]).then_inc(dg.h, 16)
                            dg.n += 16
                    @blk.tensor
                    def _(e):
                        e.wait_ge(ev.h, iq_ready)
                        for og in range(FFC):
                            e.wait_ge(ds.h, ds0 + 32 * (og + 1))
                            for q in range(4):
                                u = og * 4 + q
                                sl = slice(q * TW, (q + 1) * TW)
                                if u >= 2:
                                    e.wait_ge(sv.h, sv0 + 3 * (u - 1))
                                last = None
                                for kc in range(FC):
                                    last = e.matmul(pG[u % 2][:, :],
                                                    wgb[og % 2][:, kc * 128:(kc + 1) * 128],
                                                    iq[kc][:, sl],
                                                    start=(kc == 0), stop=(kc == FC - 1))
                                last.then_inc(mm.h, 1); mm.n += 1
                                last = None
                                for kc in range(FC):
                                    last = e.matmul(pU[u % 2][:, :],
                                                    wub[og % 2][:, kc * 128:(kc + 1) * 128],
                                                    iq[kc][:, sl],
                                                    start=(kc == 0), stop=(kc == FC - 1))
                                last.then_inc(mm.h, 1); mm.n += 1
                    @blk.scalar
                    def _(e):
                        for og in range(FFC):
                            for q in range(4):
                                u = og * 4 + q
                                e.wait_ge(mm.h, mm0 + 2 * u + 1)
                                if u >= 2:
                                    e.wait_ge(ev.h, ev0 + 3 * (u - 2) + 2)
                                e.activation(gbuf[u % 2][:, :], pG[u % 2][:, :], ACT.Identity)
                                e.wait_ge(mm.h, mm0 + 2 * u + 2)
                                e.activation(ubuf[u % 2][:, :], pU[u % 2][:, :],
                                             ACT.Identity).then_inc(sv.h, 2)
                                sv.n += 2
                                e.wait_ge(ev.h, ev0 + 3 * u + 1)
                                e.activation(sgb[u % 2][:, :], t1b[u % 2][:, :],
                                             ACT.Sigmoid).then_inc(sv.h, 1)
                                sv.n += 1
                    @blk.vector
                    def _(e):
                        for og in range(FFC):
                            for q in range(4):
                                u = og * 4 + q
                                sl = slice(q * TW, (q + 1) * TW)
                                e.wait_ge(sv.h, sv0 + 3 * u + 2)
                                if u >= 2:
                                    e.wait_ge(sv.h, sv0 + 3 * (u - 2) + 3)
                                e.tensor_tensor(t1b[u % 2][:, :], gbuf[u % 2][:, :],
                                                cgb[:, sl], ALU.mult).then_inc(ev.h, 1)
                                ev.n += 1
                                e.wait_ge(sv.h, sv0 + 3 * u + 3)
                                if og >= 2 and q == 0:
                                    e.wait_ge(dg.h, dg0 + 16 * (og - 1))
                                e.tensor_tensor(vt[og % 2][:, sl], gbuf[u % 2][:, :],
                                                ubuf[u % 2][:, :], ALU.mult)
                                e.tensor_tensor(vt[og % 2][:, sl], vt[og % 2][:, sl],
                                                sgb[u % 2][:, :], ALU.mult).then_inc(ev.h, 1)
                                ev.n += 1
                                e.tensor_tensor(mv[:, sl], vt[og % 2][:, sl], mv[:, sl],
                                                ALU.max)
                                e.tensor_tensor(mvn[:, sl], vt[og % 2][:, sl], mvn[:, sl],
                                                ALU.min).then_inc(ev.h, 1)
                                ev.n += 1
                with nc.Block() as blk:
                    @blk.vector
                    def _(e):
                        e.tensor_scalar(mvn[:, :], mvn[:, :], -1.0, None, ALU.mult)
                        e.tensor_tensor(mv[:, :], mv[:, :], mvn[:, :],
                                        ALU.max).then_inc(ev.h, 1)
                        ev.n += 1
                    @blk.gpsimd
                    def _(e):
                        e.wait_ge(ev.h, ev.n)
                        e.wait_ge(dg.h, dg.n)
                        e.tensor_reduce(r5(2), mv[:, :], AXC, ALU.max).then_inc(gp.h, 1)
                        gp.n += 1
                if debug:
                    d_cg = dbgt("cg", [2, 128, T], F32)
                    d_sc = dbgt("cells", [1, 64], F32)
                    with nc.Block() as blk:
                        @blk.sync
                        def _(e):
                            e.dma_start(AP(d_cg, 0, [[T, 128], [1, T]]),
                                        cgb[:, :]).then_inc(ds.h, 16)
                            ds.n += 16
                            e.dma_start(AP(d_cg, 128 * T, [[T, 128], [1, T]]),
                                        mb[:, :]).then_inc(ds.h, 16)
                            ds.n += 16
                            e.dma_start(AP(d_sc, 0, [[1, 1], [1, 64]]),
                                        scal[0:1, :]).then_inc(ds.h, 16)
                            ds.n += 16
                            e.wait_ge(ds.h, ds.n)
                with nc.Block() as blk:
                    @blk.vector
                    def _(e):
                        e.wait_ge(gp.h, gp.n)
                        e.tensor_tensor(r5(2), r5(2), r5(1), ALU.mult)
                        e.tensor_scalar(r5(2), r5(2), 1e-5, None, ALU.max)
                        e.reciprocal(r5(3), r5(2))
                        e.tensor_tensor(r5(3), r5(3), r5(1), ALU.mult)
                        e.tensor_scalar(r5(3), r5(3), 127.0, None, ALU.mult)
                        e.reciprocal(cell(8), cell(2))
                        e.drain()
                        e.tensor_scalar(r5(4), r5(2), 1.0 / 127.0, cell(8),
                                        ALU.mult, ALU.mult)
            if debug:
                d_iq = dbgt("iq", [FC, 128, T], BF16)
                d_vt = dbgt("vt", [FFC, 128, T], BF16)
                d_r5 = dbgt("r5", [5, T], F32)
                d_nw = dbgt("nw", [3, FF * D], BF16)
                with nc.Block() as blk:
                    @blk.sync
                    def _(e):
                        for f in range(FC):
                            e.dma_start(AP(d_iq, f * 128 * T, [[T, 128], [1, T]]),
                                        iq[f][:, :]).then_inc(ds.h, 16)
                            ds.n += 16
                        e.dma_start(AP(d_vt, 0, [[1, FFC * 128 * T]]),
                                    AP(vt_sp, 0, [[1, FFC * 128 * T]])).then_inc(ds.h, 16)
                        ds.n += 16
                        for i in range(5):
                            e.dma_start(AP(d_r5, i * T, [[1, 1], [1, T]]),
                                        r5(i)).then_inc(ds.h, 16)
                            ds.n += 16
                        for i, nat in enumerate([nwg, nwu, nwd]):
                            e.dma_start(AP(d_nw, i * FF * D, [[1, FF * D]]),
                                        AP(nat, 0, [[1, FF * D]])).then_inc(ds.h, 16)
                            ds.n += 16
                        e.wait_ge(ds.h, ds.n)
            psm.close()
            with ExitStack() as psc:
                sbc = lambda n, s, d=F32: psc.enter_context(nc.sbuf_tensor(n, s, d))
                wdT = psc.enter_context(nc.sbuf_tensor("p5wd", [128, FFC * D], BF16))
                fbb = sbc("p5fbb", [128, TW])
                ebb = sbc("p5ebb", [128, TW])
                scr6 = sbc("p5sc6", [128, TW])
                u6 = sbc("p5u6", [128, TW])
                i6 = psc.enter_context(nc.sbuf_tensor("p5i6", [128, TW], I32))
                g6 = sbc("p5g6", [128, TW])
                iq2 = psc.enter_context(nc.sbuf_tensor("p5iqq", [128, FFC * TW], BF16))
                peb = psc.enter_context(nc.psum_tensor("p5peb", [128, TW], F32))
                pfb = psc.enter_context(nc.psum_tensor("p5pfb", [128, TW], F32))
                pout5 = [psc.enter_context(nc.psum_tensor(f"p5po{i}", [128, TW], F32))
                         for i in range(2)]
                with nc.Block() as blk:
                    @blk.sync
                    def _(e):
                        for kc in range(FFC):
                            e.dma_start(wdT[:, kc * D:(kc + 1) * D],
                                        AP(nwd, kc * 128, [[FF, D], [1, 128]]),
                                        transpose=True).then_inc(ds.h, 16)
                            ds.n += 16
                        e.wait_ge(ds.h, ds.n)
                with nc.Block() as blk:
                    mm0 = mm.n; ev0 = ev.n; sv0 = sv.n; ds0 = ds.n
                    @blk.sync
                    def _(e):
                        for tt in range(NTT):
                            if tt >= 1:
                                e.wait_ge(ev.h, ev0 + tt * 72)
                            e.dma_start(iq2[:, :],
                                        AP(vt_sp, tt * TW,
                                           [[T, 128], [128 * T, FFC], [1, TW]])
                                        ).then_inc(ds.h, 16)
                            ds.n += 16
                    @blk.tensor
                    def _(e):
                        for tt in range(NTT):
                            sl = slice(tt * TW, (tt + 1) * TW)
                            if tt > 0:
                                e.wait_ge(ev.h, ev0 + tt * 72)
                            e.matmul(pfb[:, :], ones[0:1, :], r5(3)[:, sl],
                                     start=True, stop=True)
                            e.matmul(peb[:, :], ones[0:1, :], r5(4)[:, sl],
                                     start=True, stop=True).then_inc(mm.h, 2)
                            mm.n += 2
                            e.wait_ge(ev.h, ev0 + tt * 72 + 64)
                            for fcx in range(FC):
                                if fcx >= 2:
                                    e.wait_ge(ev.h, ev0 + tt * 72 + 64 + (fcx - 1))
                                last = None
                                for kc in range(FFC):
                                    last = e.matmul(pout5[fcx % 2][:, :],
                                                    wdT[:, kc * D + fcx * 128:
                                                        kc * D + (fcx + 1) * 128],
                                                    iq2[:, kc * TW:(kc + 1) * TW],
                                                    start=(kc == 0), stop=(kc == FFC - 1))
                                last.then_inc(mm.h, 1); mm.n += 1
                    @blk.scalar
                    def _(e):
                        for tt in range(NTT):
                            for kc in range(FFC):
                                i = tt * FFC + kc
                                e.wait_ge(ev.h, ev0 + tt * 72 + 2 * kc + 1)
                                e.activation(u6[:, :], scr6[:, :], ACT.Identity,
                                             bias=halfc[:, 0:1]).then_inc(sv.h, 1)
                                sv.n += 1
                    @blk.vector
                    def _(e):
                        for tt in range(NTT):
                            sl = slice(tt * TW, (tt + 1) * TW)
                            e.wait_ge(mm.h, mm0 + tt * 10 + 2)
                            e.wait_ge(ds.h, ds0 + 16 * (tt + 1))
                            e.tensor_copy(fbb[:, :], pfb[:, :])
                            e.tensor_copy(ebb[:, :], peb[:, :])
                            for kc in range(FFC):
                                i = tt * FFC + kc
                                if kc >= 1:
                                    e.wait_ge(sv.h, sv0 + tt * FFC + kc)
                                e.tensor_tensor(scr6[:, :], iq2[:, kc * TW:(kc + 1) * TW],
                                                fbb[:, :], ALU.mult).then_inc(ev.h, 1)
                                ev.n += 1
                                e.wait_ge(sv.h, sv0 + tt * FFC + kc + 1)
                                e.tensor_copy(i6[:, :], u6[:, :])
                                e.tensor_tensor(g6[:, :], i6[:, :], u6[:, :], ALU.is_gt)
                                e.scalar_tensor_tensor(iq2[:, kc * TW:(kc + 1) * TW],
                                                       i6[:, :], -1.0, g6[:, :],
                                                       ALU.mult, ALU.add).then_inc(ev.h, 1)
                                ev.n += 1
                            for fcx in range(FC):
                                e.wait_ge(mm.h, mm0 + tt * 10 + 2 + fcx + 1)
                                e.tensor_tensor(scr6[:, :], pout5[fcx % 2][:, :], ebb[:, :],
                                                ALU.mult)
                                e.tensor_tensor(X[fcx][:, sl], X[fcx][:, sl], scr6[:, :],
                                                ALU.add).then_inc(ev.h, 1)
                                ev.n += 1

        if upto >= 6:
          with ExitStack() as ps:
            psmid = ps.enter_context(ExitStack())
            sb_ = lambda n, s, d=F32: psmid.enter_context(nc.sbuf_tensor(n, s, d))
            HT6 = 1024
            rows6 = sb_("p6rows", [1, 3 * T])
            r6 = lambda i: rows6[0:1, i * T:(i + 1) * T]
            xn6 = [sb_(f"p6xn{f}", [128, T], BF16) for f in range(FC)]
            Opair2 = psmid.enter_context(nc.sbuf_tensor("p6Op", [64, 2 * T], BF16))
            cosb = sb_("p6cos", [128, T]); sinb = sb_("p6sin", [128, T])
            tri0 = psmid.enter_context(nc.sbuf_tensor("p6t0", [128, 128], BF16))
            tri4 = psmid.enter_context(nc.sbuf_tensor("p6t4", [128, 128], BF16))
            qaug = [psmid.enter_context(nc.sbuf_tensor(f"p6qa{h}", [65, T], BF16)) for h in range(2)]
            kaug = [psmid.enter_context(nc.sbuf_tensor(f"p6ka{h}", [65, T], BF16)) for h in range(2)]
            vtm65 = psmid.enter_context(nc.sbuf_tensor("p6vtm", [128, 16 * 130], BF16))
            qf = sb_("p6qf", [128, HT6]); qr = sb_("p6qr", [128, HT6])
            qhat = psmid.enter_context(nc.sbuf_tensor("p6qh", [128, HT6], BF16))
            wqb0 = psmid.enter_context(nc.sbuf_tensor("p6wq0", [128, FC * 128], BF16))
            wqb = [wqb0, wqb0]
            pt6 = [psmid.enter_context(nc.sbuf_tensor(f"p6pt{i}", [128, 640], BF16))
                   for i in range(2)]


            with nc.Block() as blk:
                @blk.sync
                def _(e):
                    e.dma_start(cosb[:, :], W["rope_cos"].ap()).then_inc(ds.h, 16)
                    ds.n += 16
                    e.dma_start(sinb[:, :], W["rope_sin"].ap()).then_inc(ds.h, 16)
                    ds.n += 16
                    e.wait_ge(ds.h, ds.n)
                @blk.gpsimd
                def _(e):
                    e.affine_select(tri0[:, :], onesb[:, :], [[1, 128]], ALU.is_ge, 0.0,
                                    base=0, channel_multiplier=-1)
                    e.affine_select(tri4[:, :], onesb[:, :], [[-1, 128]], ALU.is_gt, 0.0,
                                    base=0, channel_multiplier=1).then_inc(gp.h, 1)
                    gp.n += 1
                @blk.vector
                def _(e):
                    last = None
                    for tc in range(16):
                        e.memset(vtm65[:, tc * 130 + 64:tc * 130 + 65], 1.0)
                        last = e.memset(vtm65[:, tc * 130 + 129:tc * 130 + 130], 1.0)
                    last.then_inc(ev.h, 1)
                    ev.n += 1

            with ExitStack() as psa:
                sba = lambda n, s, d=F32: psa.enter_context(nc.sbuf_tensor(n, s, d))
                sq6 = sba("p6sq6", [128, TW])
                scr = sba("p6scr", [128, TW])
                pst1 = psa.enter_context(nc.psum_tensor("p6s1", [1, TW], F32))
                pst2 = psa.enter_context(nc.psum_tensor("p6s2", [1, TW], F32))
                pbA = psa.enter_context(nc.psum_tensor("p6bA", [128, TW], F32))
                pbB = psa.enter_context(nc.psum_tensor("p6bB", [128, TW], F32))
                with nc.Block() as blk:
                    sv0 = sv.n; mm0 = mm.n; ev0 = ev.n
                    @blk.scalar
                    def _(e):
                        i = 0
                        for tt in range(NTT):
                            for f in range(FC):
                                if i >= 1:
                                    e.wait_ge(mm.h, mm0 + 2 * i)
                                e.activation(sq6[:, :], X[f][:, tt * TW:(tt + 1) * TW],
                                             ACT.Square).then_inc(sv.h, 1)
                                sv.n += 1
                                i += 1
                    @blk.tensor
                    def _(e):
                        e.wait_ge(ev.h, ev0)
                        i = 0
                        for tt in range(NTT):
                            for f in range(FC):
                                if f == 0 and tt > 0:
                                    e.wait_ge(ev.h, ev0 + 2 * tt)
                                e.matmul(pst1[:, :], ones[:, 0:1],
                                         X[f][:, tt * TW:(tt + 1) * TW],
                                         start=(f == 0), stop=(f == FC - 1)).then_inc(mm.h, 1)
                                mm.n += 1
                                e.wait_ge(sv.h, sv0 + i + 1)
                                e.matmul(pst2[:, :], ones[:, 0:1], sq6[:, :],
                                         start=(f == 0), stop=(f == FC - 1)).then_inc(mm.h, 1)
                                mm.n += 1
                                i += 1
                    @blk.vector
                    def _(e):
                        for tt in range(NTT):
                            e.wait_ge(mm.h, mm0 + (tt + 1) * 16)
                            sl = slice(tt * TW, (tt + 1) * TW)
                            e.tensor_scalar(r6(0)[:, sl], pst1[:, :], 1.0 / D, None, ALU.mult)
                            e.tensor_scalar(r6(2)[:, sl], pst2[:, :], 1.0 / D, None,
                                            ALU.mult).then_inc(ev.h, 2)
                            ev.n += 2
                with nc.Block() as blk:
                    @blk.vector
                    def _(e):
                        e.tensor_tensor(r6(1), r6(0), r6(0), ALU.mult)
                        e.tensor_tensor(r6(1), r6(2), r6(1), ALU.subtract)
                        e.tensor_scalar(r6(2), r6(1), 1e-5, None, ALU.add).then_inc(ev.h, 1)
                        ev.n += 1
                    @blk.scalar
                    def _(e):
                        e.wait_ge(ev.h, ev.n)
                        e.activation(r6(1), r6(2), ACT.Ln)
                        e.activation(r6(1), r6(1), ACT.Exp, scale=-0.5)
                with nc.Block() as blk:
                    @blk.vector
                    def _(e):
                        e.tensor_tensor(r6(0), r6(0), r6(1), ALU.mult)
                        e.tensor_scalar(r6(0), r6(0), -1.0, None,
                                        ALU.mult).then_inc(ev.h, 1)
                        ev.n += 1
                with nc.Block() as blk:
                    mm0 = mm.n; ev0 = ev.n; sv0 = sv.n
                    @blk.tensor
                    def _(e):
                        e.wait_ge(ev.h, ev0)
                        for tt in range(NTT):
                            sl = slice(tt * TW, (tt + 1) * TW)
                            if tt > 0:
                                e.wait_ge(ev.h, ev0 + tt * 8)
                            e.matmul(pbA[:, :], ones[0:1, :], r6(1)[:, sl], start=True, stop=True)
                            e.matmul(pbB[:, :], ones[0:1, :], r6(0)[:, sl],
                                     start=True, stop=True).then_inc(mm.h, 2)
                            mm.n += 2
                    @blk.vector
                    def _(e):
                        i = 0
                        for tt in range(NTT):
                            e.wait_ge(mm.h, mm0 + 2 * (tt + 1))
                            sl = slice(tt * TW, (tt + 1) * TW)
                            for f in range(FC):
                                e.tensor_tensor(scr[:, :], X[f][:, sl], pbA[:, :], ALU.mult)
                                e.tensor_tensor(xn6[f][:, sl], scr[:, :], pbB[:, :],
                                                ALU.add).then_inc(ev.h, 1)
                                ev.n += 1
                                i += 1

            for p_ in range(8):
              with ExitStack() as psp:
                psqkv = psp.enter_context(ExitStack())
                pqk = [psqkv.enter_context(nc.psum_tensor(f"p6qk{p_}{i}", [128, TW], F32))
                       for i in range(2)]
                pnrm = psqkv.enter_context(nc.psum_tensor(f"p6nr{p_}", [1, TW], F32))
                for which, coloff in [("q", 128 * p_), ("k", D + 128 * p_)]:
                    with nc.Block() as blk:
                        dg0 = dg.n; mm0 = mm.n; sv0 = sv.n; ev0 = ev.n; ds0 = ds.n
                        @blk.gpsimd
                        def _(e):
                            e.dma_start(wqb0[:, :],
                                        AP(W["L1_attn_Wqkv"], coloff,
                                           [[3 * D, 128], [128 * 3 * D, FC], [1, 128]])
                                        ).then_inc(dg.h, 16)
                            dg.n += 16
                        @blk.tensor
                        def _(e):
                            e.wait_ge(dg.h, dg0 + 16)
                            e.wait_ge(sv.h, sv0)
                            for hh in range(2):
                                for q2 in range(2):
                                    if hh + q2 >= 1:
                                        e.wait_ge(sv.h, sv0 + 2 * (2 * hh + q2))
                                    last = None
                                    for kc in range(FC):
                                        last = e.matmul(
                                            pqk[q2][:, :],
                                            wqb0[:, kc * 128:(kc + 1) * 128],
                                            xn6[kc][:, (2 * hh + q2) * TW:(2 * hh + q2 + 1) * TW],
                                            start=(kc == 0), stop=(kc == FC - 1))
                                    last.then_inc(mm.h, 1); mm.n += 1
                        @blk.scalar
                        def _(e):
                            for hh in range(2):
                                if hh == 1:
                                    e.wait_ge(ds.h, ds0 + 96)
                                    e.wait_ge(ev.h, ev0 + 1)
                                for q2 in range(2):
                                    i = 2 * hh + q2
                                    e.wait_ge(mm.h, mm0 + i + 1)
                                    if which == "q":
                                        e.activation(qf[:, q2 * TW:(q2 + 1) * TW],
                                                     pqk[q2][:, :], ACT.Identity,
                                                     scale=pcol[:, 0:1]).then_inc(sv.h, 2)
                                    else:
                                        e.activation(qf[:, q2 * TW:(q2 + 1) * TW],
                                                     pqk[q2][:, :],
                                                     ACT.Identity).then_inc(sv.h, 2)
                                    sv.n += 2
                        @blk.sync
                        def _(e):
                            dsts = qaug if which == "q" else kaug
                            for hh in range(2):
                                hsl = slice(hh * HT6, (hh + 1) * HT6)
                                e.wait_ge(sv.h, sv0 + 4 * hh + 4)
                                for (d0, s0) in [(0, 32), (32, 0), (64, 96), (96, 64)]:
                                    e.dma_start(qr[d0:d0 + 32, :], qf[s0:s0 + 32, :]
                                                ).then_inc(ds.h, 16)
                                    ds.n += 16
                                e.wait_ge(ev.h, ev0 + hh + 1)
                                e.dma_start(dsts[0][0:64, hsl],
                                            qhat[0:64, :]).then_inc(ds.h, 16)
                                ds.n += 16
                                e.dma_start(dsts[1][0:64, hsl],
                                            qhat[64:128, :]).then_inc(ds.h, 16)
                                ds.n += 16
                            e.wait_ge(ds.h, ds.n)
                        @blk.vector
                        def _(e):
                            for hh in range(2):
                                e.wait_ge(ds.h, ds0 + 96 * hh + 64)
                                hsl = slice(hh * HT6, (hh + 1) * HT6)
                                e.tensor_tensor(qf[:, :], qf[:, :], cosb[:, hsl], ALU.mult)
                                e.tensor_tensor(qr[:, :], qr[:, :], sinb[:, hsl], ALU.mult)
                                e.tensor_tensor(qhat[:, :], qf[:, :], qr[:, :],
                                                ALU.add).then_inc(ev.h, 1)
                                ev.n += 1
                with nc.Block() as blk:
                    @blk.vector
                    def _(e):
                        e.memset(kaug[0][64:65, :], -1.0)
                        e.memset(kaug[1][64:65, :], -1.0).then_inc(ev.h, 1)
                        ev.n += 1
                for hl in range(2):
                    with nc.Block() as blk:
                        mm0 = mm.n; sv0 = sv.n; ev0 = ev.n
                        @blk.scalar
                        def _(e):
                            for i_, (tile, _c0) in enumerate(
                                    [(qaug[hl], 12), (qaug[hl], 12),
                                     (kaug[hl], 16), (kaug[hl], 16)]):
                                hh = i_ % 2
                                if i_ > 0:
                                    e.wait_ge(mm.h, mm0 + 2 * i_)
                                hsl = slice(hh * HT6, (hh + 1) * HT6)
                                e.activation(qhat[0:64, :], tile[0:64, hsl],
                                             ACT.Square).then_inc(sv.h, 1)
                                sv.n += 1
                        @blk.tensor
                        def _(e):
                            for j in range(8):
                                e.wait_ge(sv.h, sv0 + j // 2 + 1)
                                if j > 0:
                                    e.wait_ge(ev.h, ev0 + j)
                                e.matmul(pnrm[:, :], onesb[0:64, 0:1],
                                         qhat[0:64, (j % 2) * TW:(j % 2 + 1) * TW],
                                         start=True, stop=True).then_inc(mm.h, 1)
                                mm.n += 1
                        @blk.vector
                        def _(e):
                            for j in range(8):
                                e.wait_ge(mm.h, mm0 + j + 1)
                                c0 = 12 if j < 4 else 16
                                e.reduce_max(scal[0:1, c0 + (j % 4):c0 + (j % 4) + 1],
                                             pnrm[:, :], AXX).then_inc(ev.h, 1)
                                ev.n += 1
                            e.drain()
                            for c0 in (12, 16):
                                for k_ in (1, 2, 3):
                                    e.tensor_scalar(cell(c0), cell(c0), cell(c0 + k_),
                                                    None, ALU.max)
                                    e.drain()
                    with nc.Block() as blk:
                        @blk.scalar
                        def _(e):
                            e.activation(cell(12), cell(12), ACT.Sqrt)
                            e.activation(cell(16), cell(16), ACT.Sqrt)
                            e.nop().then_inc(sv.h, 1); sv.n += 1
                    with nc.Block() as blk:
                        @blk.vector
                        def _(e):
                            e.tensor_scalar(cell(12), cell(12), cell(16), None, ALU.mult)
                            e.drain()
                            e.tensor_scalar(cell(12), cell(12), -1.0625, -1.0,
                                            ALU.mult, ALU.add)
                            e.drain()
                            e.tensor_scalar(qaug[hl][64:65, :], kaug[hl][64:65, :],
                                            cell(12), None, ALU.mult).then_inc(ev.h, 1)
                            ev.n += 1
                with nc.Block() as blk:
                    dg0 = dg.n; mm0 = mm.n; sv0 = sv.n
                    @blk.gpsimd
                    def _(e):
                        e.dma_start(wqb0[:, :],
                                    AP(W["L1_attn_Wqkv"], 2 * D + 128 * p_,
                                       [[3 * D, 128], [128 * 3 * D, FC], [1, 128]])
                                    ).then_inc(dg.h, 16)
                        dg.n += 16
                    @blk.tensor
                    def _(e):
                        e.wait_ge(dg.h, dg0 + 16)
                        for tc in range(16):
                            if tc >= 2:
                                e.wait_ge(sv.h, sv0 + tc - 1)
                            last = None
                            for kc in range(FC):
                                last = e.matmul(pqk[tc % 2][:, 0:128],
                                                xn6[kc][:, tc * 128:(tc + 1) * 128],
                                                wqb0[:, kc * 128:(kc + 1) * 128],
                                                start=(kc == 0), stop=(kc == FC - 1))
                            last.then_inc(mm.h, 1); mm.n += 1
                    @blk.scalar
                    def _(e):
                        for tc in range(16):
                            e.wait_ge(mm.h, mm0 + tc + 1)
                            e.activation(vtm65[:, tc * 130:tc * 130 + 64],
                                         pqk[tc % 2][:, 0:64], ACT.Identity)
                            e.activation(vtm65[:, tc * 130 + 65:tc * 130 + 129],
                                         pqk[tc % 2][:, 64:128],
                                         ACT.Identity).then_inc(sv.h, 1)
                            sv.n += 1
                psqkv.close()
                with ExitStack() as psat:
                    pst6 = [psat.enter_context(nc.psum_tensor(f"p6st{p_}{i}", [128, 640], F32))
                            for i in range(2)]
                    po6 = [psat.enter_context(nc.psum_tensor(f"p6po{p_}{i}", [128, 128], F32))
                           for i in range(2)]
                    prb6 = [psat.enter_context(nc.psum_tensor(f"p6rb{p_}{i}", [128, TW], F32))
                            for i in range(2)]
                    srow = psat.enter_context(nc.sbuf_tensor(f"p6sr{p_}", [1, 2 * T], F32))
                    with nc.Block() as blk:
                        mm0 = mm.n; sv0 = sv.n; ev0 = ev.n
                        NI = 32
                        steps = [(hl, qt) for hl in range(2) for qt in range(16)]
                        _posST = {}; _posPV = {}
                        _c = 0
                        for _j in (0, 1):
                            _c += 1; _posST[_j] = _c
                        for _i in range(NI):
                            _c += 1; _posPV[_i] = _c
                            if _i + 2 < NI:
                                _c += 1; _posST[_i + 2] = _c
                        def emit_ST(e, j):
                            hl, qt = steps[j]
                            kcs = list(range(max(0, qt - 4), qt + 1))
                            if j >= 2:
                                e.wait_ge(sv.h, sv0 + (j - 2) + 1)
                            last = None
                            for ki, kc in enumerate(kcs):
                                last = e.matmul(pst6[j % 2][:, 128 * ki:128 * (ki + 1)],
                                                kaug[hl][:, 128 * kc:128 * (kc + 1)],
                                                qaug[hl][:, 128 * qt:128 * (qt + 1)],
                                                start=True, stop=True)
                            last.then_inc(mm.h, 1); mm.n += 1
                        def emit_PV(e, i):
                            hl, qt = steps[i]
                            kcs = list(range(max(0, qt - 4), qt + 1))
                            e.wait_ge(ev.h, ev0 + 2 * i + 1)
                            last = None
                            for ki, kc in enumerate(kcs):
                                last = e.matmul(po6[i % 2][0:65, :],
                                                vtm65[:, kc * 130 + 65 * hl:
                                                      kc * 130 + 65 * hl + 65],
                                                pt6[i % 2][:, 128 * ki:128 * (ki + 1)],
                                                start=(ki == 0), stop=(ki == len(kcs) - 1))
                            last.then_inc(mm.h, 1); mm.n += 1
                        @blk.tensor
                        def _(e):
                            e.wait_ge(ev.h, ev0)
                            emit_ST(e, 0)
                            emit_ST(e, 1)
                            for i in range(NI):
                                emit_PV(e, i)
                                if i + 2 < NI:
                                    emit_ST(e, i + 2)
                        @blk.scalar
                        def _(e):
                            for i in range(NI):
                                hl, qt = steps[i]
                                nk = min(qt, 4) + 1
                                e.wait_ge(mm.h, mm0 + _posST[i])
                                e.activation(pt6[i % 2][:, 0:128 * nk],
                                             pst6[i % 2][:, 0:128 * nk],
                                             ACT.Exp).then_inc(sv.h, 1)
                                sv.n += 1
                        @blk.vector
                        def _(e):
                            for i in range(NI):
                                hl, qt = steps[i]
                                nk = min(qt, 4) + 1
                                e.wait_ge(sv.h, sv0 + i + 1)
                                ins = e.tensor_tensor(
                                    pt6[i % 2][:, 128 * (nk - 1):128 * nk],
                                    pt6[i % 2][:, 128 * (nk - 1):128 * nk],
                                    tri0[:, :], ALU.mult)
                                if qt >= 4:
                                    ins = e.tensor_tensor(pt6[i % 2][:, 0:128],
                                                          pt6[i % 2][:, 0:128],
                                                          tri4[:, :], ALU.mult)
                                ins.then_inc(ev.h, 1)
                                ev.n += 1
                                e.wait_ge(mm.h, mm0 + _posPV[i])
                                e.tensor_copy(Opair2[0:64, hl * T + 128 * qt:
                                                      hl * T + 128 * (qt + 1)],
                                              po6[i % 2][0:64, :])
                                e.tensor_copy(srow[0:1, hl * T + 128 * qt: hl * T + 128 * (qt + 1)],
                                              po6[i % 2][64:65, :]).then_inc(ev.h, 1)
                                ev.n += 1
                    with nc.Block() as blk:
                        mm0 = mm.n; ev0 = ev.n; sv0 = sv.n
                        @blk.scalar
                        def _(e):
                            e.wait_ge(ev.h, ev0)
                            e.activation(rows6[0:1, 0:2 * T], srow[0:1, 0:2 * T], ACT.Ln)
                            e.activation(srow[0:1, 0:2 * T], rows6[0:1, 0:2 * T],
                                         ACT.Exp, scale=-1.0).then_inc(sv.h, 1)
                            sv.n += 1
                        @blk.tensor
                        def _(e):
                            e.wait_ge(sv.h, sv0 + 1)
                            k = 0
                            for hl in range(2):
                                for tt in range(NTT):
                                    if k >= 2:
                                        e.wait_ge(ev.h, ev0 + k - 1)
                                    e.matmul(prb6[k % 2][0:64, :],
                                             ones[0:1, 0:64],
                                             srow[0:1, hl * T + tt * TW: hl * T + (tt + 1) * TW],
                                             start=True, stop=True).then_inc(mm.h, 1)
                                    mm.n += 1
                                    k += 1
                        @blk.vector
                        def _(e):
                            k = 0
                            for hl in range(2):
                                for tt in range(NTT):
                                    e.wait_ge(mm.h, mm0 + k + 1)
                                    e.tensor_tensor(Opair2[0:64, hl * T + tt * TW:
                                                           hl * T + (tt + 1) * TW],
                                                    Opair2[0:64, hl * T + tt * TW:
                                                           hl * T + (tt + 1) * TW],
                                                    prb6[k % 2][0:64, :],
                                                    ALU.mult).then_inc(ev.h, 1)
                                    ev.n += 1
                                    k += 1
                with nc.Block() as blk:
                    @blk.sync
                    def _(e):
                        e.wait_ge(ev.h, ev.n)
                        for hl in range(2):
                            e.dma_start(AP(o_sp, p_ * 128 * T + hl * 64 * T,
                                           [[T, 64], [1, T]]),
                                        Opair2[0:64, hl * T:(hl + 1) * T]).then_inc(ds.h, 16)
                            ds.n += 16
                        e.wait_ge(ds.h, ds.n)
                if debug and p_ == 0:
                    d_aug = dbgt("aug", [4, 65, T], BF16)
                    d_vtm = dbgt("vtm", [128, 16 * 130], BF16)
                    with nc.Block() as blk:
                        @blk.sync
                        def _(e):
                            for i_, tile in enumerate([qaug[0], qaug[1], kaug[0], kaug[1]]):
                                e.dma_start(AP(d_aug, i_ * 65 * T, [[T, 65], [1, T]]),
                                            tile[:, :]).then_inc(ds.h, 16)
                                ds.n += 16
                            e.dma_start(AP(d_vtm, 0, [[16 * 130, 128], [1, 16 * 130]]),
                                        vtm65[:, :]).then_inc(ds.h, 16)
                            ds.n += 16
                            e.wait_ge(ds.h, ds.n)

            psmid.close()
            with ExitStack() as psw:
                wo6 = psw.enter_context(nc.sbuf_tensor("p6wo", [128, FC * D], BF16))
                ost = [psw.enter_context(nc.sbuf_tensor(f"p6os{i}", [128, FC * TW], BF16))
                       for i in range(2)]
                pout6 = [psw.enter_context(nc.psum_tensor(f"p6ou{i}", [128, TW], F32))
                         for i in range(2)]
                with nc.Block() as blk:
                    dg0 = dg.n
                    @blk.gpsimd
                    def _(e):
                        e.dma_start(wo6[:, :],
                                    AP(W["L1_attn_Wout"], 0,
                                       [[D, 128], [128 * D, FC], [1, D]])
                                    ).then_inc(dg.h, 16)
                        dg.n += 16
                        e.wait_ge(dg.h, dg.n)
                with nc.Block() as blk:
                    mm0 = mm.n; ev0 = ev.n; ds0 = ds.n
                    @blk.sync
                    def _(e):
                        for tt in range(NTT):
                            if tt >= 2:
                                e.wait_ge(mm.h, mm0 + (tt - 1) * FC)
                            e.dma_start(ost[tt % 2][:, :],
                                        AP(o_sp, tt * TW,
                                           [[T, 128], [128 * T, FC], [1, TW]])
                                        ).then_inc(ds.h, 16)
                            ds.n += 16
                    @blk.tensor
                    def _(e):
                        e.wait_ge(ev.h, ev0)
                        i = 0
                        for tt in range(NTT):
                            e.wait_ge(ds.h, ds0 + 16 * (tt + 1))
                            for fcx in range(FC):
                                if i >= 2:
                                    e.wait_ge(ev.h, ev0 + i - 1)
                                last = None
                                for kc in range(FC):
                                    last = e.matmul(pout6[i % 2][:, :],
                                                    wo6[:, kc * D + fcx * 128:
                                                        kc * D + (fcx + 1) * 128],
                                                    ost[tt % 2][:, kc * TW:(kc + 1) * TW],
                                                    start=(kc == 0), stop=(kc == FC - 1))
                                last.then_inc(mm.h, 1); mm.n += 1
                                i += 1
                    @blk.vector
                    def _(e):
                        i = 0
                        for tt in range(NTT):
                            sl = slice(tt * TW, (tt + 1) * TW)
                            for fcx in range(FC):
                                e.wait_ge(mm.h, mm0 + i + 1)
                                e.tensor_tensor(X[fcx][:, sl], X[fcx][:, sl],
                                                pout6[i % 2][:, :], ALU.add).then_inc(ev.h, 1)
                                ev.n += 1
                                i += 1

        if upto >= 7:
          with ExitStack() as ps:
            sb_ = lambda n, s, d=F32: ps.enter_context(nc.sbuf_tensor(n, s, d))
            rows7 = sb_("p7rows", [1, 3 * T])
            r7 = lambda i: rows7[0:1, i * T:(i + 1) * T]
            ones_row = sb_("p7or", [1, TW])
            brow1 = sb_("p7br", [1, D])
            with ExitStack() as psm7:
                sbm = lambda n, s, d=F32: psm7.enter_context(nc.sbuf_tensor(n, s, d))
                scr7 = sbm("p7sc", [128, TW])
                u7 = sbm("p7u", [128, TW])
                i7 = psm7.enter_context(nc.sbuf_tensor("p7i", [128, TW], I32))
                g7 = sbm("p7g", [128, TW])
                ps_iq = psm7.enter_context(ExitStack())
                iq7 = [ps_iq.enter_context(nc.sbuf_tensor(f"p7iq{f}", [128, T], BF16))
                       for f in range(FC)]
                mx7 = ps_iq.enter_context(nc.sbuf_tensor("p7mx", [128, T], F32))
                pst1 = psm7.enter_context(nc.psum_tensor("p7s1", [1, TW], F32))
                pst2 = psm7.enter_context(nc.psum_tensor("p7s2", [1, TW], F32))
                pbA = psm7.enter_context(nc.psum_tensor("p7bA", [128, TW], F32))
                pbB = psm7.enter_context(nc.psum_tensor("p7bB", [128, TW], F32))
                with nc.Block() as blk:
                    sv0 = sv.n; mm0 = mm.n; ev0 = ev.n
                    @blk.scalar
                    def _(e):
                        i = 0
                        for tt in range(NTT):
                            for f in range(FC):
                                if i >= 1:
                                    e.wait_ge(mm.h, mm0 + 2 * i)
                                e.activation(u7[:, :], X[f][:, tt * TW:(tt + 1) * TW],
                                             ACT.Square).then_inc(sv.h, 1)
                                sv.n += 1
                                i += 1
                    @blk.tensor
                    def _(e):
                        e.wait_ge(ev.h, ev0)
                        i = 0
                        for tt in range(NTT):
                            for f in range(FC):
                                if f == 0 and tt > 0:
                                    e.wait_ge(ev.h, ev0 + 2 * tt)
                                e.matmul(pst1[:, :], ones[:, 0:1],
                                         X[f][:, tt * TW:(tt + 1) * TW],
                                         start=(f == 0), stop=(f == FC - 1)).then_inc(mm.h, 1)
                                mm.n += 1
                                e.wait_ge(sv.h, sv0 + i + 1)
                                e.matmul(pst2[:, :], ones[:, 0:1], u7[:, :],
                                         start=(f == 0), stop=(f == FC - 1)).then_inc(mm.h, 1)
                                mm.n += 1
                                i += 1
                    @blk.vector
                    def _(e):
                        for tt in range(NTT):
                            e.wait_ge(mm.h, mm0 + (tt + 1) * 16)
                            sl = slice(tt * TW, (tt + 1) * TW)
                            e.tensor_scalar(r7(0)[:, sl], pst1[:, :], 1.0 / D, None, ALU.mult)
                            e.tensor_scalar(r7(2)[:, sl], pst2[:, :], 1.0 / D, None,
                                            ALU.mult).then_inc(ev.h, 2)
                            ev.n += 2
                with nc.Block() as blk:
                    @blk.vector
                    def _(e):
                        e.tensor_tensor(r7(1), r7(0), r7(0), ALU.mult)
                        e.tensor_tensor(r7(1), r7(2), r7(1), ALU.subtract)
                        e.tensor_scalar(r7(2), r7(1), 1e-5, None, ALU.add).then_inc(ev.h, 1)
                        ev.n += 1
                    @blk.scalar
                    def _(e):
                        e.wait_ge(ev.h, ev.n)
                        e.activation(r7(1), r7(2), ACT.Ln)
                        e.activation(r7(1), r7(1), ACT.Exp, scale=-0.5)
                with nc.Block() as blk:
                    @blk.vector
                    def _(e):
                        e.tensor_tensor(r7(0), r7(0), r7(1), ALU.mult)
                        e.tensor_scalar(r7(0), r7(0), -1.0, None, ALU.mult).then_inc(ev.h, 1)
                        ev.n += 1
                with nc.Block() as blk:
                    mm0 = mm.n; ev0 = ev.n
                    @blk.tensor
                    def _(e):
                        e.wait_ge(ev.h, ev0)
                        for tt in range(NTT):
                            sl = slice(tt * TW, (tt + 1) * TW)
                            if tt > 0:
                                e.wait_ge(ev.h, ev0 + tt * 8)
                            e.matmul(pbA[:, :], ones[0:1, :], r7(1)[:, sl], start=True, stop=True)
                            e.matmul(pbB[:, :], ones[0:1, :], r7(0)[:, sl],
                                     start=True, stop=True).then_inc(mm.h, 2)
                            mm.n += 2
                    @blk.vector
                    def _(e):
                        for tt in range(NTT):
                            e.wait_ge(mm.h, mm0 + 2 * (tt + 1))
                            sl = slice(tt * TW, (tt + 1) * TW)
                            for f in range(FC):
                                e.tensor_tensor(scr7[:, :], X[f][:, sl], pbA[:, :], ALU.mult)
                                e.tensor_tensor(scr7[:, :], scr7[:, :], pbB[:, :], ALU.add)
                                if tt == 0 and f == 0:
                                    ins = e.tensor_copy(mx7[:, sl], scr7[:, :])
                                else:
                                    ins = e.tensor_tensor(mx7[:, sl], scr7[:, :],
                                                          mx7[:, sl], ALU.max)
                            ins.then_inc(ev.h, 8)
                            ev.n += 8
                        for tt in range(1, NTT):
                            e.tensor_tensor(mx7[:, 0:TW], mx7[:, tt * TW:(tt + 1) * TW],
                                            mx7[:, 0:TW], ALU.max)
                        e.drain()
                        e.nop().then_inc(ev.h, 1)
                        ev.n += 1
                with nc.Block() as blk:
                    @blk.gpsimd
                    def _(e):
                        e.wait_ge(ev.h, ev.n)
                        e.tensor_reduce(cell(5), mx7[:, 0:TW], AXALL,
                                        ALU.max).then_inc(gp.h, 1)
                        gp.n += 1
                with nc.Block() as blk:
                    @blk.gpsimd
                    def _(e):
                        e.wait_ge(gp.h, gp.n)
                        e.dma_start(cc_b[0].ap(), cell(5)).then_inc(dg.h, 16)
                        dg.n += 16
                        e.wait_ge(dg.h, dg.n)
                        e.collective_compute(
                            "AllReduce", ALU.max,
                            replica_groups=[list(range(NCORES))],
                            ins=[cc_b[0].ap().opt()],
                            outs=[cc_b[1].ap().opt()],
                        ).then_inc(cc.h, 1)
                        cc.n += 1
                        e.wait_ge(cc.h, cc.n)
                        e.dma_start(cell(5), cc_b[1].ap()).then_inc(dg.h, 16)
                        dg.n += 16
                        e.wait_ge(dg.h, dg.n)
                with nc.Block() as blk:
                    @blk.vector
                    def _(e):
                        e.tensor_scalar(cell(5), cell(5), 1e-8, None, ALU.max)
                        e.drain()
                        e.tensor_scalar(cell(6), cell(5), 1.0 / 255.0, None, ALU.mult)
                        e.reciprocal(cell(7), cell(5))
                        e.drain()
                        e.tensor_scalar(cell(7), cell(7), 255.0, None, ALU.mult)
                        e.drain()
                        e.tensor_scalar(r7(1), r7(1), cell(7), None, ALU.mult)
                        e.tensor_scalar(r7(0), r7(0), cell(7), 0.5, ALU.mult,
                                        ALU.add).then_inc(ev.h, 1)
                        ev.n += 1
                    @blk.tensor
                    def _(e):
                        e.wait_ge(ev.h, ev.n)
                        e.matmul(pbA[:, 0:1], ones[0:1, :], cell(6),
                                 start=True, stop=True).then_inc(mm.h, 1)
                        mm.n += 1
                with nc.Block() as blk:
                    @blk.vector
                    def _(e):
                        e.wait_ge(mm.h, mm.n)
                        e.tensor_copy(pcol[:, 2:3], pbA[:, 0:1]).then_inc(ev.h, 1)
                        ev.n += 1
                with nc.Block() as blk:
                    mm0 = mm.n; ev0 = ev.n; sv0 = sv.n
                    @blk.tensor
                    def _(e):
                        e.wait_ge(ev.h, ev0)
                        for tt in range(NTT):
                            sl = slice(tt * TW, (tt + 1) * TW)
                            if tt > 0:
                                e.wait_ge(ev.h, ev0 + tt * 3 * FC)
                            e.matmul(pbA[:, :], ones[0:1, :], r7(1)[:, sl], start=True, stop=True)
                            e.matmul(pbB[:, :], ones[0:1, :], r7(0)[:, sl],
                                     start=True, stop=True).then_inc(mm.h, 2)
                            mm.n += 2
                    @blk.vector
                    def _(e):
                        for tt in range(NTT):
                            e.wait_ge(mm.h, mm0 + 2 * (tt + 1))
                            sl = slice(tt * TW, (tt + 1) * TW)
                            for f in range(FC):
                                e.tensor_tensor(scr7[:, :], X[f][:, sl], pbA[:, :], ALU.mult)
                                e.tensor_tensor(scr7[:, :], scr7[:, :], pbB[:, :], ALU.add)
                                e.tensor_copy(i7[:, :], scr7[:, :])
                                e.tensor_tensor(g7[:, :], i7[:, :], scr7[:, :], ALU.is_gt)
                                e.scalar_tensor_tensor(g7[:, :], i7[:, :], -1.0,
                                                       g7[:, :], ALU.mult, ALU.add)
                                e.tensor_scalar(iq7[f][:, sl], g7[:, :], 0.0, None,
                                                ALU.min).then_inc(ev.h, 3)
                                ev.n += 3
                with ExitStack() as ps1:
                    HF = FF // 2
                    w1T = ps1.enter_context(nc.sbuf_tensor("p7w1", [128, FC * HF], BF16))
                    hsl_t = ps1.enter_context(nc.sbuf_tensor("p7h", [128, 16 * TW], BF16))
                    ph7 = [ps1.enter_context(nc.psum_tensor(f"p7ph{i}", [128, TW], F32))
                           for i in range(2)]
                    with nc.Block() as blk:
                        @blk.vector
                        def _(e):
                            e.tensor_scalar(sc1[:, :], sw1[:, :], pcol[:, 2:3], None, ALU.mult)
                            e.drain()
                            e.reciprocal(bq1[:, :], sc1[:, :])
                            e.drain()
                            e.tensor_tensor(bq1[:, :], b1v[:, :], bq1[:, :], ALU.mult)
                            e.drain()
                            e.tensor_scalar(bq1[:, :], bq1[:, :], 0.5, None, ALU.add)
                            e.drain()
                            e.memset(mx7[:, :], 0.0)
                            e.drain()
                        @blk.gpsimd
                        def _(e):
                            pass
                    with nc.Block() as blk:
                        @blk.vector
                        def _(e):
                            e.tensor_copy(i7[:, 0:FFC], bq1[:, :])
                            e.tensor_tensor(g7[:, 0:FFC], i7[:, 0:FFC], bq1[:, :], ALU.is_gt)
                            e.drain()
                            e.scalar_tensor_tensor(bq1[:, :], i7[:, 0:FFC], 1.0,
                                                   g7[:, 0:FFC], ALU.mult, ALU.subtract)
                            e.drain()
                            e.tensor_tensor(bq1[:, :], bq1[:, :], sc1[:, :], ALU.mult)
                            e.drain()
                            e.nop().then_inc(ev.h, 1)
                            ev.n += 1
                    with nc.Block() as blk:
                        mm0 = mm.n; sv0 = sv.n; ev0 = ev.n; ds0 = ds.n; dg0 = dg.n
                        @blk.sync
                        def _(e):
                            for oh2 in range(2):
                                if oh2 > 0:
                                    e.wait_ge(mm.h, mm0 + 64 * oh2)
                                for kc in range(FC):
                                    e.dma_start(w1T[:, kc * HF:(kc + 1) * HF],
                                                AP(nw1, oh2 * HF * D + kc * 128,
                                                   [[D, HF], [1, 128]]),
                                                transpose=True).then_inc(dg.h, 16)
                                    dg.n += 16
                                for tt in range(NTT):
                                    s_ = oh2 * NTT + tt
                                    e.wait_ge(ev.h, ev0 + (s_ + 1) * 16)
                                    e.dma_start(AP(h_sp, oh2 * 16 * 128 * T + tt * TW,
                                                   [[T, 128], [128 * T, 16], [1, TW]]),
                                                hsl_t[:, :]).then_inc(ds.h, 16)
                                    ds.n += 16
                            e.wait_ge(ds.h, ds.n)
                        @blk.tensor
                        def _(e):
                            e.wait_ge(ev.h, ev0)
                            for oh2 in range(2):
                                e.wait_ge(dg.h, dg0 + 128 * (oh2 + 1))
                                for tt in range(NTT):
                                    s_ = oh2 * NTT + tt
                                    if s_ > 0:
                                        e.wait_ge(ds.h, ds0 + 16 * s_)
                                    for og in range(16):
                                        u = s_ * 16 + og
                                        if u >= 2:
                                            e.wait_ge(sv.h, sv0 + u - 1)
                                        last = None
                                        for kc in range(FC):
                                            last = e.matmul(
                                                ph7[u % 2][:, :],
                                                w1T[:, kc * HF + og * 128:
                                                    kc * HF + (og + 1) * 128],
                                                iq7[kc][:, tt * TW:(tt + 1) * TW],
                                                start=(kc == 0), stop=(kc == FC - 1))
                                        last.then_inc(mm.h, 1); mm.n += 1
                        @blk.scalar
                        def _(e):
                            for oh2 in range(2):
                                for tt in range(NTT):
                                    s_ = oh2 * NTT + tt
                                    for og in range(16):
                                        u = s_ * 16 + og
                                        e.wait_ge(mm.h, mm0 + u + 1)
                                        if u >= 2:
                                            e.wait_ge(ev.h, ev0 + u - 1)
                                        gog = oh2 * 16 + og
                                        e.activation(hsl_t[:, og * TW:(og + 1) * TW],
                                                     ph7[u % 2][:, :], ACT.Relu,
                                                     scale=sc1[:, gog:gog + 1],
                                                     bias=bq1[:, gog:gog + 1]).then_inc(sv.h, 1)
                                        sv.n += 1
                        @blk.vector
                        def _(e):
                            for oh2 in range(2):
                                for tt in range(NTT):
                                    sl = slice(tt * TW, (tt + 1) * TW)
                                    for og in range(16):
                                        u = (oh2 * NTT + tt) * 16 + og
                                        e.wait_ge(sv.h, sv0 + u + 1)
                                        e.tensor_tensor(mx7[:, sl],
                                                        hsl_t[:, og * TW:(og + 1) * TW],
                                                        mx7[:, sl], ALU.max).then_inc(ev.h, 1)
                                        ev.n += 1

                    with nc.Block() as blk:
                        @blk.vector
                        def _(e):
                            for tt in range(1, NTT):
                                e.tensor_tensor(mx7[:, 0:TW], mx7[:, tt * TW:(tt + 1) * TW],
                                                mx7[:, 0:TW], ALU.max)
                            e.drain()
                            e.nop().then_inc(ev.h, 1)
                            ev.n += 1
                        @blk.gpsimd
                        def _(e):
                            e.wait_ge(ev.h, ev.n)
                            e.tensor_reduce(cell(5), mx7[:, 0:TW], AXALL,
                                            ALU.max).then_inc(gp.h, 1)
                            gp.n += 1
                    with nc.Block() as blk:
                        @blk.gpsimd
                        def _(e):
                            e.wait_ge(gp.h, gp.n)
                            e.dma_start(cc_b[2].ap(), cell(5)).then_inc(dg.h, 16)
                            dg.n += 16
                            e.wait_ge(dg.h, dg.n)
                            e.collective_compute(
                                "AllReduce", ALU.max,
                                replica_groups=[list(range(NCORES))],
                                ins=[cc_b[2].ap().opt()],
                                outs=[cc_b[3].ap().opt()],
                            ).then_inc(cc.h, 1)
                            cc.n += 1
                            e.wait_ge(cc.h, cc.n)
                            e.dma_start(cell(5), cc_b[3].ap()).then_inc(dg.h, 16)
                            dg.n += 16
                            e.wait_ge(dg.h, dg.n)
                    with nc.Block() as blk:
                        @blk.vector
                        def _(e):
                            e.tensor_scalar(cell(5), cell(5), 1e-8, None, ALU.max)
                            e.drain()
                            e.tensor_scalar(cell(6), cell(5), 1.0 / 255.0, None, ALU.mult)
                            e.reciprocal(cell(7), cell(5))
                            e.drain()
                            e.tensor_scalar(cell(7), cell(7), 255.0, None, ALU.mult)
                            e.drain()
                            e.nop().then_inc(ev.h, 1)
                            ev.n += 1
                        @blk.tensor
                        def _(e):
                            e.wait_ge(ev.h, ev.n)
                            e.matmul(ph7[0][:, 0:1], ones[0:1, :], cell(7),
                                     start=True, stop=True)
                            e.matmul(ph7[1][:, 0:1], ones[0:1, :], cell(6),
                                     start=True, stop=True).then_inc(mm.h, 2)
                            mm.n += 2
                    with nc.Block() as blk:
                        @blk.vector
                        def _(e):
                            e.tensor_copy(pcol[:, 1:2], ph7[0][:, 0:1])
                            e.tensor_copy(pcol[:, 3:4], ph7[1][:, 0:1])
                            e.drain()
                            e.tensor_scalar(sc2[:, :], sw2[:, :], pcol[:, 3:4], None, ALU.mult)
                            e.drain()
                            e.reciprocal(bq2[:, :], sc2[:, :])
                            e.drain()
                            e.tensor_tensor(bq2[:, :], b2v[:, :], bq2[:, :], ALU.mult)
                            e.drain()
                            e.tensor_scalar(bq2[:, :], bq2[:, :], 0.5, None, ALU.add)
                            e.drain()
                            e.tensor_copy(i7[:, 0:FC], bq2[:, :])
                            e.tensor_tensor(g7[:, 0:FC], i7[:, 0:FC], bq2[:, :], ALU.is_gt)
                            e.drain()
                            e.scalar_tensor_tensor(bq2[:, :], i7[:, 0:FC], 1.0,
                                                   g7[:, 0:FC], ALU.mult, ALU.subtract)
                            e.drain()
                            e.nop().then_inc(ev.h, 1)
                            ev.n += 1
                        @blk.sync
                        def _(e):
                            e.wait_ge(ev.h, ev.n)
                            e.dma_start(AP(rows_sp, 8 * T, [[1, 128], [128, FC]]),
                                        bq2[:, :], allow_slow_non_contiguous=True
                                        ).then_inc(ds.h, 16)
                            ds.n += 16
                            e.wait_ge(ds.h, ds.n)
                            e.dma_start(brow1[0:1, :],
                                        AP(rows_sp, 8 * T, [[1, 1], [1, D]])).then_inc(ds.h, 16)
                            ds.n += 16
                            e.wait_ge(ds.h, ds.n)
                        @blk.scalar
                        def _(e):
                            pass
                ps_iq.close()
                with ExitStack() as ps2:
                    w2T = ps2.enter_context(nc.sbuf_tensor("p7w2", [128, FFC * D], BF16))
                    pout7 = [ps2.enter_context(nc.psum_tensor(f"p7po{i}", [128, TW], F32))
                             for i in range(2)]
                    with nc.Block() as blk:
                        @blk.sync
                        def _(e):
                            for kc in range(FFC):
                                e.dma_start(w2T[:, kc * D:(kc + 1) * D],
                                            AP(nw2, kc * 128, [[FF, D], [1, 128]]),
                                            transpose=True).then_inc(ds.h, 16)
                                ds.n += 16
                            e.wait_ge(ds.h, ds.n)
                        @blk.vector
                        def _(e):
                            e.memset(ones_row[0:1, :], 1.0).then_inc(ev.h, 1)
                            ev.n += 1
                    with nc.Block() as blk:
                        mm0 = mm.n; sv0 = sv.n; ev0 = ev.n; ds0 = ds.n; dg0 = dg.n
                        h2 = ps2.enter_context(nc.sbuf_tensor("p7h2", [128, FFC * TW], BF16))
                        @blk.gpsimd
                        def _(e):
                            for tt in range(NTT):
                                if tt >= 1:
                                    e.wait_ge(ev.h, ev0 + tt * (3 * FFC + FC))
                                e.dma_start(h2[:, :],
                                            AP(h_sp, tt * TW,
                                               [[T, 128], [128 * T, FFC], [1, TW]])
                                            ).then_inc(dg.h, 16)
                                dg.n += 16
                        @blk.scalar
                        def _(e):
                            for tt in range(NTT):
                                e.wait_ge(dg.h, dg0 + 16 * (tt + 1))
                                for og in range(FFC):
                                    u = tt * FFC + og
                                    if u >= 1:
                                        e.wait_ge(ev.h, ev0 + tt * (3 * FFC + FC) + 3 * og)
                                    e.activation(u7[:, :], h2[:, og * TW:(og + 1) * TW],
                                                 ACT.Identity, scale=pcol[:, 1:2],
                                                 bias=halfc[:, 0:1]).then_inc(sv.h, 1)
                                    sv.n += 1
                        @blk.vector
                        def _(e):
                            for tt in range(NTT):
                                sl = slice(tt * TW, (tt + 1) * TW)
                                for og in range(FFC):
                                    u = tt * FFC + og
                                    e.wait_ge(sv.h, sv0 + u + 1)
                                    e.tensor_copy(i7[:, :], u7[:, :])
                                    e.tensor_tensor(g7[:, :], i7[:, :], u7[:, :], ALU.is_gt)
                                    e.scalar_tensor_tensor(h2[:, og * TW:(og + 1) * TW],
                                                           i7[:, :], -1.0, g7[:, :],
                                                           ALU.mult, ALU.add).then_inc(ev.h, 3)
                                    ev.n += 3
                                for fcx in range(FC):
                                    e.wait_ge(mm.h, mm0 + tt * FC + fcx + 1)
                                    e.scalar_tensor_tensor(X[fcx][:, sl],
                                                           pout7[fcx % 2][:, :],
                                                           sc2[:, fcx:fcx + 1],
                                                           X[fcx][:, sl],
                                                           ALU.mult, ALU.add).then_inc(ev.h, 1)
                                    ev.n += 1
                        @blk.tensor
                        def _(e):
                            for tt in range(NTT):
                                for fcx in range(FC):
                                    u = tt * FC + fcx
                                    e.wait_ge(ev.h, ev0 + tt * (3 * FFC + FC) + 3 * FFC)
                                    if u >= 2:
                                        e.wait_ge(ev.h, ev0 + (u - 2) // FC * (3 * FFC + FC)
                                                  + 3 * FFC + (u - 2) % FC + 1)
                                    last = None
                                    for kc in range(FFC):
                                        last = e.matmul(pout7[fcx % 2][:, :],
                                                        w2T[:, kc * D + fcx * 128:
                                                            kc * D + (fcx + 1) * 128],
                                                        h2[:, kc * TW:(kc + 1) * TW],
                                                        start=(kc == 0), stop=False)
                                    last = e.matmul(pout7[fcx % 2][:, :],
                                                    brow1[0:1, fcx * 128:(fcx + 1) * 128],
                                                    ones_row[0:1, :],
                                                    start=False, stop=True)
                                    last.then_inc(mm.h, 1); mm.n += 1

        if debug and upto >= 6:
            d_osp = dbgt("osp", [FC, 128, T], BF16)
            with nc.Block() as blk:
                @blk.sync
                def _(e):
                    e.dma_start(AP(d_osp, 0, [[1, FC * 128 * T]]),
                                AP(o_sp, 0, [[1, FC * 128 * T]])).then_inc(ds.h, 16)
                    ds.n += 16
                    e.wait_ge(ds.h, ds.n)
        if debug:
            d_xc = dbgt("xc", [DIC, 128, T], BF16)
            with nc.Block() as blk:
                @blk.sync
                def _(e):
                    e.dma_start(AP(d_xc, 0, [[1, DIC * 128 * T]]),
                                AP(xc_sp, 0, [[1, DIC * 128 * T]])).then_inc(ds.h, 16)
                    ds.n += 16
                    e.wait_ge(ds.h, ds.n)

        with ExitStack() as ps:
            ostage = ps.enter_context(nc.sbuf_tensor("ostage", [128, 16 * D], F32))
            tp = [ps.enter_context(nc.psum_tensor(f"otp{i}", [128, 128], F32)) for i in range(4)]
            with nc.Block() as blk:
                @blk.tensor
                def _(e):
                    ev0 = ev.n
                    e.wait_ge(ev.h, ev0)
                    e.wait_ge(sv.h, sv.n)
                    i = 0
                    for tc in range(16):
                        for f in range(FC):
                            if i >= 4:
                                e.wait_ge(ev.h, ev0 + i - 3)
                            e.transpose(tp[i % 4][:, :], X[f][:, 128 * tc:128 * (tc + 1)],
                                        iden[:, :]).then_inc(mm.h, 1)
                            mm.n += 1
                            i += 1
                @blk.vector
                def _(e):
                    base = mm.n - 128
                    i = 0
                    for tc in range(16):
                        for f in range(FC):
                            e.wait_ge(mm.h, base + i + 1)
                            e.tensor_copy(ostage[:, tc * D + 128 * f: tc * D + 128 * (f + 1)],
                                          tp[i % 4][:, :]).then_inc(ev.h, 1)
                            ev.n += 1
                            i += 1
            with nc.Block() as blk:
                @blk.sync
                def _(e):
                    e.dma_start(AP(out, 0, [[D, 128], [128 * D, 16], [1, D]]),
                                ostage[:, :]).then_inc(ds.h, 16)
                    ds.n += 16
                    e.wait_ge(ds.h, ds.n)

    return nc, list(dbg.keys())


def _rope_tables():
    half = 32
    theta = 1.0 / 10000.0 ** (np.arange(half, dtype=np.float64) / half)
    pos = np.arange(T, dtype=np.float64)
    fr = pos[:, None] * theta[None, :]
    fr = np.concatenate([fr, fr], axis=1)
    cos = np.cos(fr).T.astype(np.float32)
    sin = np.sin(fr).T.astype(np.float32)
    sin_signed = sin.copy()
    sin_signed[:half, :] *= -1.0
    cos2 = np.concatenate([cos, cos], axis=0)
    sin2 = np.concatenate([sin_signed, sin_signed], axis=0)
    return np.ascontiguousarray(cos2), np.ascontiguousarray(sin2)


_NC_CACHE = {}

def kernel(**inputs):
    key = ("full", UPTO)
    if key not in _NC_CACHE:
        _NC_CACHE[key] = build_program(debug=False)
    nc, _ = _NC_CACHE[key]
    cos2, sin2 = _rope_tables()
    x = np.ascontiguousarray(np.asarray(inputs["L0_attn_x"], dtype=np.float32))
    base = {k: np.ascontiguousarray(np.asarray(v, dtype=np.float32))
            for k, v in inputs.items() if k != "L0_attn_x"}
    base["rope_cos"] = cos2
    base["rope_sin"] = sin2
    in_maps = []
    for c in range(NCORES):
        m = dict(base)
        m["x0"] = x[c]
        in_maps.append(m)
    res = run_bass_kernel_spmd(nc, in_maps, core_ids=list(range(NCORES)))
    return np.stack([np.asarray(res.results[c]["out"]) for c in range(NCORES)], axis=0)
